# version 01
# baseline (speedup 1.0000x reference)
import numpy as np
import ml_dtypes
from contextlib import ExitStack

import concourse.bass as bass
import concourse.mybir as mybir
from concourse.bass_utils import run_bass_kernel_spmd

F32 = mybir.dt.float32
BF16 = mybir.dt.bfloat16
AF = mybir.ActivationFunctionType
ALU = mybir.AluOpType
AX = mybir.AxisListType

S = 2048
TCN = 16
D = 1024
DCN = 8
FF = 4096
FCN = 32
NE = 8
TBS = 512
NTB = S // TBS
CPB = TBS // 128
NH = 16
HD = 64
STRIDES = [1, 1, 2, 2, 2, 2, 4, 4, 4, 4, 8, 8, 8, 8, 16, 16]
EPS = 1e-5


class Sem:
    def __init__(self, ctx, nc, name):
        self.h = ctx.enter_context(nc.semaphore(name))
        self.n = 0

    def inc(self, instr, k=1):
        instr.then_inc(self.h, k)
        self.n += k
        return self.n

    def rec(self):
        return (self, self.n)


def _wait(e, recs):
    seen = {}
    for r in recs:
        if r is None:
            continue
        s, v = r
        if v > 0 and seen.get(id(s), -1) < v:
            seen[id(s)] = v
            e.wait_ge(s.h, v)


class Builder:
    def __init__(self, debug=False, stop_after="full", n_tb=NTB, n_exp=NE,
                 do_router=True, do_ep=True):
        self.debug = debug
        self.stop_after = stop_after
        self.n_tb = n_tb
        self.n_exp = n_exp
        self.do_router = do_router
        self.do_ep = do_ep
        self.nc = bass.Bass("TRN2", target_bir_lowering=False)
        self._uid = 0

    def uid(self, p):
        self._uid += 1
        return f"{p}{self._uid}"

    def T(self, fn): self.blk.tensor(fn)

    def A(self, fn, dr=True):
        self.blk.scalar((lambda e: (e.drain(), fn(e))) if dr else fn)

    def V(self, fn, dr=True):
        self.blk.vector((lambda e: (e.drain(), fn(e))) if dr else fn)

    def P(self, fn, dr=True):
        self.blk.gpsimd((lambda e: (e.drain(), fn(e))) if dr else fn)

    def Sy(self, fn): self.blk.sync(fn)

    def barrier(self):
        sems = [self.sT, self.sA, self.sV, self.sP, self.dS, self.sIc,
                self.sWo] + self.sW1 + self.sW1c + self.sW2 + self.sW2c + \
            self.sQ + self.sW1x + self.sW2x
        recs = [s.rec() for s in sems]
        for em in (self.T, self.A, self.V, self.P, self.Sy):
            em(lambda e: _wait(e, recs))

    def build(self):
        nc = self.nc
        d = {}
        d["x"] = nc.dram_tensor("x", [S, D], F32, kind="ExternalInput").ap()
        d["w1s"] = nc.dram_tensor("w1s", [NE, FCN, 128, DCN, 128], F32, kind="ExternalInput").ap()
        d["w2s"] = nc.dram_tensor("w2s", [NE, DCN, 128, FCN, 128], F32, kind="ExternalInput").ap()
        d["w1a"] = nc.dram_tensor("w1a", [FCN, 128, DCN, 128], F32, kind="ExternalInput").ap()
        d["w2a"] = nc.dram_tensor("w2a", [DCN, 128, FCN, 128], F32, kind="ExternalInput").ap()
        d["wqkv"] = nc.dram_tensor("wqkv", [24, 128, DCN, 128], F32, kind="ExternalInput").ap()
        d["wout"] = nc.dram_tensor("wout", [D, D], F32, kind="ExternalInput").ap()
        d["wr"] = nc.dram_tensor("wr", [128, DCN, NE], F32, kind="ExternalInput").ap()
        d["b1c"] = nc.dram_tensor("b1c", [128, NE, FCN], F32, kind="ExternalInput").ap()
        d["b2c"] = nc.dram_tensor("b2c", [128, NE, DCN], F32, kind="ExternalInput").ap()
        d["b1ac"] = nc.dram_tensor("b1ac", [128, FCN], F32, kind="ExternalInput").ap()
        d["b2ac"] = nc.dram_tensor("b2ac", [128, DCN], F32, kind="ExternalInput").ap()
        d["bqkvc"] = nc.dram_tensor("bqkvc", [128, 24], F32, kind="ExternalInput").ap()
        d["br_b"] = nc.dram_tensor("br_b", [128, NE], F32, kind="ExternalInput").ap()
        d["g_f"] = nc.dram_tensor("g_f", [128, D], F32, kind="ExternalInput").ap()
        d["be_f"] = nc.dram_tensor("be_f", [128, D], F32, kind="ExternalInput").ap()
        d["g_a"] = nc.dram_tensor("g_a", [128, D], F32, kind="ExternalInput").ap()
        d["be_a"] = nc.dram_tensor("be_a", [128, D], F32, kind="ExternalInput").ap()
        d["bout_r"] = nc.dram_tensor("bout_r", [1, D], F32, kind="ExternalInput").ap()
        d["identf"] = nc.dram_tensor("identf", [128, 128], F32, kind="ExternalInput").ap()
        d["ones1"] = nc.dram_tensor("ones1", [1, 128], F32, kind="ExternalInput").ap()
        d["y"] = nc.dram_tensor("y", [S, D], F32, kind="ExternalOutput").ap()
        d["w1b"] = nc.dram_tensor("w1b", [NE, FCN, 128, DCN, 128], BF16, kind="Internal").ap()
        d["w2b"] = nc.dram_tensor("w2b", [NE, DCN, 128, FCN, 128], BF16, kind="Internal").ap()
        d["w1ab"] = nc.dram_tensor("w1ab", [FCN, 128, DCN, 128], BF16, kind="Internal").ap()
        d["w2ab"] = nc.dram_tensor("w2ab", [DCN, 128, FCN, 128], BF16, kind="Internal").ap()
        if self.debug:
            d["dbg_x1"] = nc.dram_tensor("dbg_x1", [S, D], F32, kind="ExternalOutput").ap()
            d["dbg_x2"] = nc.dram_tensor("dbg_x2", [S, D], F32, kind="ExternalOutput").ap()
            d["dbg_rw"] = nc.dram_tensor("dbg_rw", [128, CPB, NE], F32, kind="ExternalOutput").ap()
            d["dbg_acc"] = nc.dram_tensor("dbg_acc", [128, DCN, TBS], F32, kind="ExternalOutput").ap()
            d["dbg_we"] = nc.dram_tensor("dbg_we", [128, TBS], F32, kind="ExternalOutput").ap()
            d["dbg_stat"] = nc.dram_tensor("dbg_stat", [128, 16, 8], F32, kind="ExternalOutput").ap()
            d["dbg_hc"] = nc.dram_tensor("dbg_hc", [128, DCN, S], BF16, kind="ExternalOutput").ap()
            d["dbg_a"] = nc.dram_tensor("dbg_a", [128, S], BF16, kind="ExternalOutput").ap()
            d["dbg_qkv"] = nc.dram_tensor("dbg_qkv", [128, 3, S], BF16, kind="ExternalOutput").ap()
            d["dbg_at"] = nc.dram_tensor("dbg_at", [128, 16, 128], BF16, kind="ExternalOutput").ap()
            d["dbg_vs"] = nc.dram_tensor("dbg_vs", [128, 16, HD], BF16, kind="ExternalOutput").ap()
            d["dbg_rlog"] = nc.dram_tensor("dbg_rlog", [128, CPB, NE], F32, kind="ExternalOutput").ap()
        self.d = d

        with ExitStack() as ctx:
            self.ctx = ctx
            self._alloc_outer()
            self.blk = ctx.enter_context(nc.Block())
            self._emit()
        return nc

    def _alloc_outer(self):
        nc, ctx = self.nc, self.ctx
        sb = lambda n, sh, dt: ctx.enter_context(nc.sbuf_tensor(n, sh, dt))
        self.x_tm = sb("s_x", [128, TCN, D], F32)
        self.gfh = sb("s_gfh", [128, D], F32)
        self.bfh = sb("s_bfh", [128, D], F32)
        self.gab = sb("s_gab", [128, D], F32)
        self.bab = sb("s_bab", [128, D], F32)
        self.wr_sb = sb("s_wr", [128, DCN, NE], F32)
        self.b1c = sb("s_b1c", [128, NE, FCN], F32)
        self.b2c = sb("s_b2c", [128, NE, DCN], F32)
        self.b1ac = sb("s_b1ac", [128, FCN], F32)
        self.b2ac = sb("s_b2ac", [128, DCN], F32)
        self.bqkvs = sb("s_bqkvs", [128, 24], F32)
        self.br_b = sb("s_brb", [128, NE], F32)
        self.bout_bf = sb("s_boutbf", [1, D], BF16)
        self.identf = sb("s_idf", [128, 128], F32)
        self.identb = sb("s_idb", [128, 128], BF16)
        self.ones_f = sb("s_onesf", [1, 128], F32)
        self.ones_b = sb("s_onesb", [1, 128], BF16)
        self.stat = sb("s_stat", [128, 16, 8], F32)
        self.rstd1 = sb("s_rstd1", [128, TCN], F32)
        self.nmr1 = sb("s_nmr1", [128, TCN], F32)
        self.rlog = sb("s_rlog", [128, CPB, NE], F32)
        self.rw = sb("s_rw", [128, CPB, NE], F32)
        self.req = sb("s_req", [128, 2, NE], F32)
        self.w_row = sb("s_wrow", [1, TBS], F32)
        self.sm = sb("s_sm", [128, 8, 4], F32)
        self.xa_scr = sb("s_xa", [128, D], F32)
        self.t1_scr = sb("s_t1", [128, D], F32)
        self.z_scr = sb("s_z", [128, D], F32)
        self.ps4 = ctx.enter_context(nc.psum_tensor("p_ps4", [128, 2048], F32))
        self.ps2 = ctx.enter_context(nc.psum_tensor("p_ps2", [128, 1024], F32))
        self.psb = ctx.enter_context(nc.psum_tensor("p_psb", [128, 1024], BF16))
        self.pss = ctx.enter_context(nc.psum_tensor("p_pss", [128, 512], F32))
        self.sT = Sem(ctx, nc, "sT")
        self.sA = Sem(ctx, nc, "sA")
        self.sV = Sem(ctx, nc, "sV")
        self.sP = Sem(ctx, nc, "sP")
        self.dS = Sem(ctx, nc, "dS")
        self.sIc = Sem(ctx, nc, "sIc")
        self.sW1 = [Sem(ctx, nc, f"sW1_{i}") for i in range(4)]
        self.sW1x = [Sem(ctx, nc, f"sW1x_{i}") for i in range(4)]
        self.sW2x = [Sem(ctx, nc, f"sW2x_{i}") for i in range(2)]
        self.sW1c = [Sem(ctx, nc, f"sW1c_{i}") for i in range(4)]
        self.sW2 = [Sem(ctx, nc, f"sW2_{i}") for i in range(2)]
        self.sW2c = [Sem(ctx, nc, f"sW2c_{i}") for i in range(2)]
        self.sQ = [Sem(ctx, nc, f"sQ_{i}") for i in range(2)]
        self.sWo = Sem(ctx, nc, "sWo")

    def _emit(self):
        d = self.d
        dS, sIc, sA = self.dS, self.sIc, self.sA

        def init_dmas(e):
            x_r = d["x"].rearrange("(tc p) dd -> p tc dd", p=128)
            for i in range(4):
                dS.inc(e.dma_start(out=self.x_tm[:, 4 * i:4 * i + 4, :],
                                   in_=x_r[:, 4 * i:4 * i + 4, :]), 16)
            for dst, src in ((self.wr_sb[:], d["wr"][:, :, :]),
                             (self.b1c[:], d["b1c"][:, :, :]),
                             (self.b2c[:], d["b2c"][:, :, :]),
                             (self.b1ac[:], d["b1ac"][:, :]),
                             (self.b2ac[:], d["b2ac"][:, :]),
                             (self.bqkvs[:], d["bqkvc"][:, :]),
                             (self.br_b[:], d["br_b"][:, :]),
                             (self.identf[:], d["identf"][:, :]),
                             (self.ones_f[:], d["ones1"][:, :]),
                             (self.gab[:], d["g_a"][:, :]),
                             (self.bab[:], d["be_a"][:, :]),
                             (self.gfh[:], d["g_f"][:, :]),
                             (self.bfh[:], d["be_f"][:, :])):
                dS.inc(e.dma_start(out=dst, in_=src), 16)
        self.Sy(init_dmas)
        init_dS = dS.rec()

        def init_casts(e):
            sIc.inc(e.dma_start(out=self.identb[:], in_=d["identf"][:, :]), 16)
            sIc.inc(e.dma_start(out=self.ones_b[:], in_=d["ones1"][:, :]), 16)
            sIc.inc(e.dma_start(out=self.bout_bf[:], in_=d["bout_r"][:, :]), 16)
        self.P(init_casts, dr=False)

        def init_zero(e):
            e.memset(self.stat[:], 0.0)
            e.memset(self.sm[:], 0.0)
            e.memset(self.rlog[:], 0.0)
            e.memset(self.req[:], 0.0)
            e.memset(self.rw[:], 0.0)
            self.sV.inc(e.memset(self.w_row[:], 0.0))
        self.V(init_zero, dr=False)
        init_zero_r = self.sV.rec()

        def init_scale(e):
            _wait(e, [init_dS])
            e.activation(self.xa_scr[:], self.gfh[:], AF.Identity, scale=0.5)
            e.drain()
            e.activation(self.gfh[:], self.xa_scr[:], AF.Identity)
            e.drain()
            e.activation(self.xa_scr[:], self.bfh[:], AF.Identity, scale=0.5)
            e.drain()
            e.activation(self.bfh[:], self.xa_scr[:], AF.Identity)
            e.drain()
            e.activation(self.xa_scr[:, 0:8], self.bqkvs[:, 0:8], AF.Identity, scale=0.125)
            e.drain()
            sA.inc(e.activation(self.bqkvs[:, 0:8], self.xa_scr[:, 0:8], AF.Identity))
        self.A(init_scale)
        self.x_ready = [init_dS, sA.rec()]
        self.w1_cached = {}
        self.w2_cached = {}
        self.w1a_cached = {}
        self.w2a_cached = {}

        self.barrier()
        if self.stop_after != "init":
            self._moe_call(0)
            self.barrier()
            if self.debug:
                self._dump_x("dbg_x1")
                self.barrier()
        if self.stop_after not in ("init", "moe0"):
            self._attn_block()
            self.barrier()
            if self.debug:
                self._dump_x("dbg_x2")
                self.barrier()
        if self.stop_after not in ("init", "moe0", "attn"):
            self._moe_call(1)
            self.barrier()

        def out_dma(e):
            y_r = d["y"].rearrange("(tc p) dd -> p tc dd", p=128)
            for i in range(4):
                self.dS.inc(e.dma_start(out=y_r[:, 4 * i:4 * i + 4, :],
                                        in_=self.x_tm[:, 4 * i:4 * i + 4, :]), 16)
            e.wait_ge(self.dS.h, self.dS.n)
        self.Sy(out_dma)

    def _dump_x(self, name):
        def f(e):
            y_r = self.d[name].rearrange("(tc p) dd -> p tc dd", p=128)
            for i in range(4):
                self.dS.inc(e.dma_start(out=y_r[:, 4 * i:4 * i + 4, :],
                                        in_=self.x_tm[:, 4 * i:4 * i + 4, :]), 16)
        self.Sy(f)

    def _ln_stats_chunk(self, src_ap, rstd_col, nmr_col, dep_recs, sq_scr):
        sA, sV = self.sA, self.sV
        ssq = self.stat[:, 0:1, 6]
        ssm = self.stat[:, 1:2, 6]
        mu = self.stat[:, 2:3, 6]
        var = self.stat[:, 0:1, 7]
        t2 = self.stat[:, 1:2, 7]
        lnv = self.stat[:, 2:3, 7]

        def a1(e):
            _wait(e, dep_recs)
            sA.inc(e.activation(sq_scr, src_ap, AF.Square, accum_out=ssq))
        self.A(a1)
        a1r = sA.rec()

        def v1(e):
            _wait(e, dep_recs + [a1r])
            e.reduce_sum(ssm, src_ap, axis=AX.X)
            e.drain()
            e.tensor_scalar(mu, ssm, 1.0 / D, None, op0=ALU.mult)
            e.tensor_scalar(var, ssq, 1.0 / D, None, op0=ALU.mult)
            e.drain()
            e.tensor_tensor(t2, mu, mu, op=ALU.mult)
            e.drain()
            sV.inc(e.scalar_tensor_tensor(var, var, EPS, t2,
                                          op0=ALU.add, op1=ALU.subtract))
        self.V(v1)
        v1r = sV.rec()

        def a2(e):
            _wait(e, [v1r])
            e.activation(lnv, var, AF.Ln)
            e.drain()
            sA.inc(e.activation(rstd_col, lnv, AF.Exp, scale=-0.5))
        self.A(a2)
        a2r = sA.rec()

        def v2(e):
            _wait(e, [a2r])
            sV.inc(e.scalar_tensor_tensor(nmr_col, mu, -1.0, rstd_col,
                                          op0=ALU.mult, op1=ALU.mult))
        self.V(v2)
        return sV.rec()

    def _mlp_block(self, lnT, hbuf, w1sl, w2sl, nw1buf, nw2buf,
                   w1_src, w1_cache, w1_cached, b1col,
                   w2_src, w2_cache, w2_cached, b2fn,
                   first, lnT_ready, state):
        sT, sA, sV = self.sT, self.sA, self.sV
        sW1, sW1c = self.sW1, self.sW1c
        sW2, sW2c = self.sW2, self.sW2c

        for fc in range(FCN):
            buf = fc % nw1buf
            cache_rec = None
            if first:
                def wd(e, fc=fc, buf=buf):
                    _wait(e, state["w1buf_free"][buf])
                    self.sW1x[buf].inc(e.dma_start(out=w1sl[:, buf, :, :], in_=w1_src(fc)), 16)
                self.P(wd, dr=False)
                dmar = self.sW1x[buf].rec()

                def wc(e, fc=fc, buf=buf, dmar=dmar):
                    _wait(e, [dmar])
                    sW1c[buf].inc(e.dma_start(out=w1_cache(fc), in_=w1sl[:, buf, :, :]), 16)
                self.Sy(wc)
                cache_rec = sW1c[buf].rec()
            else:
                def wd(e, fc=fc, buf=buf):
                    _wait(e, state["w1buf_free"][buf] + w1_cached["all"])
                    sW1[buf].inc(e.dma_start(out=w1sl[:, buf, :, :], in_=w1_cache(fc)), 16)
                self.A(wd, dr=False)
                dmar = sW1[buf].rec()

            hb = fc % 2
            psum = self.ps4[:, hb * 512:(hb + 1) * 512]

            def w1mm(e, fc=fc, buf=buf, psum=psum, dmar=dmar):
                _wait(e, [dmar, lnT_ready, state["h_evac"][fc - 2]])
                for dc in range(DCN):
                    mm = e.matmul(psum, w1sl[:, buf, dc, :], lnT[:, dc, :],
                             start=(dc == 0), stop=(dc == DCN - 1))
                sT.inc(mm)
            self.T(w1mm)
            state["pe_w1"][fc] = sT.rec()
            state["w1buf_free"][buf] = [state["pe_w1"][fc]] + \
                ([cache_rec] if cache_rec else [])

            def hev(e, fc=fc, psum=psum, gr=state["pe_w1"][fc]):
                _wait(e, [gr])
                sA.inc(e.activation(hbuf[:, fc, :], psum, AF.Relu, bias=b1col(fc)))
            self.A(hev)
            state["h_evac"][fc] = sA.rec()

        if first:
            w1_cached["all"] = [s.rec() for s in sW1c]
        h_all = state["h_evac"][FCN - 1]
        for dc in range(DCN):
            buf = dc % nw2buf
            cache_rec2 = None
            if first:
                def wd2(e, dc=dc, buf=buf):
                    _wait(e, state["w2buf_free"][buf])
                    self.sW2x[buf].inc(e.dma_start(out=w2sl[:, buf, :, :], in_=w2_src(dc)), 16)
                self.P(wd2, dr=False)
                dmar2 = self.sW2x[buf].rec()

                def wc2(e, dc=dc, buf=buf, dmar2=dmar2):
                    _wait(e, [dmar2])
                    sW2c[buf].inc(e.dma_start(out=w2_cache(dc), in_=w2sl[:, buf, :, :]), 16)
                self.Sy(wc2)
                cache_rec2 = sW2c[buf].rec()
            else:
                def wd2(e, dc=dc, buf=buf):
                    _wait(e, state["w2buf_free"][buf] + w2_cached["all"])
                    sW2[buf].inc(e.dma_start(out=w2sl[:, buf, :, :], in_=w2_cache(dc)), 16)
                self.Sy(wd2)
                dmar2 = sW2[buf].rec()

            ob = dc % 2
            psum = self.ps4[:, 1024 + ob * 512:1024 + (ob + 1) * 512]

            def w2mm(e, dc=dc, buf=buf, psum=psum, dmar2=dmar2,
                     pfree=state["o_evac"][dc - 2]):
                _wait(e, [dmar2, h_all, pfree])
                for fc in range(FCN):
                    mm = e.matmul(psum, w2sl[:, buf, fc, :], hbuf[:, fc, :],
                             start=(fc == 0), stop=(fc == FCN - 1))
                sT.inc(mm)
            self.T(w2mm)
            state["pe_w2"][dc] = sT.rec()
            state["w2buf_free"][buf] = [state["pe_w2"][dc]] + \
                ([cache_rec2] if cache_rec2 else [])

            b2fn(dc, psum, state["pe_w2"][dc])
        if first:
            w2_cached["all"] = [s.rec() for s in sW2c]

    def _moe_call(self, call):
        nc, d = self.nc, self.d
        sT, sA, sV, sP = self.sT, self.sA, self.sV, self.sP
        u = self.uid
        with ExitStack() as ph:
            sb = lambda n, sh, dt: ph.enter_context(nc.sbuf_tensor(u(n), sh, dt))
            lnT = sb("m_lnT", [128, DCN, TBS], BF16)
            u16 = sb("m_u16", [128, DCN, TBS], F32)
            hbuf = sb("m_h", [128, FCN, TBS], BF16)
            w1sl = sb("m_w1sl", [128, 4, DCN, 128], BF16)
            w2sl = sb("m_w2sl", [128, 2, FCN, 128], BF16)
            w_e = sb("m_we", [128, 2, TBS], F32)
            o_tmp = sb("m_ot", [128, 2, TBS], F32)

            state = dict(
                h_evac=[None] * FCN, pe_w1=[None] * FCN,
                o_evac=[None] * DCN, pe_w2=[None] * DCN,
                w1buf_free=[[] for _ in range(4)],
                w2buf_free=[[] for _ in range(2)],
            )
            o_add = [None] * DCN
            we_ready = [None, None]
            we_free = [None, None]

            for tb in range(self.n_tb):
                chunks = [tb * CPB + i for i in range(CPB)]
                lnF = u16
                lnF_ready = None
                prev_tr1 = getattr(self, "_moe_prev_tr1", None)
                for i, tc in enumerate(chunks):
                    rstd = self.stat[:, tc:tc + 1, 0]
                    nmr = self.stat[:, tc:tc + 1, 1]
                    dep = (self.x_ready if call == 0 else []) + \
                        [getattr(self, "_prev_epg", None)]
                    st_r = self._ln_stats_chunk(self.x_tm[:, tc, :], rstd, nmr,
                                                dep, self.t1_scr[:])

                    def ap1(e, tc=tc, rstd=rstd, nmr=nmr, st_r=st_r, pt=prev_tr1,
                            xu=getattr(self, "_prev_xupd", None)):
                        _wait(e, [st_r, pt, xu])
                        sA.inc(e.activation(self.xa_scr[:], self.x_tm[:, tc, :],
                                            AF.Identity, bias=nmr, scale=rstd))
                    self.A(ap1)
                    ap1r = sA.rec()

                    def tr1(e, ap1r=ap1r, prev=lnF_ready):
                        _wait(e, [ap1r, prev])
                        for g in range(2):
                            for j in range(4):
                                dcx = g * 4 + j
                                mm = e.matmul(self.ps2[:, g * 512 + j * 128:g * 512 + (j + 1) * 128],
                                         self.xa_scr[:, dcx * 128:(dcx + 1) * 128],
                                         self.identf[:], start=(j == 0), stop=(j == 3),
                                         is_transpose=True)
                        sT.inc(mm)
                    self.T(tr1)
                    tr1r = sT.rec()
                    prev_tr1 = tr1r
                    self._moe_prev_tr1 = tr1r

                    def ev1(e, i=i, tr1r=tr1r):
                        _wait(e, [tr1r])
                        for g in range(2):
                            sV.inc(e.tensor_copy(
                                lnF.ap()[:, g * 4:(g + 1) * 4, i * 128:(i + 1) * 128],
                                self.ps2.ap()[:, g * 512:(g + 1) * 512]
                                .rearrange("p (a b) -> p a b", a=4)))
                    self.V(ev1)
                    lnF_ready = sV.rec()

                def cast_ln(e, lnF_ready=lnF_ready):
                    _wait(e, [lnF_ready])
                    sA.inc(e.copy(lnT[:], lnF[:]))
                self.A(cast_ln)
                lnT_ready = sA.rec()

                rmm_r = None
                router_r = None
                if not self.do_router:
                    pass
                def rmm(e, lnF_ready=lnF_ready):
                    _wait(e, [lnF_ready])
                    for i in range(CPB):
                        for dc in range(DCN):
                            mm = e.matmul(self.pss[:, i * 8:(i + 1) * 8],
                                     lnF[:, dc, i * 128:(i + 1) * 128],
                                     self.wr_sb[:, dc, :],
                                     start=(i == 0 and dc == 0),
                                     stop=(i == CPB - 1 and dc == DCN - 1))
                    sT.inc(mm)
                if self.do_router:
                    self.T(rmm)
                    rmm_r = sT.rec()

                def rsoft(e, rmm_r=rmm_r):
                    _wait(e, [rmm_r])
                    for i in range(CPB):
                        e.tensor_tensor(self.rlog[:, i, :], self.pss[:, i * 8:(i + 1) * 8],
                                        self.br_b[:], op=ALU.add)
                    e.drain()
                    m1 = self.stat[:, 0:CPB, 2]
                    e.reduce_max(m1, self.rlog[:], axis=AX.X)
                    e.drain()
                    for i in range(CPB):
                        e.tensor_scalar(self.req[:, 0, :], self.rlog[:, i, :],
                                        self.stat[:, i:i + 1, 2], None, op0=ALU.is_equal)
                        e.drain()
                        e.scalar_tensor_tensor(self.req[:, 1, :], self.req[:, 0, :], -1e30,
                                               self.rlog[:, i, :], op0=ALU.mult, op1=ALU.add)
                        e.drain()
                        e.reduce_max(self.stat[:, i:i + 1, 3], self.req[:, 1, :], axis=AX.X)
                        e.drain()
                    sV.inc(e.tensor_tensor(self.stat[:, 0:CPB, 4],
                                           self.stat[:, 0:CPB, 3],
                                           self.stat[:, 0:CPB, 2], op=ALU.subtract))
                if self.do_router:
                    self.V(rsoft)
                rsoft_r = sV.rec() if self.do_router else None

                def rtanh(e, rsoft_r=rsoft_r):
                    _wait(e, [rsoft_r])
                    sA.inc(e.activation(self.stat[:, 0:CPB, 5], self.stat[:, 0:CPB, 4],
                                        AF.Tanh, scale=0.5))
                if self.do_router:
                    self.A(rtanh)
                rtanh_r = sA.rec() if self.do_router else None

                def rfin(e, rtanh_r=rtanh_r):
                    _wait(e, [rtanh_r])
                    w2v = self.stat[:, 0:CPB, 4]
                    w1v = self.stat[:, 0:CPB, 5]
                    e.tensor_scalar(w2v, self.stat[:, 0:CPB, 5], 0.5, 0.5,
                                    op0=ALU.mult, op1=ALU.add)
                    e.drain()
                    e.tensor_scalar(w1v, w2v, -1.0, 1.0, op0=ALU.mult, op1=ALU.add)
                    e.drain()
                    last = None
                    for i in range(CPB):
                        e.tensor_scalar(self.req[:, 0, :], self.rlog[:, i, :],
                                        self.stat[:, i:i + 1, 2], None, op0=ALU.is_equal)
                        e.tensor_scalar(self.req[:, 1, :], self.rlog[:, i, :],
                                        self.stat[:, i:i + 1, 3], None, op0=ALU.is_equal)
                        e.drain()
                        e.tensor_scalar(self.req[:, 0, :], self.req[:, 0, :],
                                        self.stat[:, i:i + 1, 5], None, op0=ALU.mult)
                        e.drain()
                        last = e.scalar_tensor_tensor(self.rw[:, i, :], self.req[:, 1, :],
                                                      self.stat[:, i:i + 1, 4],
                                                      self.req[:, 0, :],
                                                      op0=ALU.mult, op1=ALU.add)
                        e.drain()
                    sV.inc(last)
                if self.do_router:
                    self.V(rfin)
                    router_r = sV.rec()

                def acc0(e, a=lnT_ready, b=rmm_r, c=router_r):
                    _wait(e, [a, b, c])
                    sP.inc(e.memset(u16[:], 0.0))
                self.P(acc0)
                o_add = [sP.rec()] * DCN

                first = (call == 0 and tb == 0)
                for ei in range(self.n_exp):
                    web = ei % 2

                    def wet(e, ei=ei, rr=router_r, wf=we_free[web],
                            wprev=we_ready[1 - web]):
                        _wait(e, [rr, wf, wprev])
                        for i in range(CPB):
                            mm = e.matmul(self.pss.ap()[0:1, i * 128:(i + 1) * 128],
                                     self.rw[:, i, ei:ei + 1], self.identf[:],
                                     start=(i == 0), stop=(i == CPB - 1),
                                     is_transpose=True)
                        sT.inc(mm)
                    self.T(wet)
                    wet_r = sT.rec()

                    def wev(e, wet_r=wet_r):
                        _wait(e, [wet_r])
                        sV.inc(e.tensor_copy(self.w_row[:], self.pss.ap()[0:1, 0:TBS]))
                    self.V(wev)
                    wev_r = sV.rec()

                    def webm(e, wev_r=wev_r):
                        _wait(e, [wev_r])
                        sT.inc(e.matmul(self.pss[:, 0:TBS], self.ones_f[:], self.w_row[:],
                                        start=True, stop=True))
                    self.T(webm)
                    webm_r = sT.rec()

                    def webe(e, web=web, webm_r=webm_r):
                        _wait(e, [webm_r])
                        sV.inc(e.tensor_copy(w_e[:, web, :], self.pss[:, 0:TBS]))
                    self.V(webe)
                    we_ready[web] = sV.rec()

                    def b2fn(dc, psum, group_rec, ei=ei, web=web):
                        def oev(e, dc=dc, psum=psum, gr=group_rec,
                                wer=we_ready[web], pf=o_add[(dc - 2) % DCN]):
                            _wait(e, [gr, wer, pf])
                            sV.inc(e.scalar_tensor_tensor(
                                o_tmp[:, dc % 2, :], psum, self.b2c[:, ei, dc:dc + 1],
                                w_e[:, web, :], op0=ALU.add, op1=ALU.mult))
                        self.V(oev)
                        state["o_evac"][dc] = sV.rec()
                        we_free[web] = sV.rec()

                        def oad(e, dc=dc, er=state["o_evac"][dc]):
                            _wait(e, [er])
                            sP.inc(e.tensor_tensor(u16[:, dc, :], u16[:, dc, :],
                                                   o_tmp[:, dc % 2, :], op=ALU.add))
                        self.P(oad)
                        o_add[dc] = sP.rec()

                    self._mlp_block(
                        lnT, hbuf, w1sl, w2sl, 4, 2,
                        lambda fc, ei=ei: d["w1s"][ei, fc, :, :, :],
                        lambda fc, ei=ei: d["w1b"][ei, fc, :, :, :],
                        self._kcache("w1", ei),
                        lambda fc, ei=ei: self.b1c[:, ei, fc:fc + 1],
                        lambda dc, ei=ei: d["w2s"][ei, dc, :, :, :],
                        lambda dc, ei=ei: d["w2b"][ei, dc, :, :, :],
                        self._kcache("w2", ei),
                        b2fn, first, lnT_ready, state)

                dbg_acc_deps = [o_add[dc] for dc in range(DCN)] + [router_r, we_ready[0]]

                if not self.do_ep:
                    continue
                acc_done = [o_add[dc] for dc in range(DCN)]
                ep_ev = None
                x_upd_prev = None
                epg_prev = None
                for i, tc in enumerate(chunks):
                    def ept(e, i=i, deps=acc_done, prev=ep_ev):
                        _wait(e, deps + [prev])
                        for g in range(2):
                            for j in range(4):
                                dcx = g * 4 + j
                                mm = e.matmul(self.ps2[:, g * 512 + j * 128:g * 512 + (j + 1) * 128],
                                         u16[:, dcx, i * 128:(i + 1) * 128],
                                         self.identf[:], start=(j == 0), stop=(j == 3),
                                         is_transpose=True)
                        sT.inc(mm)
                    self.T(ept)
                    eptr = sT.rec()

                    def epv(e, eptr=eptr, xup=x_upd_prev):
                        _wait(e, [eptr, xup])
                        sV.inc(e.tensor_copy(self.t1_scr[:], self.ps2[:]))
                    self.V(epv)
                    ep_ev = sV.rec()

                    def epa(e, tc=tc, pg=epg_prev, xu=x_upd_prev):
                        _wait(e, [pg, xu])
                        rstd = self.stat[:, tc:tc + 1, 0]
                        nmr = self.stat[:, tc:tc + 1, 1]
                        sA.inc(e.activation(self.xa_scr[:], self.x_tm[:, tc, :],
                                            AF.Identity, bias=nmr, scale=rstd))
                    self.A(epa)
                    epar = sA.rec()

                    def epz(e, epar=epar, evr=ep_ev):
                        _wait(e, [epar, evr])
                        sP.inc(e.tensor_tensor(self.z_scr[:], self.xa_scr[:],
                                               self.t1_scr[:], op=ALU.add))
                    self.P(epz)
                    z_r = sP.rec()

                    rstdz = self.stat[:, 8 + i:9 + i, 4]
                    nmrz = self.stat[:, 8 + i:9 + i, 5]
                    st_r = self._ln_stats_chunk(self.z_scr[:], rstdz, nmrz, [z_r],
                                                self.t1_scr[:])

                    def epn(e, st_r=st_r, rstdz=rstdz, nmrz=nmrz):
                        _wait(e, [st_r])
                        sA.inc(e.activation(self.t1_scr[:], self.z_scr[:], AF.Identity,
                                            bias=nmrz, scale=rstdz))
                    self.A(epn)
                    epnr = sA.rec()

                    def epg(e, epnr=epnr):
                        _wait(e, [epnr])
                        sV.inc(e.tensor_tensor(self.xa_scr[:], self.t1_scr[:],
                                               self.gfh[:], op=ALU.mult))
                    self.V(epg)
                    epgr = sV.rec()
                    epg_prev = epgr
                    self._prev_epg = epgr

                    def epx(e, tc=tc, epgr=epgr):
                        _wait(e, [epgr])
                        e.tensor_tensor(self.x_tm[:, tc, :], self.x_tm[:, tc, :],
                                        self.xa_scr[:], op=ALU.add)
                        e.drain()
                        sP.inc(e.tensor_tensor(self.x_tm[:, tc, :], self.x_tm[:, tc, :],
                                               self.bfh[:], op=ALU.add))
                    self.P(epx)
                    x_upd_prev = sP.rec()
                    self._prev_xupd = x_upd_prev

                if self.debug and call == 0 and tb == 0 and self.n_tb == 1:
                    def dbg_dump(e, deps=dbg_acc_deps + [x_upd_prev, sA.rec(),
                                                         sV.rec(), sT.rec()]):
                        _wait(e, deps)
                        self.dS.inc(e.dma_start(out=d["dbg_rw"][:, :, :], in_=self.rw[:]), 16)
                        self.dS.inc(e.dma_start(out=d["dbg_acc"][:, :, :], in_=u16[:]), 16)
                        self.dS.inc(e.dma_start(out=d["dbg_we"][:, :], in_=w_e[:, 0, :]), 16)
                        self.dS.inc(e.dma_start(out=d["dbg_stat"][:, :, :], in_=self.stat[:]), 16)
                        self.dS.inc(e.dma_start(out=d["dbg_rlog"][:, :, :], in_=self.rlog[:]), 16)
                    self.Sy(dbg_dump)

    def _kcache(self, kind, ei):
        base = self.w1_cached if kind == "w1" else self.w2_cached
        return base.setdefault(ei, {})

    def _attn_block(self):
        nc, d = self.nc, self.d
        sT, sA, sV, sP = self.sT, self.sA, self.sV, self.sP
        dS, sQ, sWo = self.dS, self.sQ, self.sWo
        u = self.uid
        with ExitStack() as ph:
            sbA = lambda n, sh, dt: ph.enter_context(nc.sbuf_tensor(u(n), sh, dt))
            hcT = sbA("a_hcT", [128, DCN, S], BF16)
            h1v = hcT.ap().rearrange("p dc (tc m) -> p dc tc m", m=128)
            h1_written = [None] * TCN
            hc_last = None

            with ExitStack() as ph2:
                sb2 = lambda n, sh, dt: ph2.enter_context(nc.sbuf_tensor(u(n), sh, dt))
                ln1T = sb2("a_ln1T", [128, DCN, S], BF16)
                qkvT = sb2("a_qkvT", [128, 3, S], BF16)
                qkvw = sb2("a_qkvw", [128, 2, 3, DCN, 128], BF16)
                abuf = sb2("a_abuf", [128, S], BF16)
                atb = sb2("a_atb", [128, 16, 128], BF16)
                vst = sb2("a_vst", [128, 16, HD], BF16)

                ln_ev = None
                prev_tr1 = getattr(self, "_moe_prev_tr1", None)
                for tc in range(TCN):
                    st_r = self._ln_stats_chunk(self.x_tm[:, tc, :],
                                                self.rstd1[:, tc:tc + 1],
                                                self.nmr1[:, tc:tc + 1], [],
                                                self.t1_scr[:])

                    def ap1(e, tc=tc, st_r=st_r, pt=prev_tr1):
                        _wait(e, [st_r, pt])
                        sA.inc(e.activation(self.xa_scr[:], self.x_tm[:, tc, :],
                                            AF.Identity,
                                            bias=self.nmr1[:, tc:tc + 1],
                                            scale=self.rstd1[:, tc:tc + 1]))
                    self.A(ap1)
                    ap1r = sA.rec()

                    def tr1(e, ap1r=ap1r, prev=ln_ev):
                        _wait(e, [ap1r, prev])
                        for g in range(2):
                            for j in range(4):
                                dcx = g * 4 + j
                                mm = e.matmul(self.ps2[:, g * 512 + j * 128:g * 512 + (j + 1) * 128],
                                         self.xa_scr[:, dcx * 128:(dcx + 1) * 128],
                                         self.identf[:], start=(j == 0), stop=(j == 3),
                                         is_transpose=True)
                        sT.inc(mm)
                    self.T(tr1)
                    tr1r = sT.rec()
                    prev_tr1 = tr1r
                    self._moe_prev_tr1 = tr1r

                    def ev1(e, tc=tc, tr1r=tr1r):
                        _wait(e, [tr1r])
                        for g in range(2):
                            sV.inc(e.tensor_copy(
                                ln1T.ap()[:, g * 4:(g + 1) * 4, tc * 128:(tc + 1) * 128],
                                self.ps2.ap()[:, g * 512:(g + 1) * 512]
                                .rearrange("p (a b) -> p a b", a=4)))
                    self.V(ev1)
                    ln_ev = sV.rec()
                ln1_ready = ln_ev

                qkv_dma = [None, None]
                qkv_pe_done = [None, None]
                at_ev_last = None
                vev_last = None
                for hp in range(8):
                    qb = hp % 2
                    jcs = [hp, 8 + hp, 16 + hp]

                    def qwd(e, jcs=jcs, qb=qb, free=qkv_pe_done[qb]):
                        _wait(e, [free])
                        for si, jc in enumerate(jcs):
                            sQ[qb].inc(e.dma_start(out=qkvw[:, qb, si, :, :],
                                                   in_=d["wqkv"][jc, :, :, :]), 16)
                    self.P(qwd, dr=False)
                    qkv_dma[qb] = sQ[qb].rec()

                    qev_last = None
                    for si in range(3):
                        for tq in range(4):
                            hb = (si * 4 + tq) % 2
                            psum = self.ps4[:, hb * 512:(hb + 1) * 512]

                            def qmm(e, si=si, tq=tq, qb=qb, psum=psum, dr=qkv_dma[qb],
                                    ev=qev_last):
                                _wait(e, [dr, ln1_ready, ev])
                                for dc in range(DCN):
                                    mm = e.matmul(psum, qkvw[:, qb, si, dc, :],
                                             ln1T[:, dc, tq * 512:(tq + 1) * 512],
                                             start=(dc == 0), stop=(dc == DCN - 1))
                                sT.inc(mm)
                            self.T(qmm)
                            qmr = sT.rec()

                            def qev(e, si=si, tq=tq, psum=psum, qmr=qmr, hp=hp):
                                _wait(e, [qmr])
                                col = [hp, 8 + hp, 16 + hp][si]
                                sA.inc(e.activation(
                                    qkvT[:, si, tq * 512:(tq + 1) * 512], psum,
                                    AF.Identity, bias=self.bqkvs[:, col:col + 1],
                                    scale=0.125 if si == 0 else 1.0))
                            self.A(qev)
                            qev_last = sA.rec()
                    qkv_ready = qev_last

                    for hh in range(2):
                        h = 2 * hp + hh
                        s = STRIDES[h]
                        Ls = S // s
                        KC = Ls // 128
                        off = 64 * hh

                        def vtr(e, s=s, off=off, KC=KC, qr=qkv_ready):
                            _wait(e, [qr])
                            for kc in range(KC):
                                src = qkvT.ap()[off:off + 64, 2,
                                                kc * 128 * s:(kc * 128 + 128) * s:s]
                                mm = e.matmul(self.psb[:, kc * 64:(kc + 1) * 64], src,
                                         self.identb.ap()[off:off + 64, off:off + 64],
                                         start=(kc == 0), stop=(kc == KC - 1),
                                         is_transpose=True)
                            sT.inc(mm)
                        self.T(vtr)
                        vtr_r = sT.rec()

                        def vev(e, KC=KC, vtr_r=vtr_r):
                            _wait(e, [vtr_r])
                            sV.inc(e.tensor_copy(vst.ap()[:, 0:KC, :],
                                                 self.psb.ap()[:, 0:KC * 64]
                                                 .rearrange("p (a b) -> p a b", b=64)))
                        self.V(vev)
                        vev_r = sV.rec()
                        vev_last = vev_r

                        at_last = None
                        pss_gate = None
                        for qc in range(TCN):
                            qg = qc % 4
                            nmc = self.sm[:, 0:1, qg]
                            zc = self.sm[:, 1:2, qg]
                            bc2 = self.sm[:, 2:3, qg]

                            def smm(e, off=off, qc=qc, s=s, Ls=Ls, qr=qkv_ready,
                                    gate=at_last):
                                _wait(e, [qr, gate])
                                nkb = (Ls + 511) // 512
                                for kb in range(nkb):
                                    n = min(512, Ls - kb * 512)
                                    src = qkvT.ap()[off:off + 64, 1,
                                                    kb * 512 * s:(kb * 512 + n) * s:s]
                                    mm = e.matmul(self.ps4[:, kb * 512:kb * 512 + n],
                                             qkvT.ap()[off:off + 64, 0,
                                                       qc * 128:(qc + 1) * 128],
                                             src, start=True, stop=True)
                                sT.inc(mm)
                            self.T(smm)
                            smm_r = sT.rec()

                            def smax(e, Ls=Ls, nmc=nmc, smm_r=smm_r):
                                _wait(e, [smm_r])
                                sV.inc(e.reduce_max(nmc, self.ps4[:, 0:Ls], axis=AX.X,
                                                    negate=True))
                            self.V(smax)
                            smax_r = sV.rec()

                            def sexp1(e, Ls=Ls, nmc=nmc, zc=zc, smax_r=smax_r,
                                      gate=at_last):
                                _wait(e, [smax_r, gate])
                                e.activation(abuf[:, 0:Ls], self.ps4[:, 0:Ls], AF.Exp,
                                             bias=nmc, accum_out=zc)
                                e.drain()
                                sA.inc(e.activation(self.sm[:, 3:4, qg], zc, AF.Ln))
                            self.A(sexp1)
                            sexp1_r = sA.rec()

                            def sbias(e, nmc=nmc, bc2=bc2, r=sexp1_r):
                                _wait(e, [r])
                                sV.inc(e.tensor_tensor(bc2, nmc, self.sm[:, 3:4, qg],
                                                       op=ALU.subtract))
                            self.V(sbias)
                            sbias_r = sV.rec()

                            def sexp2(e, Ls=Ls, bc2=bc2, r=sbias_r):
                                _wait(e, [r])
                                sA.inc(e.activation(abuf[:, 0:Ls], self.ps4[:, 0:Ls],
                                                    AF.Exp, bias=bc2))
                            self.A(sexp2)
                            sexp2_r = sA.rec()

                            at_ready = None
                            nbat = (KC + 7) // 8
                            at_evs = []
                            for bi in range(nbat):
                                k0, k1 = bi * 8, min(KC, bi * 8 + 8)

                                def atrb(e, k0=k0, k1=k1, r=sexp2_r, vr=vev_r,
                                         aev=at_evs[-1] if at_evs else None):
                                    _wait(e, [r, vr, aev])
                                    for kc in range(k0, k1):
                                        mm = e.matmul(self.psb[:, (kc - k0) * 128:(kc - k0 + 1) * 128],
                                                 abuf[:, kc * 128:(kc + 1) * 128],
                                                 self.identb[:],
                                                 start=(kc == k0), stop=(kc == k1 - 1),
                                                 is_transpose=True)
                                    sT.inc(mm)
                                self.T(atrb)
                                atr_r = sT.rec()

                                def atev(e, k0=k0, k1=k1, atr_r=atr_r):
                                    _wait(e, [atr_r])
                                    sV.inc(e.tensor_copy(
                                        atb.ap()[:, k0:k1, :],
                                        self.psb.ap()[:, 0:(k1 - k0) * 128]
                                        .rearrange("p (a b) -> p a b", b=128)))
                                self.V(atev)
                                at_evs.append(sV.rec())
                                at_ev_last = sV.rec()
                            at_ready = at_evs[-1]
                            at_last = sT.rec()

                            def avm(e, qg=qg, KC=KC, atr=at_ready, vr=vev_r,
                                    gate=pss_gate):
                                _wait(e, [atr, vr] + ([gate] if qg == 0 else []))
                                for kc in range(KC):
                                    mm = e.matmul(self.pss.ap()[0:64, qg * 128:(qg + 1) * 128],
                                             vst[:, kc, :], atb[:, kc, :],
                                             start=(qg == 0 and kc == 0),
                                             stop=(qg == 3 and kc == KC - 1))
                                sT.inc(mm)
                            self.T(avm)
                            av_r = sT.rec()

                            if qg == 3:
                                def avev(e, h=h, qc=qc, av_r=av_r):
                                    _wait(e, [av_r])
                                    dst = hcT.ap()[64 * (h % 2):64 * (h % 2) + 64,
                                                   h // 2, (qc - 3) * 128:(qc + 1) * 128]
                                    sA.inc(e.activation(dst, self.pss.ap()[0:64, 0:512],
                                                        AF.Identity))
                                self.A(avev)
                                hc_last = sA.rec()
                                pss_gate = hc_last

                    qkv_pe_done[qb] = sT.rec()

            if self.debug:
                def dump_hc(e, hl=hc_last, qd=qkv_pe_done[1], aev=at_ev_last,
                            vv=vev_last):
                    _wait(e, [hl, qd, aev, vv])
                    self.dS.inc(e.dma_start(out=d["dbg_hc"][:, :, :], in_=hcT[:]), 16)
                    self.dS.inc(e.dma_start(out=d["dbg_a"][:, :], in_=abuf[:]), 16)
                    self.dS.inc(e.dma_start(out=d["dbg_qkv"][:, :, :], in_=qkvT[:]), 16)
                    self.dS.inc(e.dma_start(out=d["dbg_at"][:, :, :], in_=atb[:]), 16)
                    self.dS.inc(e.dma_start(out=d["dbg_vs"][:, :, :], in_=vst[:]), 16)
                self.Sy(dump_hc)
            self.barrier()
            with ExitStack() as ph3:
                sb3 = lambda n, sh, dt: ph3.enter_context(nc.sbuf_tensor(u(n), sh, dt))
                wout_sb = sb3("a_wout", [128, DCN, D], BF16)

                def wod(e):
                    sWo.inc(e.dma_start(out=wout_sb[:], in_=d["wout"]
                                        .rearrange("(dc p) n -> p dc n", p=128)), 16)
                self.P(wod, dr=False)
                wout_r = sWo.rec()

                z_prev = None
                wog_prev = None
                h1w_prev = None
                for tc in range(TCN):
                    def womm(e, tc=tc, wr=wout_r, hca=hc_last, zp=z_prev):
                        _wait(e, [wr, hca, zp])
                        for half in range(2):
                            for dc in range(DCN):
                                e.matmul(self.ps2[:, half * 512:(half + 1) * 512],
                                         hcT.ap()[:, dc, tc * 128:(tc + 1) * 128],
                                         wout_sb[:, dc, half * 512:(half + 1) * 512],
                                         start=(dc == 0), stop=False)
                            mm = e.matmul(self.ps2[:, half * 512:(half + 1) * 512],
                                     self.ones_b[:],
                                     self.bout_bf.ap()[:, half * 512:(half + 1) * 512],
                                     start=False, stop=True)
                        sT.inc(mm)
                    self.T(womm)
                    womm_r = sT.rec()

                    def woxa(e, tc=tc, wp=wog_prev):
                        _wait(e, [wp])
                        sA.inc(e.activation(self.xa_scr[:], self.x_tm[:, tc, :],
                                            AF.Identity,
                                            bias=self.nmr1[:, tc:tc + 1],
                                            scale=self.rstd1[:, tc:tc + 1]))
                    self.A(woxa)
                    woxa_r = sA.rec()

                    def woz(e, womm_r=womm_r, woxa_r=woxa_r):
                        _wait(e, [womm_r, woxa_r])
                        sV.inc(e.tensor_tensor(self.z_scr[:], self.xa_scr[:],
                                               self.ps2[:], op=ALU.add))
                    self.V(woz)
                    z_prev = sV.rec()

                    rstdh = self.sm[:, 4:5, 0]
                    nmrh = self.sm[:, 5:6, 0]
                    st_r = self._ln_stats_chunk(self.z_scr[:], rstdh, nmrh,
                                                [z_prev, h1w_prev], self.t1_scr[:])

                    def woh(e, st_r=st_r, rstdh=rstdh, nmrh=nmrh):
                        _wait(e, [st_r])
                        sA.inc(e.activation(self.xa_scr[:], self.z_scr[:], AF.Identity,
                                            bias=nmrh, scale=rstdh))
                    self.A(woh)
                    woh_r = sA.rec()

                    def wog(e, woh_r=woh_r):
                        _wait(e, [woh_r])
                        sV.inc(e.tensor_tensor(self.t1_scr[:], self.xa_scr[:],
                                               self.gab[:], op=ALU.mult))
                    self.V(wog)
                    wog_r = sV.rec()
                    wog_prev = wog_r

                    def woh1(e, tc=tc, wog_r=wog_r, hp_=h1w_prev):
                        _wait(e, [wog_r, hp_])
                        sP.inc(e.tensor_tensor(
                            h1v[:, :, tc, :],
                            self.t1_scr.ap().rearrange("p (a b) -> p a b", b=128),
                            self.bab.ap().rearrange("p (a b) -> p a b", b=128),
                            op=ALU.add))
                    self.P(woh1)
                    h1_written[tc] = sP.rec()
                    h1w_prev = h1_written[tc]
            self.barrier()

            with ExitStack() as ph4:
                sb4 = lambda n, sh, dt: ph4.enter_context(nc.sbuf_tensor(u(n), sh, dt))
                lnT = sb4("f_lnT", [128, DCN, TBS], BF16)
                u16 = sb4("f_u16", [128, DCN, TBS], F32)
                hbuf = sb4("f_h", [128, FCN, TBS], BF16)
                w1sl = sb4("f_w1sl", [128, 4, DCN, 128], BF16)
                w2sl = sb4("f_w2sl", [128, 1, FCN, 128], BF16)

                state = dict(
                    h_evac=[None] * FCN, pe_w1=[None] * FCN,
                    o_evac=[None] * DCN, pe_w2=[None] * DCN,
                    w1buf_free=[[] for _ in range(4)],
                    w2buf_free=[[]],
                )
                o_guard = [None] * DCN

                for tb in range(NTB):
                    chunks = [tb * CPB + i for i in range(CPB)]
                    first = (tb == 0)
                    lnq = None
                    for i, tc in enumerate(chunks):
                        def htr(e, tc=tc, h1r=h1_written[tc], prev=lnq):
                            _wait(e, [h1r, prev])
                            for dcx in range(DCN):
                                mm = e.matmul(self.psb[:, dcx * 128:(dcx + 1) * 128],
                                         h1v[:, dcx, tc, :], self.identb[:],
                                         start=(dcx == 0), stop=(dcx == DCN - 1),
                                         is_transpose=True)
                            sT.inc(mm)
                        self.T(htr)
                        htr_r = sT.rec()

                        def hev2(e, i=i, htr_r=htr_r):
                            _wait(e, [htr_r])
                            sV.inc(e.tensor_copy(lnT.ap()[:, :, i * 128:(i + 1) * 128],
                                                 self.psb.ap()[:, 0:1024]
                                                 .rearrange("p (a b) -> p a b", a=8)))
                        self.V(hev2)
                        lnq = sV.rec()
                    lnT_ready = lnq

                    def b2fn(dc, psum, group_rec):
                        def oev(e, dc=dc, psum=psum, gr=group_rec, og=o_guard[dc]):
                            _wait(e, [gr, og])
                            sV.inc(e.tensor_scalar(u16[:, dc, :], psum,
                                                   self.b2ac[:, dc:dc + 1], None,
                                                   op0=ALU.add))
                        self.V(oev)
                        state["o_evac"][dc] = sV.rec()

                    self._mlp_block(
                        lnT, hbuf, w1sl, w2sl, 4, 1,
                        lambda fc: d["w1a"][fc, :, :, :],
                        lambda fc: d["w1ab"][fc, :, :, :],
                        self.w1a_cached,
                        lambda fc: self.b1ac[:, fc:fc + 1],
                        lambda dc: d["w2a"][dc, :, :, :],
                        lambda dc: d["w2ab"][dc, :, :, :],
                        self.w2a_cached,
                        b2fn, first, lnT_ready, state)

                    ffn_done = [state["o_evac"][dc] for dc in range(DCN)]
                    ep_ev = None
                    x_upd_prev = None
                    ffn_epn_prev = getattr(self, "_ffn_epn_prev", [None])
                    self._ffn_epn_prev = ffn_epn_prev
                    for i, tc in enumerate(chunks):
                        def ept(e, i=i, deps=ffn_done, prev=ep_ev):
                            _wait(e, deps + [prev])
                            for g in range(2):
                                for j in range(4):
                                    dcx = g * 4 + j
                                    mm = e.matmul(self.ps2[:, g * 512 + j * 128:g * 512 + (j + 1) * 128],
                                             u16[:, dcx, i * 128:(i + 1) * 128],
                                             self.identf[:], start=(j == 0), stop=(j == 3),
                                             is_transpose=True)
                            sT.inc(mm)
                        self.T(ept)
                        eptr = sT.rec()
                        o_guard = [eptr] * DCN

                        def epv(e, eptr=eptr, xup=x_upd_prev):
                            _wait(e, [eptr, xup])
                            sV.inc(e.tensor_copy(self.t1_scr[:], self.ps2[:]))
                        self.V(epv)
                        ep_ev = sV.rec()

                        def epz(e, tc=tc, evr=ep_ev, h1r=h1_written[tc],
                                pn=ffn_epn_prev[0]):
                            _wait(e, [evr, h1r, pn])
                            sP.inc(e.tensor_tensor(
                                self.z_scr.ap().rearrange("p (a b) -> p a b", b=128),
                                self.t1_scr.ap().rearrange("p (a b) -> p a b", b=128),
                                h1v[:, :, tc, :], op=ALU.add))
                        self.P(epz)
                        z_r = sP.rec()

                        rstdz = self.sm[:, 6:7, 0]
                        nmrz = self.sm[:, 7:8, 0]
                        st_r = self._ln_stats_chunk(self.z_scr[:], rstdz, nmrz, [z_r],
                                                    self.t1_scr[:])

                        def epn(e, st_r=st_r, rstdz=rstdz, nmrz=nmrz):
                            _wait(e, [st_r])
                            sA.inc(e.activation(self.t1_scr[:], self.z_scr[:],
                                                AF.Identity, bias=nmrz, scale=rstdz))
                        self.A(epn)
                        epnr = sA.rec()
                        ffn_epn_prev[0] = epnr

                        def epg(e, epnr=epnr):
                            _wait(e, [epnr])
                            sV.inc(e.tensor_tensor(self.xa_scr[:], self.t1_scr[:],
                                                   self.gab[:], op=ALU.mult))
                        self.V(epg)
                        epgr = sV.rec()

                        def epx(e, tc=tc, epgr=epgr):
                            _wait(e, [epgr])
                            e.tensor_tensor(self.x_tm[:, tc, :], self.x_tm[:, tc, :],
                                            self.xa_scr[:], op=ALU.add)
                            e.drain()
                            sP.inc(e.tensor_tensor(self.x_tm[:, tc, :],
                                                   self.x_tm[:, tc, :],
                                                   self.bab[:], op=ALU.add))
                        self.P(epx)
                        x_upd_prev = sP.rec()


def _prep_weights(ins):
    g = lambda k: np.ascontiguousarray(np.asarray(ins[k]))
    W1 = g("L0_ffn_W1")
    W2 = g("L0_ffn_W2")
    W1a = g("L0_attn_W1")
    W2a = g("L0_attn_W2")
    Wqkv = g("L0_attn_Wqkv")
    com = {}
    com["w1s"] = np.ascontiguousarray(
        W1.reshape(NE, FCN, 128, DCN, 128).transpose(0, 1, 4, 3, 2))
    com["w2s"] = np.ascontiguousarray(
        W2.reshape(NE, DCN, 128, FCN, 128).transpose(0, 1, 4, 3, 2))
    com["w1a"] = np.ascontiguousarray(
        W1a.reshape(DCN, 128, FCN, 128).transpose(2, 1, 0, 3))
    com["w2a"] = np.ascontiguousarray(
        W2a.reshape(FCN, 128, DCN, 128).transpose(2, 1, 0, 3))
    com["wqkv"] = np.ascontiguousarray(
        Wqkv.reshape(DCN, 128, 24, 128).transpose(2, 1, 0, 3))
    com["wout"] = g("L0_attn_Wout")
    com["wr"] = np.ascontiguousarray(
        g("L0_ffn_Wr").T.reshape(DCN, 128, NE).transpose(1, 0, 2))
    com["b1c"] = np.ascontiguousarray(
        g("L0_ffn_b1").reshape(NE, FCN, 128).transpose(2, 0, 1))
    com["b2c"] = np.ascontiguousarray(
        g("L0_ffn_b2").reshape(NE, DCN, 128).transpose(2, 0, 1))
    com["b1ac"] = np.ascontiguousarray(g("L0_attn_b1").reshape(FCN, 128).T)
    com["b2ac"] = np.ascontiguousarray(g("L0_attn_b2").reshape(DCN, 128).T)
    com["bqkvc"] = np.ascontiguousarray(g("L0_attn_bqkv").reshape(24, 128).T)
    com["br_b"] = np.ascontiguousarray(np.broadcast_to(g("L0_ffn_br"), (128, NE)))
    com["g_f"] = np.ascontiguousarray(np.broadcast_to(g("L0_ffn_gamma"), (128, D)))
    com["be_f"] = np.ascontiguousarray(np.broadcast_to(g("L0_ffn_beta"), (128, D)))
    com["g_a"] = np.ascontiguousarray(np.broadcast_to(g("L0_attn_gamma"), (128, D)))
    com["be_a"] = np.ascontiguousarray(np.broadcast_to(g("L0_attn_beta"), (128, D)))
    com["bout_r"] = np.ascontiguousarray(g("L0_attn_bout").reshape(1, D))
    com["identf"] = np.eye(128, dtype=np.float32)
    com["ones1"] = np.ones((1, 128), dtype=np.float32)
    return com


def kernel(**inputs):
    com = _prep_weights(inputs)
    xs = np.ascontiguousarray(np.asarray(inputs["L0_attn_x"]))
    nc = Builder(debug=False).build()
    in_maps = []
    for b in range(8):
        m = dict(com)
        m["x"] = np.ascontiguousarray(xs[b])
        in_maps.append(m)
    res = run_bass_kernel_spmd(nc, in_maps, core_ids=list(range(8)))
    return np.stack([np.asarray(r["y"]) for r in res.results], axis=0)


# version 02
# speedup vs baseline: 1.6676x; 1.6676x over previous
import numpy as np
import ml_dtypes
from contextlib import ExitStack

import concourse.bass as bass
import concourse.mybir as mybir
from concourse.bass_utils import run_bass_kernel_spmd

F32 = mybir.dt.float32
BF16 = mybir.dt.bfloat16
AF = mybir.ActivationFunctionType
ALU = mybir.AluOpType
AX = mybir.AxisListType

S = 2048
TCN = 16
D = 1024
DCN = 8
FF = 4096
FCN = 32
NE = 8
TBS = 512
NTB = S // TBS
CPB = TBS // 128
NH = 16
HD = 64
STRIDES = [1, 1, 2, 2, 2, 2, 4, 4, 4, 4, 8, 8, 8, 8, 16, 16]
EPS = 1e-5


class Sem:
    def __init__(self, ctx, nc, name):
        self.h = ctx.enter_context(nc.semaphore(name))
        self.n = 0

    def inc(self, instr, k=1):
        instr.then_inc(self.h, k)
        self.n += k
        return self.n

    def rec(self):
        return (self, self.n)


def _wait(e, recs):
    seen = {}
    for r in recs:
        if r is None:
            continue
        s, v = r
        if v > 0 and seen.get(id(s), -1) < v:
            seen[id(s)] = v
            e.wait_ge(s.h, v)


class Builder:
    def __init__(self, debug=False, stop_after="full", n_tb=NTB, n_exp=NE,
                 do_router=True, do_ep=True):
        self.debug = debug
        self.stop_after = stop_after
        self.n_tb = n_tb
        self.n_exp = n_exp
        self.do_router = do_router
        self.do_ep = do_ep
        self.nc = bass.Bass("TRN2", target_bir_lowering=False)
        self._uid = 0

    def uid(self, p):
        self._uid += 1
        return f"{p}{self._uid}"

    def T(self, fn): self.blk.tensor(fn)

    def _fence(self, sem, fn):
        n = sem.n
        def g(e):
            if n > 0:
                e.wait_ge(sem.h, n)
            fn(e)
        return g

    def A(self, fn, dr=True):
        self.blk.scalar(self._fence(self.sA, fn) if dr else fn)

    def V(self, fn, dr=True):
        self.blk.vector(self._fence(self.sV, fn) if dr else fn)

    def P(self, fn, dr=True):
        self.blk.gpsimd(self._fence(self.sP, fn) if dr else fn)

    def Sy(self, fn): self.blk.sync(fn)

    def barrier(self):
        sems = [self.sT, self.sA, self.sV, self.sP, self.dS, self.sIc,
                self.sWo] + self.sW1 + self.sW1c + self.sW2 + self.sW2c + \
            self.sQ + self.sW1x + self.sW2x
        recs = [s.rec() for s in sems]
        for em in (self.T, self.A, self.V, self.P, self.Sy):
            em(lambda e: _wait(e, recs))

    def build(self):
        nc = self.nc
        d = {}
        d["x"] = nc.dram_tensor("x", [S, D], F32, kind="ExternalInput").ap()
        d["w1s"] = nc.dram_tensor("w1s", [NE, FCN, 128, DCN, 128], F32, kind="ExternalInput").ap()
        d["w2s"] = nc.dram_tensor("w2s", [NE, DCN, 128, FCN, 128], F32, kind="ExternalInput").ap()
        d["w1a"] = nc.dram_tensor("w1a", [FCN, 128, DCN, 128], F32, kind="ExternalInput").ap()
        d["w2a"] = nc.dram_tensor("w2a", [DCN, 128, FCN, 128], F32, kind="ExternalInput").ap()
        d["wqkv"] = nc.dram_tensor("wqkv", [24, 128, DCN, 128], F32, kind="ExternalInput").ap()
        d["wout"] = nc.dram_tensor("wout", [D, D], F32, kind="ExternalInput").ap()
        d["wr"] = nc.dram_tensor("wr", [128, DCN, NE], F32, kind="ExternalInput").ap()
        d["b1c"] = nc.dram_tensor("b1c", [128, NE, FCN], F32, kind="ExternalInput").ap()
        d["b2c"] = nc.dram_tensor("b2c", [128, NE, DCN], F32, kind="ExternalInput").ap()
        d["b1ac"] = nc.dram_tensor("b1ac", [128, FCN], F32, kind="ExternalInput").ap()
        d["b2ac"] = nc.dram_tensor("b2ac", [128, DCN], F32, kind="ExternalInput").ap()
        d["bqkvc"] = nc.dram_tensor("bqkvc", [128, 24], F32, kind="ExternalInput").ap()
        d["br_b"] = nc.dram_tensor("br_b", [128, NE], F32, kind="ExternalInput").ap()
        d["g_f"] = nc.dram_tensor("g_f", [128, D], F32, kind="ExternalInput").ap()
        d["be_f"] = nc.dram_tensor("be_f", [128, D], F32, kind="ExternalInput").ap()
        d["g_a"] = nc.dram_tensor("g_a", [128, D], F32, kind="ExternalInput").ap()
        d["be_a"] = nc.dram_tensor("be_a", [128, D], F32, kind="ExternalInput").ap()
        d["bout_r"] = nc.dram_tensor("bout_r", [1, D], F32, kind="ExternalInput").ap()
        d["identf"] = nc.dram_tensor("identf", [128, 128], F32, kind="ExternalInput").ap()
        d["ones1"] = nc.dram_tensor("ones1", [1, 128], F32, kind="ExternalInput").ap()
        d["y"] = nc.dram_tensor("y", [S, D], F32, kind="ExternalOutput").ap()
        d["w1b"] = nc.dram_tensor("w1b", [NE, FCN, 128, DCN, 128], BF16, kind="Internal").ap()
        d["w2b"] = nc.dram_tensor("w2b", [NE, DCN, 128, FCN, 128], BF16, kind="Internal").ap()
        d["w1ab"] = nc.dram_tensor("w1ab", [FCN, 128, DCN, 128], BF16, kind="Internal").ap()
        d["w2ab"] = nc.dram_tensor("w2ab", [DCN, 128, FCN, 128], BF16, kind="Internal").ap()
        if self.debug:
            d["dbg_x1"] = nc.dram_tensor("dbg_x1", [S, D], F32, kind="ExternalOutput").ap()
            d["dbg_x2"] = nc.dram_tensor("dbg_x2", [S, D], F32, kind="ExternalOutput").ap()
            d["dbg_rw"] = nc.dram_tensor("dbg_rw", [128, CPB, NE], F32, kind="ExternalOutput").ap()
            d["dbg_acc"] = nc.dram_tensor("dbg_acc", [128, DCN, TBS], F32, kind="ExternalOutput").ap()
            d["dbg_we"] = nc.dram_tensor("dbg_we", [128, TBS], F32, kind="ExternalOutput").ap()
            d["dbg_stat"] = nc.dram_tensor("dbg_stat", [128, 16, 8], F32, kind="ExternalOutput").ap()
            d["dbg_hc"] = nc.dram_tensor("dbg_hc", [128, DCN, S], BF16, kind="ExternalOutput").ap()
            d["dbg_a"] = nc.dram_tensor("dbg_a", [128, S], BF16, kind="ExternalOutput").ap()
            d["dbg_qkv"] = nc.dram_tensor("dbg_qkv", [128, 3, S], BF16, kind="ExternalOutput").ap()
            d["dbg_at"] = nc.dram_tensor("dbg_at", [128, 16, 128], BF16, kind="ExternalOutput").ap()
            d["dbg_vs"] = nc.dram_tensor("dbg_vs", [128, 16, HD], BF16, kind="ExternalOutput").ap()
            d["dbg_rlog"] = nc.dram_tensor("dbg_rlog", [128, CPB, NE], F32, kind="ExternalOutput").ap()
        self.d = d

        with ExitStack() as ctx:
            self.ctx = ctx
            self._alloc_outer()
            self.blk = ctx.enter_context(nc.Block())
            self._emit()
        return nc

    def _alloc_outer(self):
        nc, ctx = self.nc, self.ctx
        sb = lambda n, sh, dt: ctx.enter_context(nc.sbuf_tensor(n, sh, dt))
        self.x_tm = sb("s_x", [128, TCN, D], F32)
        self.gfh = sb("s_gfh", [128, D], F32)
        self.bfh = sb("s_bfh", [128, D], F32)
        self.gab = sb("s_gab", [128, D], F32)
        self.bab = sb("s_bab", [128, D], F32)
        self.wr_sb = sb("s_wr", [128, DCN, NE], F32)
        self.b1c = sb("s_b1c", [128, NE, FCN], F32)
        self.b2c = sb("s_b2c", [128, NE, DCN], F32)
        self.b1ac = sb("s_b1ac", [128, FCN], F32)
        self.b2ac = sb("s_b2ac", [128, DCN], F32)
        self.bqkvs = sb("s_bqkvs", [128, 24], F32)
        self.br_b = sb("s_brb", [128, NE], F32)
        self.bout_bf = sb("s_boutbf", [1, D], BF16)
        self.identf = sb("s_idf", [128, 128], F32)
        self.identb = sb("s_idb", [128, 128], BF16)
        self.ones_f = sb("s_onesf", [1, 128], F32)
        self.ones_b = sb("s_onesb", [1, 128], BF16)
        self.stat = sb("s_stat", [128, 16, 8], F32)
        self.rstd1 = sb("s_rstd1", [128, TCN], F32)
        self.nmr1 = sb("s_nmr1", [128, TCN], F32)
        self.rlog = sb("s_rlog", [128, CPB, NE], F32)
        self.rw = sb("s_rw", [128, CPB, NE], F32)
        self.req = sb("s_req", [128, 2, NE], F32)
        self.w_row = sb("s_wrow", [1, TBS], F32)
        self.sm = sb("s_sm", [128, 8, 4], F32)
        self.xa_scr = sb("s_xa", [128, D], F32)
        self.t1_scr = sb("s_t1", [128, D], F32)
        self.z_scr = sb("s_z", [128, D], F32)
        self.ps4 = ctx.enter_context(nc.psum_tensor("p_ps4", [128, 2048], F32))
        self.ps2 = ctx.enter_context(nc.psum_tensor("p_ps2", [128, 1024], F32))
        self.psb = ctx.enter_context(nc.psum_tensor("p_psb", [128, 1024], BF16))
        self.pss = ctx.enter_context(nc.psum_tensor("p_pss", [128, 512], F32))
        self.sT = Sem(ctx, nc, "sT")
        self.sA = Sem(ctx, nc, "sA")
        self.sV = Sem(ctx, nc, "sV")
        self.sP = Sem(ctx, nc, "sP")
        self.dS = Sem(ctx, nc, "dS")
        self.sIc = Sem(ctx, nc, "sIc")
        self.sW1 = [Sem(ctx, nc, f"sW1_{i}") for i in range(4)]
        self.sW1x = [Sem(ctx, nc, f"sW1x_{i}") for i in range(4)]
        self.sW2x = [Sem(ctx, nc, f"sW2x_{i}") for i in range(2)]
        self.sW1c = [Sem(ctx, nc, f"sW1c_{i}") for i in range(4)]
        self.sW2 = [Sem(ctx, nc, f"sW2_{i}") for i in range(2)]
        self.sW2c = [Sem(ctx, nc, f"sW2c_{i}") for i in range(2)]
        self.sQ = [Sem(ctx, nc, f"sQ_{i}") for i in range(2)]
        self.sWo = Sem(ctx, nc, "sWo")

    def _emit(self):
        d = self.d
        dS, sIc, sA = self.dS, self.sIc, self.sA

        def init_dmas(e):
            x_r = d["x"].rearrange("(tc p) dd -> p tc dd", p=128)
            for i in range(4):
                dS.inc(e.dma_start(out=self.x_tm[:, 4 * i:4 * i + 4, :],
                                   in_=x_r[:, 4 * i:4 * i + 4, :]), 16)
            for dst, src in ((self.wr_sb[:], d["wr"][:, :, :]),
                             (self.b1c[:], d["b1c"][:, :, :]),
                             (self.b2c[:], d["b2c"][:, :, :]),
                             (self.b1ac[:], d["b1ac"][:, :]),
                             (self.b2ac[:], d["b2ac"][:, :]),
                             (self.bqkvs[:], d["bqkvc"][:, :]),
                             (self.br_b[:], d["br_b"][:, :]),
                             (self.identf[:], d["identf"][:, :]),
                             (self.ones_f[:], d["ones1"][:, :]),
                             (self.gab[:], d["g_a"][:, :]),
                             (self.bab[:], d["be_a"][:, :]),
                             (self.gfh[:], d["g_f"][:, :]),
                             (self.bfh[:], d["be_f"][:, :])):
                dS.inc(e.dma_start(out=dst, in_=src), 16)
        self.Sy(init_dmas)
        init_dS = dS.rec()

        def init_casts(e):
            sIc.inc(e.dma_start(out=self.identb[:], in_=d["identf"][:, :]), 16)
            sIc.inc(e.dma_start(out=self.ones_b[:], in_=d["ones1"][:, :]), 16)
            sIc.inc(e.dma_start(out=self.bout_bf[:], in_=d["bout_r"][:, :]), 16)
        self.P(init_casts, dr=False)

        def init_zero(e):
            e.memset(self.stat[:], 0.0)
            e.memset(self.sm[:], 0.0)
            e.memset(self.rlog[:], 0.0)
            e.memset(self.req[:], 0.0)
            e.memset(self.rw[:], 0.0)
            self.sV.inc(e.memset(self.w_row[:], 0.0))
        self.V(init_zero, dr=False)
        init_zero_r = self.sV.rec()

        def init_scale(e):
            _wait(e, [init_dS])
            e.activation(self.xa_scr[:], self.gfh[:], AF.Identity, scale=0.5)
            e.drain()
            e.activation(self.gfh[:], self.xa_scr[:], AF.Identity)
            e.drain()
            e.activation(self.xa_scr[:], self.bfh[:], AF.Identity, scale=0.5)
            e.drain()
            e.activation(self.bfh[:], self.xa_scr[:], AF.Identity)
            e.drain()
            e.activation(self.xa_scr[:, 0:8], self.bqkvs[:, 0:8], AF.Identity, scale=0.125)
            e.drain()
            sA.inc(e.activation(self.bqkvs[:, 0:8], self.xa_scr[:, 0:8], AF.Identity))
        self.A(init_scale)
        self.x_ready = [init_dS, sA.rec()]
        self.w1_cached = {}
        self.w2_cached = {}
        self.w1a_cached = {}
        self.w2a_cached = {}

        self.barrier()
        if self.stop_after != "init":
            self._moe_call(0)
            self.barrier()
            if self.debug:
                self._dump_x("dbg_x1")
                self.barrier()
        if self.stop_after not in ("init", "moe0"):
            self._attn_block()
            self.barrier()
            if self.debug:
                self._dump_x("dbg_x2")
                self.barrier()
        if self.stop_after not in ("init", "moe0", "attn"):
            self._moe_call(1)
            self.barrier()

        def out_dma(e):
            y_r = d["y"].rearrange("(tc p) dd -> p tc dd", p=128)
            for i in range(4):
                self.dS.inc(e.dma_start(out=y_r[:, 4 * i:4 * i + 4, :],
                                        in_=self.x_tm[:, 4 * i:4 * i + 4, :]), 16)
            e.wait_ge(self.dS.h, self.dS.n)
        self.Sy(out_dma)

    def _dump_x(self, name):
        def f(e):
            y_r = self.d[name].rearrange("(tc p) dd -> p tc dd", p=128)
            for i in range(4):
                self.dS.inc(e.dma_start(out=y_r[:, 4 * i:4 * i + 4, :],
                                        in_=self.x_tm[:, 4 * i:4 * i + 4, :]), 16)
        self.Sy(f)

    def _ln_stats_chunk(self, src_ap, rstd_col, nmr_col, dep_recs, sq_scr):
        sA, sV = self.sA, self.sV
        ssq = self.stat[:, 0:1, 6]
        ssm = self.stat[:, 1:2, 6]
        mu = self.stat[:, 2:3, 6]
        var = self.stat[:, 0:1, 7]
        t2 = self.stat[:, 1:2, 7]
        lnv = self.stat[:, 2:3, 7]

        def v1(e):
            _wait(e, dep_recs)
            e.scalar_tensor_tensor(sq_scr, src_ap, 1.0, src_ap,
                                   op0=ALU.mult, op1=ALU.mult, accum_out=ssq)
            e.reduce_sum(ssm, src_ap, axis=AX.X)
            e.drain()
            e.tensor_scalar(mu, ssm, 1.0 / D, None, op0=ALU.mult)
            e.tensor_scalar(var, ssq, 1.0 / D, None, op0=ALU.mult)
            e.drain()
            e.tensor_tensor(t2, mu, mu, op=ALU.mult)
            e.drain()
            sV.inc(e.scalar_tensor_tensor(var, var, EPS, t2,
                                          op0=ALU.add, op1=ALU.subtract))
        self.V(v1)
        v1r = sV.rec()

        def a2(e):
            _wait(e, [v1r])
            e.activation(lnv, var, AF.Ln)
            e.drain()
            sA.inc(e.activation(rstd_col, lnv, AF.Exp, scale=-0.5))
        self.A(a2)
        a2r = sA.rec()

        def v2(e):
            _wait(e, [a2r])
            sV.inc(e.scalar_tensor_tensor(nmr_col, mu, -1.0, rstd_col,
                                          op0=ALU.mult, op1=ALU.mult))
        self.V(v2)
        return sV.rec()

    def _mlp_block(self, lnT, hbuf, w1sl, w2sl, nw1buf, nw2buf,
                   w1_src, w1_cache, w1_cached, b1col,
                   w2_src, w2_cache, w2_cached, b2fn,
                   first, lnT_ready, state):
        sT, sA, sV = self.sT, self.sA, self.sV
        sW1, sW1c = self.sW1, self.sW1c
        sW2, sW2c = self.sW2, self.sW2c

        for fc in range(FCN):
            buf = fc % nw1buf
            cache_rec = None
            if first:
                def wd(e, fc=fc, buf=buf):
                    _wait(e, state["w1buf_free"][buf])
                    self.sW1x[buf].inc(e.dma_start(out=w1sl[:, buf, :, :], in_=w1_src(fc)), 16)
                self.P(wd, dr=False)
                dmar = self.sW1x[buf].rec()

                def wc(e, fc=fc, buf=buf, dmar=dmar):
                    _wait(e, [dmar])
                    sW1c[buf].inc(e.dma_start(out=w1_cache(fc), in_=w1sl[:, buf, :, :]), 16)
                self.Sy(wc)
                cache_rec = sW1c[buf].rec()
            else:
                def wd(e, fc=fc, buf=buf):
                    _wait(e, state["w1buf_free"][buf] + w1_cached["all"])
                    sW1[buf].inc(e.dma_start(out=w1sl[:, buf, :, :], in_=w1_cache(fc)), 16)
                self.Sy(wd)
                dmar = sW1[buf].rec()

            hb = fc % 2
            psum = self.ps4[:, hb * 512:(hb + 1) * 512]

            def w1mm(e, fc=fc, buf=buf, psum=psum, dmar=dmar):
                _wait(e, [dmar, lnT_ready, state["h_evac"][fc - 2]])
                for dc in range(DCN):
                    mm = e.matmul(psum, w1sl[:, buf, dc, :], lnT[:, dc, :],
                             start=(dc == 0), stop=(dc == DCN - 1))
                sT.inc(mm)
            self.T(w1mm)
            state["pe_w1"][fc] = sT.rec()
            state["w1buf_free"][buf] = [state["pe_w1"][fc]] + \
                ([cache_rec] if cache_rec else [])

            def hev(e, fc=fc, psum=psum, gr=state["pe_w1"][fc]):
                _wait(e, [gr])
                sA.inc(e.activation(hbuf[:, fc, :], psum, AF.Relu, bias=b1col(fc)))
            self.A(hev)
            state["h_evac"][fc] = sA.rec()

        if first:
            w1_cached["all"] = [s.rec() for s in sW1c]
        h_all = state["h_evac"][FCN - 1]
        for dc in range(DCN):
            buf = dc % nw2buf
            cache_rec2 = None
            if first:
                def wd2(e, dc=dc, buf=buf):
                    _wait(e, state["w2buf_free"][buf])
                    self.sW2x[buf].inc(e.dma_start(out=w2sl[:, buf, :, :], in_=w2_src(dc)), 16)
                self.P(wd2, dr=False)
                dmar2 = self.sW2x[buf].rec()

                def wc2(e, dc=dc, buf=buf, dmar2=dmar2):
                    _wait(e, [dmar2])
                    sW2c[buf].inc(e.dma_start(out=w2_cache(dc), in_=w2sl[:, buf, :, :]), 16)
                self.Sy(wc2)
                cache_rec2 = sW2c[buf].rec()
            else:
                def wd2(e, dc=dc, buf=buf):
                    _wait(e, state["w2buf_free"][buf] + w2_cached["all"])
                    sW2[buf].inc(e.dma_start(out=w2sl[:, buf, :, :], in_=w2_cache(dc)), 16)
                self.Sy(wd2)
                dmar2 = sW2[buf].rec()

            ob = dc % 2
            psum = self.ps4[:, 1024 + ob * 512:1024 + (ob + 1) * 512]

            def w2mm(e, dc=dc, buf=buf, psum=psum, dmar2=dmar2,
                     pfree=state["o_evac"][dc - 2]):
                _wait(e, [dmar2, h_all, pfree])
                for fc in range(FCN):
                    mm = e.matmul(psum, w2sl[:, buf, fc, :], hbuf[:, fc, :],
                             start=(fc == 0), stop=(fc == FCN - 1))
                sT.inc(mm)
            self.T(w2mm)
            state["pe_w2"][dc] = sT.rec()
            state["w2buf_free"][buf] = [state["pe_w2"][dc]] + \
                ([cache_rec2] if cache_rec2 else [])

            b2fn(dc, psum, state["pe_w2"][dc])
        if first:
            w2_cached["all"] = [s.rec() for s in sW2c]

    def _moe_call(self, call):
        nc, d = self.nc, self.d
        sT, sA, sV, sP = self.sT, self.sA, self.sV, self.sP
        u = self.uid
        with ExitStack() as ph:
            sb = lambda n, sh, dt: ph.enter_context(nc.sbuf_tensor(u(n), sh, dt))
            lnT = sb("m_lnT", [128, DCN, TBS], BF16)
            u16 = sb("m_u16", [128, DCN, TBS], F32)
            hbuf = sb("m_h", [128, FCN, TBS], BF16)
            w1sl = sb("m_w1sl", [128, 4, DCN, 128], BF16)
            w2sl = sb("m_w2sl", [128, 2, FCN, 128], BF16)
            w_e = sb("m_we", [128, 2, TBS], F32)
            o_tmp = sb("m_ot", [128, 2, TBS], F32)

            state = dict(
                h_evac=[None] * FCN, pe_w1=[None] * FCN,
                o_evac=[None] * DCN, pe_w2=[None] * DCN,
                w1buf_free=[[] for _ in range(4)],
                w2buf_free=[[] for _ in range(2)],
            )
            o_add = [None] * DCN
            we_ready = [None, None]
            we_free = [None, None]

            for tb in range(self.n_tb):
                chunks = [tb * CPB + i for i in range(CPB)]
                lnF = u16
                lnF_ready = None
                prev_tr1 = getattr(self, "_moe_prev_tr1", None)
                for i, tc in enumerate(chunks):
                    rstd = self.stat[:, tc:tc + 1, 0]
                    nmr = self.stat[:, tc:tc + 1, 1]
                    dep = (self.x_ready if call == 0 else []) + \
                        [getattr(self, "_prev_epg", None)]
                    st_r = self._ln_stats_chunk(self.x_tm[:, tc, :], rstd, nmr,
                                                dep, self.t1_scr[:])

                    def ap1(e, tc=tc, rstd=rstd, nmr=nmr, st_r=st_r, pt=prev_tr1,
                            xu=getattr(self, "_prev_xupd", None)):
                        _wait(e, [st_r, pt, xu])
                        sA.inc(e.activation(self.xa_scr[:], self.x_tm[:, tc, :],
                                            AF.Identity, bias=nmr, scale=rstd))
                    self.A(ap1)
                    ap1r = sA.rec()

                    def tr1(e, ap1r=ap1r, prev=lnF_ready):
                        _wait(e, [ap1r, prev])
                        for g in range(2):
                            for j in range(4):
                                dcx = g * 4 + j
                                mm = e.matmul(self.ps2[:, g * 512 + j * 128:g * 512 + (j + 1) * 128],
                                         self.xa_scr[:, dcx * 128:(dcx + 1) * 128],
                                         self.identf[:], start=(j == 0), stop=(j == 3),
                                         is_transpose=True)
                        sT.inc(mm)
                    self.T(tr1)
                    tr1r = sT.rec()
                    prev_tr1 = tr1r
                    self._moe_prev_tr1 = tr1r

                    def ev1(e, i=i, tr1r=tr1r):
                        _wait(e, [tr1r])
                        for g in range(2):
                            sV.inc(e.tensor_copy(
                                lnF.ap()[:, g * 4:(g + 1) * 4, i * 128:(i + 1) * 128],
                                self.ps2.ap()[:, g * 512:(g + 1) * 512]
                                .rearrange("p (a b) -> p a b", a=4)))
                    self.V(ev1)
                    lnF_ready = sV.rec()

                def cast_ln(e, lnF_ready=lnF_ready):
                    _wait(e, [lnF_ready])
                    sA.inc(e.copy(lnT[:], lnF[:]))
                self.A(cast_ln)
                lnT_ready = sA.rec()

                rmm_r = None
                router_r = None
                if not self.do_router:
                    pass
                def rmm(e, lnF_ready=lnF_ready):
                    _wait(e, [lnF_ready])
                    for i in range(CPB):
                        for dc in range(DCN):
                            mm = e.matmul(self.pss[:, i * 8:(i + 1) * 8],
                                     lnF[:, dc, i * 128:(i + 1) * 128],
                                     self.wr_sb[:, dc, :],
                                     start=(i == 0 and dc == 0),
                                     stop=(i == CPB - 1 and dc == DCN - 1))
                    sT.inc(mm)
                if self.do_router:
                    self.T(rmm)
                    rmm_r = sT.rec()

                def rsoft(e, rmm_r=rmm_r):
                    _wait(e, [rmm_r])
                    for i in range(CPB):
                        e.tensor_tensor(self.rlog[:, i, :], self.pss[:, i * 8:(i + 1) * 8],
                                        self.br_b[:], op=ALU.add)
                    e.drain()
                    m1 = self.stat[:, 0:CPB, 2]
                    e.reduce_max(m1, self.rlog[:], axis=AX.X)
                    e.drain()
                    for i in range(CPB):
                        e.tensor_scalar(self.req[:, 0, :], self.rlog[:, i, :],
                                        self.stat[:, i:i + 1, 2], None, op0=ALU.is_equal)
                        e.drain()
                        e.scalar_tensor_tensor(self.req[:, 1, :], self.req[:, 0, :], -1e30,
                                               self.rlog[:, i, :], op0=ALU.mult, op1=ALU.add)
                        e.drain()
                        e.reduce_max(self.stat[:, i:i + 1, 3], self.req[:, 1, :], axis=AX.X)
                        e.drain()
                    sV.inc(e.tensor_tensor(self.stat[:, 0:CPB, 4],
                                           self.stat[:, 0:CPB, 3],
                                           self.stat[:, 0:CPB, 2], op=ALU.subtract))
                if self.do_router:
                    self.V(rsoft)
                rsoft_r = sV.rec() if self.do_router else None

                def rtanh(e, rsoft_r=rsoft_r):
                    _wait(e, [rsoft_r])
                    sA.inc(e.activation(self.stat[:, 0:CPB, 5], self.stat[:, 0:CPB, 4],
                                        AF.Tanh, scale=0.5))
                if self.do_router:
                    self.A(rtanh)
                rtanh_r = sA.rec() if self.do_router else None

                def rfin(e, rtanh_r=rtanh_r):
                    _wait(e, [rtanh_r])
                    w2v = self.stat[:, 0:CPB, 4]
                    w1v = self.stat[:, 0:CPB, 5]
                    e.tensor_scalar(w2v, self.stat[:, 0:CPB, 5], 0.5, 0.5,
                                    op0=ALU.mult, op1=ALU.add)
                    e.drain()
                    e.tensor_scalar(w1v, w2v, -1.0, 1.0, op0=ALU.mult, op1=ALU.add)
                    e.drain()
                    last = None
                    for i in range(CPB):
                        e.tensor_scalar(self.req[:, 0, :], self.rlog[:, i, :],
                                        self.stat[:, i:i + 1, 2], None, op0=ALU.is_equal)
                        e.tensor_scalar(self.req[:, 1, :], self.rlog[:, i, :],
                                        self.stat[:, i:i + 1, 3], None, op0=ALU.is_equal)
                        e.drain()
                        e.tensor_scalar(self.req[:, 0, :], self.req[:, 0, :],
                                        self.stat[:, i:i + 1, 5], None, op0=ALU.mult)
                        e.drain()
                        last = e.scalar_tensor_tensor(self.rw[:, i, :], self.req[:, 1, :],
                                                      self.stat[:, i:i + 1, 4],
                                                      self.req[:, 0, :],
                                                      op0=ALU.mult, op1=ALU.add)
                        e.drain()
                    sV.inc(last)
                if self.do_router:
                    self.V(rfin)
                    router_r = sV.rec()

                def acc0(e, a=lnT_ready, b=rmm_r, c=router_r):
                    _wait(e, [a, b, c])
                    sP.inc(e.memset(u16[:], 0.0))
                self.P(acc0)
                o_add = [sP.rec()] * DCN

                first = (call == 0 and tb == 0)
                for ei in range(self.n_exp):
                    web = ei % 2

                    def wet(e, ei=ei, rr=router_r, wf=we_free[web],
                            wprev=we_ready[1 - web]):
                        _wait(e, [rr, wf, wprev])
                        for i in range(CPB):
                            mm = e.matmul(self.pss.ap()[0:1, i * 128:(i + 1) * 128],
                                     self.rw[:, i, ei:ei + 1], self.identf[:],
                                     start=(i == 0), stop=(i == CPB - 1),
                                     is_transpose=True)
                        sT.inc(mm)
                    self.T(wet)
                    wet_r = sT.rec()

                    def wev(e, wet_r=wet_r):
                        _wait(e, [wet_r])
                        sV.inc(e.tensor_copy(self.w_row[:], self.pss.ap()[0:1, 0:TBS]))
                    self.V(wev)
                    wev_r = sV.rec()

                    def webm(e, wev_r=wev_r):
                        _wait(e, [wev_r])
                        sT.inc(e.matmul(self.pss[:, 0:TBS], self.ones_f[:], self.w_row[:],
                                        start=True, stop=True))
                    self.T(webm)
                    webm_r = sT.rec()

                    def webe(e, web=web, webm_r=webm_r):
                        _wait(e, [webm_r])
                        sV.inc(e.tensor_copy(w_e[:, web, :], self.pss[:, 0:TBS]))
                    self.V(webe)
                    we_ready[web] = sV.rec()

                    def b2fn(dc, psum, group_rec, ei=ei, web=web):
                        def oev(e, dc=dc, psum=psum, gr=group_rec,
                                wer=we_ready[web], pf=o_add[(dc - 2) % DCN]):
                            _wait(e, [gr, wer, pf])
                            sV.inc(e.scalar_tensor_tensor(
                                o_tmp[:, dc % 2, :], psum, self.b2c[:, ei, dc:dc + 1],
                                w_e[:, web, :], op0=ALU.add, op1=ALU.mult))
                        self.V(oev)
                        state["o_evac"][dc] = sV.rec()
                        we_free[web] = sV.rec()

                        def oad(e, dc=dc, er=state["o_evac"][dc]):
                            _wait(e, [er])
                            sP.inc(e.tensor_tensor(u16[:, dc, :], u16[:, dc, :],
                                                   o_tmp[:, dc % 2, :], op=ALU.add))
                        self.P(oad)
                        o_add[dc] = sP.rec()

                    self._mlp_block(
                        lnT, hbuf, w1sl, w2sl, 4, 2,
                        lambda fc, ei=ei: d["w1s"][ei, fc, :, :, :],
                        lambda fc, ei=ei: d["w1b"][ei, fc, :, :, :],
                        self._kcache("w1", ei),
                        lambda fc, ei=ei: self.b1c[:, ei, fc:fc + 1],
                        lambda dc, ei=ei: d["w2s"][ei, dc, :, :, :],
                        lambda dc, ei=ei: d["w2b"][ei, dc, :, :, :],
                        self._kcache("w2", ei),
                        b2fn, first, lnT_ready, state)

                dbg_acc_deps = [o_add[dc] for dc in range(DCN)] + [router_r, we_ready[0]]

                if not self.do_ep:
                    continue
                acc_done = [o_add[dc] for dc in range(DCN)]
                ep_ev = None
                x_upd_prev = None
                epg_prev = None
                for i, tc in enumerate(chunks):
                    def ept(e, i=i, deps=acc_done, prev=ep_ev):
                        _wait(e, deps + [prev])
                        for g in range(2):
                            for j in range(4):
                                dcx = g * 4 + j
                                mm = e.matmul(self.ps2[:, g * 512 + j * 128:g * 512 + (j + 1) * 128],
                                         u16[:, dcx, i * 128:(i + 1) * 128],
                                         self.identf[:], start=(j == 0), stop=(j == 3),
                                         is_transpose=True)
                        sT.inc(mm)
                    self.T(ept)
                    eptr = sT.rec()

                    def epv(e, eptr=eptr, xup=x_upd_prev):
                        _wait(e, [eptr, xup])
                        sV.inc(e.tensor_copy(self.t1_scr[:], self.ps2[:]))
                    self.V(epv)
                    ep_ev = sV.rec()

                    def epa(e, tc=tc, pg=epg_prev, xu=x_upd_prev):
                        _wait(e, [pg, xu])
                        rstd = self.stat[:, tc:tc + 1, 0]
                        nmr = self.stat[:, tc:tc + 1, 1]
                        sA.inc(e.activation(self.xa_scr[:], self.x_tm[:, tc, :],
                                            AF.Identity, bias=nmr, scale=rstd))
                    self.A(epa)
                    epar = sA.rec()

                    def epz(e, epar=epar, evr=ep_ev):
                        _wait(e, [epar, evr])
                        sP.inc(e.tensor_tensor(self.z_scr[:], self.xa_scr[:],
                                               self.t1_scr[:], op=ALU.add))
                    self.P(epz)
                    z_r = sP.rec()

                    rstdz = self.stat[:, 8 + i:9 + i, 4]
                    nmrz = self.stat[:, 8 + i:9 + i, 5]
                    st_r = self._ln_stats_chunk(self.z_scr[:], rstdz, nmrz, [z_r],
                                                self.t1_scr[:])

                    def epn(e, st_r=st_r, rstdz=rstdz, nmrz=nmrz):
                        _wait(e, [st_r])
                        sA.inc(e.activation(self.t1_scr[:], self.z_scr[:], AF.Identity,
                                            bias=nmrz, scale=rstdz))
                    self.A(epn)
                    epnr = sA.rec()

                    def epg(e, epnr=epnr):
                        _wait(e, [epnr])
                        sV.inc(e.tensor_tensor(self.xa_scr[:], self.t1_scr[:],
                                               self.gfh[:], op=ALU.mult))
                    self.V(epg)
                    epgr = sV.rec()
                    epg_prev = epgr
                    self._prev_epg = epgr

                    def epx(e, tc=tc, epgr=epgr):
                        _wait(e, [epgr])
                        e.tensor_tensor(self.x_tm[:, tc, :], self.x_tm[:, tc, :],
                                        self.xa_scr[:], op=ALU.add)
                        e.drain()
                        sP.inc(e.tensor_tensor(self.x_tm[:, tc, :], self.x_tm[:, tc, :],
                                               self.bfh[:], op=ALU.add))
                    self.P(epx)
                    x_upd_prev = sP.rec()
                    self._prev_xupd = x_upd_prev

                if self.debug and call == 0 and tb == 0 and self.n_tb == 1:
                    def dbg_dump(e, deps=dbg_acc_deps + [x_upd_prev, sA.rec(),
                                                         sV.rec(), sT.rec()]):
                        _wait(e, deps)
                        self.dS.inc(e.dma_start(out=d["dbg_rw"][:, :, :], in_=self.rw[:]), 16)
                        self.dS.inc(e.dma_start(out=d["dbg_acc"][:, :, :], in_=u16[:]), 16)
                        self.dS.inc(e.dma_start(out=d["dbg_we"][:, :], in_=w_e[:, 0, :]), 16)
                        self.dS.inc(e.dma_start(out=d["dbg_stat"][:, :, :], in_=self.stat[:]), 16)
                        self.dS.inc(e.dma_start(out=d["dbg_rlog"][:, :, :], in_=self.rlog[:]), 16)
                    self.Sy(dbg_dump)

    def _kcache(self, kind, ei):
        base = self.w1_cached if kind == "w1" else self.w2_cached
        return base.setdefault(ei, {})

    def _attn_block(self):
        nc, d = self.nc, self.d
        sT, sA, sV, sP = self.sT, self.sA, self.sV, self.sP
        dS, sQ, sWo = self.dS, self.sQ, self.sWo
        u = self.uid
        with ExitStack() as ph:
            sbA = lambda n, sh, dt: ph.enter_context(nc.sbuf_tensor(u(n), sh, dt))
            hcT = sbA("a_hcT", [128, DCN, S], BF16)
            h1v = hcT.ap().rearrange("p dc (tc m) -> p dc tc m", m=128)
            h1_written = [None] * TCN
            hc_last = None

            with ExitStack() as ph2:
                sb2 = lambda n, sh, dt: ph2.enter_context(nc.sbuf_tensor(u(n), sh, dt))
                ln1T = sb2("a_ln1T", [128, DCN, S], BF16)
                qkvT = sb2("a_qkvT", [128, 3, S], BF16)
                qkvw = sb2("a_qkvw", [128, 2, 3, DCN, 128], BF16)
                abuf = sb2("a_abuf", [128, 2, S], BF16)
                atb = sb2("a_atb", [128, 16, 128], BF16)
                vst = sb2("a_vst", [128, 16, HD], BF16)

                ln_ev = None
                prev_tr1 = getattr(self, "_moe_prev_tr1", None)
                for tc in range(TCN):
                    st_r = self._ln_stats_chunk(self.x_tm[:, tc, :],
                                                self.rstd1[:, tc:tc + 1],
                                                self.nmr1[:, tc:tc + 1], [],
                                                self.t1_scr[:])

                    def ap1(e, tc=tc, st_r=st_r, pt=prev_tr1):
                        _wait(e, [st_r, pt])
                        sA.inc(e.activation(self.xa_scr[:], self.x_tm[:, tc, :],
                                            AF.Identity,
                                            bias=self.nmr1[:, tc:tc + 1],
                                            scale=self.rstd1[:, tc:tc + 1]))
                    self.A(ap1)
                    ap1r = sA.rec()

                    def tr1(e, ap1r=ap1r, prev=ln_ev):
                        _wait(e, [ap1r, prev])
                        for g in range(2):
                            for j in range(4):
                                dcx = g * 4 + j
                                mm = e.matmul(self.ps2[:, g * 512 + j * 128:g * 512 + (j + 1) * 128],
                                         self.xa_scr[:, dcx * 128:(dcx + 1) * 128],
                                         self.identf[:], start=(j == 0), stop=(j == 3),
                                         is_transpose=True)
                        sT.inc(mm)
                    self.T(tr1)
                    tr1r = sT.rec()
                    prev_tr1 = tr1r
                    self._moe_prev_tr1 = tr1r

                    def ev1(e, tc=tc, tr1r=tr1r):
                        _wait(e, [tr1r])
                        for g in range(2):
                            sV.inc(e.tensor_copy(
                                ln1T.ap()[:, g * 4:(g + 1) * 4, tc * 128:(tc + 1) * 128],
                                self.ps2.ap()[:, g * 512:(g + 1) * 512]
                                .rearrange("p (a b) -> p a b", a=4)))
                    self.V(ev1)
                    ln_ev = sV.rec()
                ln1_ready = ln_ev

                qkv_dma = [None, None]
                qkv_pe_done = [None, None]
                at_ev_last = None
                vev_last = None
                bank_gate = [None, None]
                abuf_gate = [None, None]
                for hp in range(8):
                    qb = hp % 2
                    jcs = [hp, 8 + hp, 16 + hp]

                    def qwd(e, jcs=jcs, qb=qb, free=qkv_pe_done[qb]):
                        _wait(e, [free])
                        for si, jc in enumerate(jcs):
                            sQ[qb].inc(e.dma_start(out=qkvw[:, qb, si, :, :],
                                                   in_=d["wqkv"][jc, :, :, :]), 16)
                    self.P(qwd, dr=False)
                    qkv_dma[qb] = sQ[qb].rec()

                    qev_last = None
                    for si in range(3):
                        for tq in range(4):
                            hb = (si * 4 + tq) % 2
                            psum = self.ps4[:, hb * 512:(hb + 1) * 512]

                            def qmm(e, si=si, tq=tq, qb=qb, psum=psum, dr=qkv_dma[qb],
                                    ev=qev_last):
                                _wait(e, [dr, ln1_ready, ev])
                                for dc in range(DCN):
                                    mm = e.matmul(psum, qkvw[:, qb, si, dc, :],
                                             ln1T[:, dc, tq * 512:(tq + 1) * 512],
                                             start=(dc == 0), stop=(dc == DCN - 1))
                                sT.inc(mm)
                            self.T(qmm)
                            qmr = sT.rec()

                            def qev(e, si=si, tq=tq, psum=psum, qmr=qmr, hp=hp):
                                _wait(e, [qmr])
                                col = [hp, 8 + hp, 16 + hp][si]
                                sA.inc(e.activation(
                                    qkvT[:, si, tq * 512:(tq + 1) * 512], psum,
                                    AF.Identity, bias=self.bqkvs[:, col:col + 1],
                                    scale=0.125 if si == 0 else 1.0))
                            self.A(qev)
                            qev_last = sA.rec()
                    qkv_ready = qev_last

                    for hh in range(2):
                        h = 2 * hp + hh
                        s = STRIDES[h]
                        Ls = S // s
                        KC = Ls // 128
                        off = 64 * hh

                        def vtr(e, s=s, off=off, KC=KC, qr=qkv_ready):
                            _wait(e, [qr])
                            for kc in range(KC):
                                src = qkvT.ap()[off:off + 64, 2,
                                                kc * 128 * s:(kc * 128 + 128) * s:s]
                                mm = e.matmul(self.psb[:, kc * 64:(kc + 1) * 64], src,
                                         self.identb.ap()[off:off + 64, off:off + 64],
                                         start=(kc == 0), stop=(kc == KC - 1),
                                         is_transpose=True)
                            sT.inc(mm)
                        self.T(vtr)
                        vtr_r = sT.rec()

                        def vev(e, KC=KC, vtr_r=vtr_r):
                            _wait(e, [vtr_r])
                            sV.inc(e.tensor_copy(vst.ap()[:, 0:KC, :],
                                                 self.psb.ap()[:, 0:KC * 64]
                                                 .rearrange("p (a b) -> p a b", b=64)))
                        self.V(vev)
                        vev_r = sV.rec()
                        vev_last = vev_r

                        at_last = None
                        pss_gate = None
                        for qc in range(TCN):
                            qg = qc % 4
                            par = (qc % 2) if Ls <= 1024 else 0
                            boff = par * 1024
                            nmc = self.sm[:, 0:1, qg]
                            zc = self.sm[:, 1:2, qg]
                            bc2 = self.sm[:, 2:3, qg]

                            def smm(e, off=off, qc=qc, s=s, Ls=Ls, qr=qkv_ready,
                                    gate=bank_gate[par], boff=boff):
                                _wait(e, [qr, gate])
                                nkb = (Ls + 511) // 512
                                for kb in range(nkb):
                                    n = min(512, Ls - kb * 512)
                                    src = qkvT.ap()[off:off + 64, 1,
                                                    kb * 512 * s:(kb * 512 + n) * s:s]
                                    mm = e.matmul(self.ps4[:, boff + kb * 512:boff + kb * 512 + n],
                                             qkvT.ap()[off:off + 64, 0,
                                                       qc * 128:(qc + 1) * 128],
                                             src, start=True, stop=True)
                                sT.inc(mm)
                            self.T(smm)
                            smm_r = sT.rec()

                            def smax(e, Ls=Ls, nmc=nmc, smm_r=smm_r, boff=boff):
                                _wait(e, [smm_r])
                                sV.inc(e.reduce_max(nmc, self.ps4[:, boff:boff + Ls],
                                                    axis=AX.X, negate=True))
                            self.V(smax)
                            smax_r = sV.rec()

                            def sexp1(e, Ls=Ls, nmc=nmc, zc=zc, smax_r=smax_r,
                                      gate=abuf_gate[par], boff=boff, par=par):
                                _wait(e, [smax_r, gate])
                                e.activation(abuf[:, par, 0:Ls],
                                             self.ps4[:, boff:boff + Ls], AF.Exp,
                                             bias=nmc, accum_out=zc)
                                e.drain()
                                sA.inc(e.activation(self.sm[:, 3:4, qg], zc, AF.Ln))
                            self.A(sexp1)
                            sexp1_r = sA.rec()

                            def sbias(e, nmc=nmc, bc2=bc2, r=sexp1_r):
                                _wait(e, [r])
                                sV.inc(e.tensor_tensor(bc2, nmc, self.sm[:, 3:4, qg],
                                                       op=ALU.subtract))
                            self.V(sbias)
                            sbias_r = sV.rec()

                            def sexp2(e, Ls=Ls, bc2=bc2, r=sbias_r, boff=boff, par=par):
                                _wait(e, [r])
                                sA.inc(e.activation(abuf[:, par, 0:Ls],
                                                    self.ps4[:, boff:boff + Ls],
                                                    AF.Exp, bias=bc2))
                            self.A(sexp2)
                            sexp2_r = sA.rec()
                            bank_gate[par] = sexp2_r

                            at_ready = None
                            nbat = (KC + 7) // 8
                            at_evs = []
                            for bi in range(nbat):
                                k0, k1 = bi * 8, min(KC, bi * 8 + 8)

                                def atrb(e, k0=k0, k1=k1, r=sexp2_r, vr=vev_r,
                                         aev=at_evs[-1] if at_evs else None, par=par):
                                    _wait(e, [r, vr, aev])
                                    for kc in range(k0, k1):
                                        mm = e.matmul(self.psb[:, (kc - k0) * 128:(kc - k0 + 1) * 128],
                                                 abuf[:, par, kc * 128:(kc + 1) * 128],
                                                 self.identb[:],
                                                 start=(kc == k0), stop=(kc == k1 - 1),
                                                 is_transpose=True)
                                    sT.inc(mm)
                                self.T(atrb)
                                atr_r = sT.rec()

                                def atev(e, k0=k0, k1=k1, atr_r=atr_r):
                                    _wait(e, [atr_r])
                                    sV.inc(e.tensor_copy(
                                        atb.ap()[:, k0:k1, :],
                                        self.psb.ap()[:, 0:(k1 - k0) * 128]
                                        .rearrange("p (a b) -> p a b", b=128)))
                                self.V(atev)
                                at_evs.append(sV.rec())
                                at_ev_last = sV.rec()
                            at_ready = at_evs[-1]
                            at_last = sT.rec()
                            abuf_gate[par] = at_last

                            def avm(e, qg=qg, KC=KC, atr=at_ready, vr=vev_r,
                                    gate=pss_gate):
                                _wait(e, [atr, vr] + ([gate] if qg == 0 else []))
                                for kc in range(KC):
                                    mm = e.matmul(self.pss.ap()[0:64, qg * 128:(qg + 1) * 128],
                                             vst[:, kc, :], atb[:, kc, :],
                                             start=(qg == 0 and kc == 0),
                                             stop=(qg == 3 and kc == KC - 1))
                                sT.inc(mm)
                            self.T(avm)
                            av_r = sT.rec()

                            if qg == 3:
                                def avev(e, h=h, qc=qc, av_r=av_r):
                                    _wait(e, [av_r])
                                    dst = hcT.ap()[64 * (h % 2):64 * (h % 2) + 64,
                                                   h // 2, (qc - 3) * 128:(qc + 1) * 128]
                                    sA.inc(e.activation(dst, self.pss.ap()[0:64, 0:512],
                                                        AF.Identity))
                                self.A(avev)
                                hc_last = sA.rec()
                                pss_gate = hc_last

                    qkv_pe_done[qb] = sT.rec()

            if self.debug:
                def dump_hc(e, hl=hc_last, qd=qkv_pe_done[1], aev=at_ev_last,
                            vv=vev_last):
                    _wait(e, [hl, qd, aev, vv])
                    self.dS.inc(e.dma_start(out=d["dbg_hc"][:, :, :], in_=hcT[:]), 16)
                    self.dS.inc(e.dma_start(out=d["dbg_a"][:, :], in_=abuf.ap()[:, 1, :]), 16)
                    self.dS.inc(e.dma_start(out=d["dbg_qkv"][:, :, :], in_=qkvT[:]), 16)
                    self.dS.inc(e.dma_start(out=d["dbg_at"][:, :, :], in_=atb[:]), 16)
                    self.dS.inc(e.dma_start(out=d["dbg_vs"][:, :, :], in_=vst[:]), 16)
                self.Sy(dump_hc)
            self.barrier()
            with ExitStack() as ph3:
                sb3 = lambda n, sh, dt: ph3.enter_context(nc.sbuf_tensor(u(n), sh, dt))
                wout_sb = sb3("a_wout", [128, DCN, D], BF16)

                def wod(e):
                    sWo.inc(e.dma_start(out=wout_sb[:], in_=d["wout"]
                                        .rearrange("(dc p) n -> p dc n", p=128)), 16)
                self.P(wod, dr=False)
                wout_r = sWo.rec()

                z_prev = None
                wog_prev = None
                h1w_prev = None
                for tc in range(TCN):
                    def womm(e, tc=tc, wr=wout_r, hca=hc_last, zp=z_prev):
                        _wait(e, [wr, hca, zp])
                        for half in range(2):
                            for dc in range(DCN):
                                e.matmul(self.ps2[:, half * 512:(half + 1) * 512],
                                         hcT.ap()[:, dc, tc * 128:(tc + 1) * 128],
                                         wout_sb[:, dc, half * 512:(half + 1) * 512],
                                         start=(dc == 0), stop=False)
                            mm = e.matmul(self.ps2[:, half * 512:(half + 1) * 512],
                                     self.ones_b[:],
                                     self.bout_bf.ap()[:, half * 512:(half + 1) * 512],
                                     start=False, stop=True)
                        sT.inc(mm)
                    self.T(womm)
                    womm_r = sT.rec()

                    def woxa(e, tc=tc, wp=wog_prev):
                        _wait(e, [wp])
                        sA.inc(e.activation(self.xa_scr[:], self.x_tm[:, tc, :],
                                            AF.Identity,
                                            bias=self.nmr1[:, tc:tc + 1],
                                            scale=self.rstd1[:, tc:tc + 1]))
                    self.A(woxa)
                    woxa_r = sA.rec()

                    def woz(e, womm_r=womm_r, woxa_r=woxa_r):
                        _wait(e, [womm_r, woxa_r])
                        sV.inc(e.tensor_tensor(self.z_scr[:], self.xa_scr[:],
                                               self.ps2[:], op=ALU.add))
                    self.V(woz)
                    z_prev = sV.rec()

                    rstdh = self.sm[:, 4:5, 0]
                    nmrh = self.sm[:, 5:6, 0]
                    st_r = self._ln_stats_chunk(self.z_scr[:], rstdh, nmrh,
                                                [z_prev, h1w_prev], self.t1_scr[:])

                    def woh(e, st_r=st_r, rstdh=rstdh, nmrh=nmrh):
                        _wait(e, [st_r])
                        sA.inc(e.activation(self.xa_scr[:], self.z_scr[:], AF.Identity,
                                            bias=nmrh, scale=rstdh))
                    self.A(woh)
                    woh_r = sA.rec()

                    def wog(e, woh_r=woh_r):
                        _wait(e, [woh_r])
                        sV.inc(e.tensor_tensor(self.t1_scr[:], self.xa_scr[:],
                                               self.gab[:], op=ALU.mult))
                    self.V(wog)
                    wog_r = sV.rec()
                    wog_prev = wog_r

                    def woh1(e, tc=tc, wog_r=wog_r, hp_=h1w_prev):
                        _wait(e, [wog_r, hp_])
                        sP.inc(e.tensor_tensor(
                            h1v[:, :, tc, :],
                            self.t1_scr.ap().rearrange("p (a b) -> p a b", b=128),
                            self.bab.ap().rearrange("p (a b) -> p a b", b=128),
                            op=ALU.add))
                    self.P(woh1)
                    h1_written[tc] = sP.rec()
                    h1w_prev = h1_written[tc]
            self.barrier()

            with ExitStack() as ph4:
                sb4 = lambda n, sh, dt: ph4.enter_context(nc.sbuf_tensor(u(n), sh, dt))
                lnT = sb4("f_lnT", [128, DCN, TBS], BF16)
                u16 = sb4("f_u16", [128, DCN, TBS], F32)
                hbuf = sb4("f_h", [128, FCN, TBS], BF16)
                w1sl = sb4("f_w1sl", [128, 4, DCN, 128], BF16)
                w2sl = sb4("f_w2sl", [128, 1, FCN, 128], BF16)

                state = dict(
                    h_evac=[None] * FCN, pe_w1=[None] * FCN,
                    o_evac=[None] * DCN, pe_w2=[None] * DCN,
                    w1buf_free=[[] for _ in range(4)],
                    w2buf_free=[[]],
                )
                o_guard = [None] * DCN

                for tb in range(NTB):
                    chunks = [tb * CPB + i for i in range(CPB)]
                    first = (tb == 0)
                    lnq = None
                    for i, tc in enumerate(chunks):
                        def htr(e, tc=tc, h1r=h1_written[tc], prev=lnq):
                            _wait(e, [h1r, prev])
                            for dcx in range(DCN):
                                mm = e.matmul(self.psb[:, dcx * 128:(dcx + 1) * 128],
                                         h1v[:, dcx, tc, :], self.identb[:],
                                         start=(dcx == 0), stop=(dcx == DCN - 1),
                                         is_transpose=True)
                            sT.inc(mm)
                        self.T(htr)
                        htr_r = sT.rec()

                        def hev2(e, i=i, htr_r=htr_r):
                            _wait(e, [htr_r])
                            sV.inc(e.tensor_copy(lnT.ap()[:, :, i * 128:(i + 1) * 128],
                                                 self.psb.ap()[:, 0:1024]
                                                 .rearrange("p (a b) -> p a b", a=8)))
                        self.V(hev2)
                        lnq = sV.rec()
                    lnT_ready = lnq

                    def b2fn(dc, psum, group_rec):
                        def oev(e, dc=dc, psum=psum, gr=group_rec, og=o_guard[dc]):
                            _wait(e, [gr, og])
                            sV.inc(e.tensor_scalar(u16[:, dc, :], psum,
                                                   self.b2ac[:, dc:dc + 1], None,
                                                   op0=ALU.add))
                        self.V(oev)
                        state["o_evac"][dc] = sV.rec()

                    self._mlp_block(
                        lnT, hbuf, w1sl, w2sl, 4, 1,
                        lambda fc: d["w1a"][fc, :, :, :],
                        lambda fc: d["w1ab"][fc, :, :, :],
                        self.w1a_cached,
                        lambda fc: self.b1ac[:, fc:fc + 1],
                        lambda dc: d["w2a"][dc, :, :, :],
                        lambda dc: d["w2ab"][dc, :, :, :],
                        self.w2a_cached,
                        b2fn, first, lnT_ready, state)

                    ffn_done = [state["o_evac"][dc] for dc in range(DCN)]
                    ep_ev = None
                    x_upd_prev = None
                    ffn_epn_prev = getattr(self, "_ffn_epn_prev", [None])
                    self._ffn_epn_prev = ffn_epn_prev
                    for i, tc in enumerate(chunks):
                        def ept(e, i=i, deps=ffn_done, prev=ep_ev):
                            _wait(e, deps + [prev])
                            for g in range(2):
                                for j in range(4):
                                    dcx = g * 4 + j
                                    mm = e.matmul(self.ps2[:, g * 512 + j * 128:g * 512 + (j + 1) * 128],
                                             u16[:, dcx, i * 128:(i + 1) * 128],
                                             self.identf[:], start=(j == 0), stop=(j == 3),
                                             is_transpose=True)
                            sT.inc(mm)
                        self.T(ept)
                        eptr = sT.rec()
                        o_guard = [eptr] * DCN

                        def epv(e, eptr=eptr, xup=x_upd_prev):
                            _wait(e, [eptr, xup])
                            sV.inc(e.tensor_copy(self.t1_scr[:], self.ps2[:]))
                        self.V(epv)
                        ep_ev = sV.rec()

                        def epz(e, tc=tc, evr=ep_ev, h1r=h1_written[tc],
                                pn=ffn_epn_prev[0]):
                            _wait(e, [evr, h1r, pn])
                            sP.inc(e.tensor_tensor(
                                self.z_scr.ap().rearrange("p (a b) -> p a b", b=128),
                                self.t1_scr.ap().rearrange("p (a b) -> p a b", b=128),
                                h1v[:, :, tc, :], op=ALU.add))
                        self.P(epz)
                        z_r = sP.rec()

                        rstdz = self.sm[:, 6:7, 0]
                        nmrz = self.sm[:, 7:8, 0]
                        st_r = self._ln_stats_chunk(self.z_scr[:], rstdz, nmrz, [z_r],
                                                    self.t1_scr[:])

                        def epn(e, st_r=st_r, rstdz=rstdz, nmrz=nmrz):
                            _wait(e, [st_r])
                            sA.inc(e.activation(self.t1_scr[:], self.z_scr[:],
                                                AF.Identity, bias=nmrz, scale=rstdz))
                        self.A(epn)
                        epnr = sA.rec()
                        ffn_epn_prev[0] = epnr

                        def epg(e, epnr=epnr):
                            _wait(e, [epnr])
                            sV.inc(e.tensor_tensor(self.xa_scr[:], self.t1_scr[:],
                                                   self.gab[:], op=ALU.mult))
                        self.V(epg)
                        epgr = sV.rec()

                        def epx(e, tc=tc, epgr=epgr):
                            _wait(e, [epgr])
                            e.tensor_tensor(self.x_tm[:, tc, :], self.x_tm[:, tc, :],
                                            self.xa_scr[:], op=ALU.add)
                            e.drain()
                            sP.inc(e.tensor_tensor(self.x_tm[:, tc, :],
                                                   self.x_tm[:, tc, :],
                                                   self.bab[:], op=ALU.add))
                        self.P(epx)
                        x_upd_prev = sP.rec()


def _prep_weights(ins):
    g = lambda k: np.ascontiguousarray(np.asarray(ins[k]))
    W1 = g("L0_ffn_W1")
    W2 = g("L0_ffn_W2")
    W1a = g("L0_attn_W1")
    W2a = g("L0_attn_W2")
    Wqkv = g("L0_attn_Wqkv")
    com = {}
    com["w1s"] = np.ascontiguousarray(
        W1.reshape(NE, FCN, 128, DCN, 128).transpose(0, 1, 4, 3, 2))
    com["w2s"] = np.ascontiguousarray(
        W2.reshape(NE, DCN, 128, FCN, 128).transpose(0, 1, 4, 3, 2))
    com["w1a"] = np.ascontiguousarray(
        W1a.reshape(DCN, 128, FCN, 128).transpose(2, 1, 0, 3))
    com["w2a"] = np.ascontiguousarray(
        W2a.reshape(FCN, 128, DCN, 128).transpose(2, 1, 0, 3))
    com["wqkv"] = np.ascontiguousarray(
        Wqkv.reshape(DCN, 128, 24, 128).transpose(2, 1, 0, 3))
    com["wout"] = g("L0_attn_Wout")
    com["wr"] = np.ascontiguousarray(
        g("L0_ffn_Wr").T.reshape(DCN, 128, NE).transpose(1, 0, 2))
    com["b1c"] = np.ascontiguousarray(
        g("L0_ffn_b1").reshape(NE, FCN, 128).transpose(2, 0, 1))
    com["b2c"] = np.ascontiguousarray(
        g("L0_ffn_b2").reshape(NE, DCN, 128).transpose(2, 0, 1))
    com["b1ac"] = np.ascontiguousarray(g("L0_attn_b1").reshape(FCN, 128).T)
    com["b2ac"] = np.ascontiguousarray(g("L0_attn_b2").reshape(DCN, 128).T)
    com["bqkvc"] = np.ascontiguousarray(g("L0_attn_bqkv").reshape(24, 128).T)
    com["br_b"] = np.ascontiguousarray(np.broadcast_to(g("L0_ffn_br"), (128, NE)))
    com["g_f"] = np.ascontiguousarray(np.broadcast_to(g("L0_ffn_gamma"), (128, D)))
    com["be_f"] = np.ascontiguousarray(np.broadcast_to(g("L0_ffn_beta"), (128, D)))
    com["g_a"] = np.ascontiguousarray(np.broadcast_to(g("L0_attn_gamma"), (128, D)))
    com["be_a"] = np.ascontiguousarray(np.broadcast_to(g("L0_attn_beta"), (128, D)))
    com["bout_r"] = np.ascontiguousarray(g("L0_attn_bout").reshape(1, D))
    com["identf"] = np.eye(128, dtype=np.float32)
    com["ones1"] = np.ones((1, 128), dtype=np.float32)
    return com


def kernel(**inputs):
    com = _prep_weights(inputs)
    xs = np.ascontiguousarray(np.asarray(inputs["L0_attn_x"]))
    nc = Builder(debug=False).build()
    in_maps = []
    for b in range(8):
        m = dict(com)
        m["x"] = np.ascontiguousarray(xs[b])
        in_maps.append(m)
    res = run_bass_kernel_spmd(nc, in_maps, core_ids=list(range(8)))
    return np.stack([np.asarray(r["y"]) for r in res.results], axis=0)


# version 04
# speedup vs baseline: 1.8726x; 1.1230x over previous
import numpy as np
import ml_dtypes
from contextlib import ExitStack

import concourse.bass as bass
import concourse.mybir as mybir
from concourse.bass_utils import run_bass_kernel_spmd

F32 = mybir.dt.float32
BF16 = mybir.dt.bfloat16
AF = mybir.ActivationFunctionType
ALU = mybir.AluOpType
AX = mybir.AxisListType

S = 2048
TCN = 16
D = 1024
DCN = 8
FF = 4096
FCN = 32
NE = 8
TBS = 512
NTB = S // TBS
CPB = TBS // 128
NH = 16
HD = 64
STRIDES = [1, 1, 2, 2, 2, 2, 4, 4, 4, 4, 8, 8, 8, 8, 16, 16]
EPS = 1e-5


class Sem:
    def __init__(self, ctx, nc, name):
        self.h = ctx.enter_context(nc.semaphore(name))
        self.n = 0

    def inc(self, instr, k=1):
        instr.then_inc(self.h, k)
        self.n += k
        return self.n

    def rec(self):
        return (self, self.n)


def _wait(e, recs):
    seen = {}
    for r in recs:
        if r is None:
            continue
        s, v = r
        if v > 0 and seen.get(id(s), -1) < v:
            seen[id(s)] = v
            e.wait_ge(s.h, v)


class Builder:
    def __init__(self, debug=False, stop_after="full", n_tb=NTB, n_exp=NE,
                 do_router=True, do_ep=True):
        self.debug = debug
        self.stop_after = stop_after
        self.n_tb = n_tb
        self.n_exp = n_exp
        self.do_router = do_router
        self.do_ep = do_ep
        self.nc = bass.Bass("TRN2", target_bir_lowering=False)
        self._uid = 0

    def uid(self, p):
        self._uid += 1
        return f"{p}{self._uid}"

    def T(self, fn): self.blk.tensor(fn)

    def _fence(self, sem, fn):
        n = sem.n
        def g(e):
            if n > 0:
                e.wait_ge(sem.h, n)
            fn(e)
        return g

    def A(self, fn, dr=True):
        self.blk.scalar(self._fence(self.sA, fn) if dr else fn)

    def V(self, fn, dr=True):
        self.blk.vector(self._fence(self.sV, fn) if dr else fn)

    def P(self, fn, dr=True):
        self.blk.gpsimd(self._fence(self.sP, fn) if dr else fn)

    def Sy(self, fn): self.blk.sync(fn)

    def barrier(self):
        sems = [self.sT, self.sA, self.sV, self.sP, self.dS, self.sIc,
                self.sWo] + self.sW1 + self.sW1c + self.sW2 + self.sW2c + \
            self.sQ + self.sW1x + self.sW2x
        recs = [s.rec() for s in sems]
        for em in (self.T, self.A, self.V, self.P, self.Sy):
            em(lambda e: _wait(e, recs))

    def build(self):
        nc = self.nc
        d = {}
        d["x"] = nc.dram_tensor("x", [S, D], F32, kind="ExternalInput").ap()
        d["w1s"] = nc.dram_tensor("w1s", [NE, FCN, 128, DCN, 128], F32, kind="ExternalInput").ap()
        d["w2s"] = nc.dram_tensor("w2s", [NE, DCN, 128, FCN, 128], F32, kind="ExternalInput").ap()
        d["w1a"] = nc.dram_tensor("w1a", [FCN, 128, DCN, 128], F32, kind="ExternalInput").ap()
        d["w2a"] = nc.dram_tensor("w2a", [DCN, 128, FCN, 128], F32, kind="ExternalInput").ap()
        d["wqkv"] = nc.dram_tensor("wqkv", [24, 128, DCN, 128], F32, kind="ExternalInput").ap()
        d["wout"] = nc.dram_tensor("wout", [D, D], F32, kind="ExternalInput").ap()
        d["wr"] = nc.dram_tensor("wr", [128, DCN, NE], F32, kind="ExternalInput").ap()
        d["b1c"] = nc.dram_tensor("b1c", [128, NE, FCN], F32, kind="ExternalInput").ap()
        d["b2c"] = nc.dram_tensor("b2c", [128, NE, DCN], F32, kind="ExternalInput").ap()
        d["b1ac"] = nc.dram_tensor("b1ac", [128, FCN], F32, kind="ExternalInput").ap()
        d["b2ac"] = nc.dram_tensor("b2ac", [128, DCN], F32, kind="ExternalInput").ap()
        d["bqkvc"] = nc.dram_tensor("bqkvc", [128, 24], F32, kind="ExternalInput").ap()
        d["br_b"] = nc.dram_tensor("br_b", [128, NE], F32, kind="ExternalInput").ap()
        d["g_f"] = nc.dram_tensor("g_f", [128, D], F32, kind="ExternalInput").ap()
        d["be_f"] = nc.dram_tensor("be_f", [128, D], F32, kind="ExternalInput").ap()
        d["g_a"] = nc.dram_tensor("g_a", [128, D], F32, kind="ExternalInput").ap()
        d["be_a"] = nc.dram_tensor("be_a", [128, D], F32, kind="ExternalInput").ap()
        d["bout_r"] = nc.dram_tensor("bout_r", [1, D], F32, kind="ExternalInput").ap()
        d["identf"] = nc.dram_tensor("identf", [128, 128], F32, kind="ExternalInput").ap()
        d["ones1"] = nc.dram_tensor("ones1", [1, 128], F32, kind="ExternalInput").ap()
        d["y"] = nc.dram_tensor("y", [S, D], F32, kind="ExternalOutput").ap()
        d["w1b"] = nc.dram_tensor("w1b", [NE, FCN, 128, DCN, 128], BF16, kind="Internal").ap()
        d["w2b"] = nc.dram_tensor("w2b", [NE, DCN, 128, FCN, 128], BF16, kind="Internal").ap()
        d["w1ab"] = nc.dram_tensor("w1ab", [FCN, 128, DCN, 128], BF16, kind="Internal").ap()
        d["w2ab"] = nc.dram_tensor("w2ab", [DCN, 128, FCN, 128], BF16, kind="Internal").ap()
        if self.debug:
            d["dbg_x1"] = nc.dram_tensor("dbg_x1", [S, D], F32, kind="ExternalOutput").ap()
            d["dbg_x2"] = nc.dram_tensor("dbg_x2", [S, D], F32, kind="ExternalOutput").ap()
            d["dbg_rw"] = nc.dram_tensor("dbg_rw", [128, CPB, NE], F32, kind="ExternalOutput").ap()
            d["dbg_acc"] = nc.dram_tensor("dbg_acc", [128, DCN, TBS], F32, kind="ExternalOutput").ap()
            d["dbg_we"] = nc.dram_tensor("dbg_we", [128, TBS], F32, kind="ExternalOutput").ap()
            d["dbg_stat"] = nc.dram_tensor("dbg_stat", [128, 16, 8], F32, kind="ExternalOutput").ap()
            d["dbg_hc"] = nc.dram_tensor("dbg_hc", [128, DCN, S], BF16, kind="ExternalOutput").ap()
            d["dbg_a"] = nc.dram_tensor("dbg_a", [128, S], BF16, kind="ExternalOutput").ap()
            d["dbg_qkv"] = nc.dram_tensor("dbg_qkv", [128, 3, S], BF16, kind="ExternalOutput").ap()
            d["dbg_at"] = nc.dram_tensor("dbg_at", [128, 16, 128], BF16, kind="ExternalOutput").ap()
            d["dbg_vs"] = nc.dram_tensor("dbg_vs", [128, 16, HD], BF16, kind="ExternalOutput").ap()
            d["dbg_rlog"] = nc.dram_tensor("dbg_rlog", [128, CPB, NE], F32, kind="ExternalOutput").ap()
        self.d = d

        with ExitStack() as ctx:
            self.ctx = ctx
            self._alloc_outer()
            self.blk = ctx.enter_context(nc.Block())
            self._emit()
        return nc

    def _alloc_outer(self):
        nc, ctx = self.nc, self.ctx
        sb = lambda n, sh, dt: ctx.enter_context(nc.sbuf_tensor(n, sh, dt))
        self.x_tm = sb("s_x", [128, TCN, D], F32)
        self.gfh = sb("s_gfh", [128, D], F32)
        self.bfh = sb("s_bfh", [128, D], F32)
        self.gab = sb("s_gab", [128, D], F32)
        self.bab = sb("s_bab", [128, D], F32)
        self.wr_sb = sb("s_wr", [128, DCN, NE], F32)
        self.b1c = sb("s_b1c", [128, NE, FCN], F32)
        self.b2c = sb("s_b2c", [128, NE, DCN], F32)
        self.b1ac = sb("s_b1ac", [128, FCN], F32)
        self.b2ac = sb("s_b2ac", [128, DCN], F32)
        self.bqkvs = sb("s_bqkvs", [128, 24], F32)
        self.br_b = sb("s_brb", [128, NE], F32)
        self.bout_bf = sb("s_boutbf", [1, D], BF16)
        self.identf = sb("s_idf", [128, 128], F32)
        self.identb = sb("s_idb", [128, 128], BF16)
        self.ones_f = sb("s_onesf", [1, 128], F32)
        self.ones_b = sb("s_onesb", [1, 128], BF16)
        self.stat = sb("s_stat", [128, 16, 8], F32)
        self.rstd1 = sb("s_rstd1", [128, TCN], F32)
        self.nmr1 = sb("s_nmr1", [128, TCN], F32)
        self.rlog = sb("s_rlog", [128, CPB, NE], F32)
        self.rw = sb("s_rw", [128, CPB, NE], F32)
        self.req = sb("s_req", [128, 2, NE], F32)
        self.w_row = sb("s_wrow", [1, TBS], F32)
        self.sm = sb("s_sm", [128, 8, 4], F32)
        self.xa_scr = sb("s_xa", [128, D], F32)
        self.t1_scr = sb("s_t1", [128, D], F32)
        self.z_scr = sb("s_z", [128, D], F32)
        self.ps4 = ctx.enter_context(nc.psum_tensor("p_ps4", [128, 2048], F32))
        self.ps2 = ctx.enter_context(nc.psum_tensor("p_ps2", [128, 1024], F32))
        self.psb = ctx.enter_context(nc.psum_tensor("p_psb", [128, 1024], BF16))
        self.pss = ctx.enter_context(nc.psum_tensor("p_pss", [128, 512], F32))
        self.sT = Sem(ctx, nc, "sT")
        self.sA = Sem(ctx, nc, "sA")
        self.sV = Sem(ctx, nc, "sV")
        self.sP = Sem(ctx, nc, "sP")
        self.dS = Sem(ctx, nc, "dS")
        self.sIc = Sem(ctx, nc, "sIc")
        self.sW1 = [Sem(ctx, nc, f"sW1_{i}") for i in range(4)]
        self.sW1x = [Sem(ctx, nc, f"sW1x_{i}") for i in range(4)]
        self.sW2x = [Sem(ctx, nc, f"sW2x_{i}") for i in range(4)]
        self.sW1c = [Sem(ctx, nc, f"sW1c_{i}") for i in range(4)]
        self.sW2 = [Sem(ctx, nc, f"sW2_{i}") for i in range(4)]
        self.sW2c = [Sem(ctx, nc, f"sW2c_{i}") for i in range(4)]
        self.sQ = [Sem(ctx, nc, f"sQ_{i}") for i in range(2)]
        self.sWo = Sem(ctx, nc, "sWo")

    def _emit(self):
        d = self.d
        dS, sIc, sA = self.dS, self.sIc, self.sA

        def init_dmas(e):
            x_r = d["x"].rearrange("(tc p) dd -> p tc dd", p=128)
            for i in range(4):
                dS.inc(e.dma_start(out=self.x_tm[:, 4 * i:4 * i + 4, :],
                                   in_=x_r[:, 4 * i:4 * i + 4, :]), 16)
            for dst, src in ((self.wr_sb[:], d["wr"][:, :, :]),
                             (self.b1c[:], d["b1c"][:, :, :]),
                             (self.b2c[:], d["b2c"][:, :, :]),
                             (self.b1ac[:], d["b1ac"][:, :]),
                             (self.b2ac[:], d["b2ac"][:, :]),
                             (self.bqkvs[:], d["bqkvc"][:, :]),
                             (self.br_b[:], d["br_b"][:, :]),
                             (self.identf[:], d["identf"][:, :]),
                             (self.ones_f[:], d["ones1"][:, :]),
                             (self.gab[:], d["g_a"][:, :]),
                             (self.bab[:], d["be_a"][:, :]),
                             (self.gfh[:], d["g_f"][:, :]),
                             (self.bfh[:], d["be_f"][:, :])):
                dS.inc(e.dma_start(out=dst, in_=src), 16)
        self.Sy(init_dmas)
        init_dS = dS.rec()

        def init_casts(e):
            sIc.inc(e.dma_start(out=self.identb[:], in_=d["identf"][:, :]), 16)
            sIc.inc(e.dma_start(out=self.ones_b[:], in_=d["ones1"][:, :]), 16)
            sIc.inc(e.dma_start(out=self.bout_bf[:], in_=d["bout_r"][:, :]), 16)
        self.P(init_casts, dr=False)

        def init_zero(e):
            e.memset(self.stat[:], 0.0)
            e.memset(self.sm[:], 0.0)
            e.memset(self.rlog[:], 0.0)
            e.memset(self.req[:], 0.0)
            e.memset(self.rw[:], 0.0)
            self.sV.inc(e.memset(self.w_row[:], 0.0))
        self.V(init_zero, dr=False)
        init_zero_r = self.sV.rec()

        def init_scale(e):
            _wait(e, [init_dS])
            e.activation(self.xa_scr[:], self.gfh[:], AF.Identity, scale=0.5)
            e.drain()
            e.activation(self.gfh[:], self.xa_scr[:], AF.Identity)
            e.drain()
            e.activation(self.xa_scr[:], self.bfh[:], AF.Identity, scale=0.5)
            e.drain()
            e.activation(self.bfh[:], self.xa_scr[:], AF.Identity)
            e.drain()
            e.activation(self.xa_scr[:, 0:8], self.bqkvs[:, 0:8], AF.Identity, scale=0.125)
            e.drain()
            sA.inc(e.activation(self.bqkvs[:, 0:8], self.xa_scr[:, 0:8], AF.Identity))
        self.A(init_scale)
        self.x_ready = [init_dS, sA.rec()]
        self.w1_cached = {}
        self.w2_cached = {}
        self.w1a_cached = {}
        self.w2a_cached = {}

        self.barrier()
        if self.stop_after != "init":
            self._moe_call(0)
            self.barrier()
            if self.debug:
                self._dump_x("dbg_x1")
                self.barrier()
        if self.stop_after not in ("init", "moe0"):
            self._attn_block()
            self.barrier()
            if self.debug:
                self._dump_x("dbg_x2")
                self.barrier()
        if self.stop_after not in ("init", "moe0", "attn"):
            self._moe_call(1)
            self.barrier()

        def out_dma(e):
            y_r = d["y"].rearrange("(tc p) dd -> p tc dd", p=128)
            for i in range(4):
                self.dS.inc(e.dma_start(out=y_r[:, 4 * i:4 * i + 4, :],
                                        in_=self.x_tm[:, 4 * i:4 * i + 4, :]), 16)
            e.wait_ge(self.dS.h, self.dS.n)
        self.Sy(out_dma)

    def _dump_x(self, name):
        def f(e):
            y_r = self.d[name].rearrange("(tc p) dd -> p tc dd", p=128)
            for i in range(4):
                self.dS.inc(e.dma_start(out=y_r[:, 4 * i:4 * i + 4, :],
                                        in_=self.x_tm[:, 4 * i:4 * i + 4, :]), 16)
        self.Sy(f)

    def _ln_stats_chunk(self, src_ap, rstd_col, nmr_col, dep_recs, sq_scr):
        sA, sV = self.sA, self.sV
        ssq = self.stat[:, 0:1, 6]
        ssm = self.stat[:, 1:2, 6]
        mu = self.stat[:, 2:3, 6]
        var = self.stat[:, 0:1, 7]
        t2 = self.stat[:, 1:2, 7]
        lnv = self.stat[:, 2:3, 7]

        def v1(e):
            _wait(e, dep_recs)
            e.scalar_tensor_tensor(sq_scr, src_ap, 1.0, src_ap,
                                   op0=ALU.mult, op1=ALU.mult, accum_out=ssq)
            e.reduce_sum(ssm, src_ap, axis=AX.X)
            e.drain()
            e.tensor_scalar(mu, ssm, 1.0 / D, None, op0=ALU.mult)
            e.tensor_scalar(var, ssq, 1.0 / D, None, op0=ALU.mult)
            e.drain()
            e.tensor_tensor(t2, mu, mu, op=ALU.mult)
            e.drain()
            sV.inc(e.scalar_tensor_tensor(var, var, EPS, t2,
                                          op0=ALU.add, op1=ALU.subtract))
        self.V(v1)
        v1r = sV.rec()

        def a2(e):
            _wait(e, [v1r])
            e.activation(lnv, var, AF.Ln)
            e.drain()
            sA.inc(e.activation(rstd_col, lnv, AF.Exp, scale=-0.5))
        self.A(a2)
        a2r = sA.rec()

        def v2(e):
            _wait(e, [a2r])
            sV.inc(e.scalar_tensor_tensor(nmr_col, mu, -1.0, rstd_col,
                                          op0=ALU.mult, op1=ALU.mult))
        self.V(v2)
        return sV.rec()

    def _mlp_block(self, lnT, hbuf, w1sl, w2sl, nw1buf, nw2buf,
                   w1_src, w1_cache, w1_cached, b1col,
                   w2_src, w2_cache, w2_cached, b2fn,
                   first, lnT_ready, state):
        sT, sA, sV = self.sT, self.sA, self.sV
        sW1, sW1c = self.sW1, self.sW1c
        sW2, sW2c = self.sW2, self.sW2c

        for fc in range(FCN):
            buf = fc % nw1buf
            cache_rec = None
            if first:
                def wd(e, fc=fc, buf=buf):
                    _wait(e, state["w1buf_free"][buf])
                    self.sW1x[buf].inc(e.dma_start(out=w1sl[:, buf, :, :], in_=w1_src(fc)), 16)
                self.P(wd, dr=False)
                dmar = self.sW1x[buf].rec()

                def wc(e, fc=fc, buf=buf, dmar=dmar):
                    _wait(e, [dmar])
                    sW1c[buf].inc(e.dma_start(out=w1_cache(fc), in_=w1sl[:, buf, :, :]), 16)
                self.Sy(wc)
                cache_rec = sW1c[buf].rec()
            else:
                def wd(e, fc=fc, buf=buf):
                    _wait(e, state["w1buf_free"][buf] + w1_cached["all"])
                    sW1[buf].inc(e.dma_start(out=w1sl[:, buf, :, :], in_=w1_cache(fc)), 16)
                self.Sy(wd)
                dmar = sW1[buf].rec()

            hb = fc % 2
            psum = self.ps4[:, hb * 512:(hb + 1) * 512]

            def w1mm(e, fc=fc, buf=buf, psum=psum, dmar=dmar):
                _wait(e, [dmar, lnT_ready if fc < 2 else None,
                          state["h_evac"][fc - 2]])
                for dc in range(DCN):
                    mm = e.matmul(psum, w1sl[:, buf, dc, :], lnT[:, dc, :],
                             start=(dc == 0), stop=(dc == DCN - 1))
                sT.inc(mm)
            self.T(w1mm)
            state["pe_w1"][fc] = sT.rec()
            state["w1buf_free"][buf] = [state["pe_w1"][fc]] + \
                ([cache_rec] if cache_rec else [])

            def hev(e, fc=fc, psum=psum, gr=state["pe_w1"][fc]):
                _wait(e, [gr])
                sV.inc(e.tensor_scalar(hbuf[:, fc, :], psum, b1col(fc), 0.0,
                                       op0=ALU.add, op1=ALU.max))
            self.V(hev)
            state["h_evac"][fc] = sV.rec()

        if first:
            w1_cached["all"] = [s.rec() for s in sW1c]
        h_all = state["h_evac"][FCN - 1]
        for dc in range(DCN):
            buf = dc % nw2buf
            cache_rec2 = None
            if first:
                def wd2(e, dc=dc, buf=buf):
                    _wait(e, state["w2buf_free"][buf])
                    self.sW2x[buf].inc(e.dma_start(out=w2sl[:, buf, :, :], in_=w2_src(dc)), 16)
                self.P(wd2, dr=False)
                dmar2 = self.sW2x[buf].rec()

                def wc2(e, dc=dc, buf=buf, dmar2=dmar2):
                    _wait(e, [dmar2])
                    sW2c[buf].inc(e.dma_start(out=w2_cache(dc), in_=w2sl[:, buf, :, :]), 16)
                self.Sy(wc2)
                cache_rec2 = sW2c[buf].rec()
            else:
                def wd2(e, dc=dc, buf=buf):
                    _wait(e, state["w2buf_free"][buf] + w2_cached["all"])
                    sW2[buf].inc(e.dma_start(out=w2sl[:, buf, :, :], in_=w2_cache(dc)), 16)
                self.A(wd2, dr=False)
                dmar2 = sW2[buf].rec()

            ob = dc % 2
            psum = self.ps4[:, 1024 + ob * 512:1024 + (ob + 1) * 512]

            def w2mm(e, dc=dc, buf=buf, psum=psum, dmar2=dmar2,
                     pfree=state["o_evac"][dc - 2]):
                _wait(e, [dmar2, h_all if dc < 2 else None, pfree])
                for fc in range(FCN):
                    mm = e.matmul(psum, w2sl[:, buf, fc, :], hbuf[:, fc, :],
                             start=(fc == 0), stop=(fc == FCN - 1))
                sT.inc(mm)
            self.T(w2mm)
            state["pe_w2"][dc] = sT.rec()
            state["w2buf_free"][buf] = [state["pe_w2"][dc]] + \
                ([cache_rec2] if cache_rec2 else [])

            b2fn(dc, psum, state["pe_w2"][dc])
        if first:
            w2_cached["all"] = [s.rec() for s in sW2c]

    def _moe_call(self, call):
        nc, d = self.nc, self.d
        sT, sA, sV, sP = self.sT, self.sA, self.sV, self.sP
        u = self.uid
        with ExitStack() as ph:
            sb = lambda n, sh, dt: ph.enter_context(nc.sbuf_tensor(u(n), sh, dt))
            lnT = sb("m_lnT", [128, DCN, TBS], BF16)
            u16 = sb("m_u16", [128, DCN, TBS], F32)
            hbuf = sb("m_h", [128, FCN, TBS], BF16)
            w1sl = sb("m_w1sl", [128, 4, DCN, 128], BF16)
            w2sl = sb("m_w2sl", [128, 4, FCN, 128], BF16)
            w_e = sb("m_we", [128, 2, TBS], F32)
            o_tmp = sb("m_ot", [128, 2, TBS], F32)

            state = dict(
                h_evac=[None] * FCN, pe_w1=[None] * FCN,
                o_evac=[None] * DCN, pe_w2=[None] * DCN,
                w1buf_free=[[] for _ in range(4)],
                w2buf_free=[[] for _ in range(4)],
            )
            o_add = [None] * DCN
            we_ready = [None, None]
            we_free = [None, None]

            for tb in range(self.n_tb):
                chunks = [tb * CPB + i for i in range(CPB)]
                lnF = u16
                lnF_ready = None
                prev_tr1 = getattr(self, "_moe_prev_tr1", None)
                for i, tc in enumerate(chunks):
                    rstd = self.stat[:, tc:tc + 1, 0]
                    nmr = self.stat[:, tc:tc + 1, 1]
                    dep = (self.x_ready if call == 0 else []) + \
                        [getattr(self, "_prev_epg", None)]
                    st_r = self._ln_stats_chunk(self.x_tm[:, tc, :], rstd, nmr,
                                                dep, self.t1_scr[:])

                    def ap1(e, tc=tc, rstd=rstd, nmr=nmr, st_r=st_r, pt=prev_tr1,
                            xu=getattr(self, "_prev_xupd", None)):
                        _wait(e, [st_r, pt, xu])
                        sV.inc(e.tensor_scalar(self.xa_scr[:], self.x_tm[:, tc, :],
                                               rstd, nmr, op0=ALU.mult, op1=ALU.add))
                    self.V(ap1)
                    ap1r = sV.rec()

                    def tr1(e, ap1r=ap1r, prev=lnF_ready):
                        _wait(e, [ap1r, prev])
                        for g in range(2):
                            for j in range(4):
                                dcx = g * 4 + j
                                mm = e.matmul(self.ps2[:, g * 512 + j * 128:g * 512 + (j + 1) * 128],
                                         self.xa_scr[:, dcx * 128:(dcx + 1) * 128],
                                         self.identf[:], start=(j == 0), stop=(j == 3),
                                         is_transpose=True)
                        sT.inc(mm)
                    self.T(tr1)
                    tr1r = sT.rec()
                    prev_tr1 = tr1r
                    self._moe_prev_tr1 = tr1r

                    def ev1(e, i=i, tr1r=tr1r):
                        _wait(e, [tr1r])
                        for g in range(2):
                            sV.inc(e.tensor_copy(
                                lnF.ap()[:, g * 4:(g + 1) * 4, i * 128:(i + 1) * 128],
                                self.ps2.ap()[:, g * 512:(g + 1) * 512]
                                .rearrange("p (a b) -> p a b", a=4)))
                    self.V(ev1)
                    lnF_ready = sV.rec()

                def cast_ln(e, lnF_ready=lnF_ready):
                    _wait(e, [lnF_ready])
                    sA.inc(e.copy(lnT[:], lnF[:]))
                self.A(cast_ln)
                lnT_ready = sA.rec()

                rmm_r = None
                router_r = None
                if not self.do_router:
                    pass
                def rmm(e, lnF_ready=lnF_ready):
                    _wait(e, [lnF_ready])
                    for i in range(CPB):
                        for dc in range(DCN):
                            mm = e.matmul(self.pss[:, i * 8:(i + 1) * 8],
                                     lnF[:, dc, i * 128:(i + 1) * 128],
                                     self.wr_sb[:, dc, :],
                                     start=(i == 0 and dc == 0),
                                     stop=(i == CPB - 1 and dc == DCN - 1))
                    sT.inc(mm)
                if self.do_router:
                    self.T(rmm)
                    rmm_r = sT.rec()

                def rsoft(e, rmm_r=rmm_r):
                    _wait(e, [rmm_r])
                    for i in range(CPB):
                        e.tensor_tensor(self.rlog[:, i, :], self.pss[:, i * 8:(i + 1) * 8],
                                        self.br_b[:], op=ALU.add)
                    e.drain()
                    m1 = self.stat[:, 0:CPB, 2]
                    e.reduce_max(m1, self.rlog[:], axis=AX.X)
                    e.drain()
                    for i in range(CPB):
                        e.tensor_scalar(self.req[:, 0, :], self.rlog[:, i, :],
                                        self.stat[:, i:i + 1, 2], None, op0=ALU.is_equal)
                        e.drain()
                        e.scalar_tensor_tensor(self.req[:, 1, :], self.req[:, 0, :], -1e30,
                                               self.rlog[:, i, :], op0=ALU.mult, op1=ALU.add)
                        e.drain()
                        e.reduce_max(self.stat[:, i:i + 1, 3], self.req[:, 1, :], axis=AX.X)
                        e.drain()
                    sV.inc(e.tensor_tensor(self.stat[:, 0:CPB, 4],
                                           self.stat[:, 0:CPB, 3],
                                           self.stat[:, 0:CPB, 2], op=ALU.subtract))
                if self.do_router:
                    self.V(rsoft)
                rsoft_r = sV.rec() if self.do_router else None

                def rtanh(e, rsoft_r=rsoft_r):
                    _wait(e, [rsoft_r])
                    sA.inc(e.activation(self.stat[:, 0:CPB, 5], self.stat[:, 0:CPB, 4],
                                        AF.Tanh, scale=0.5))
                if self.do_router:
                    self.A(rtanh)
                rtanh_r = sA.rec() if self.do_router else None

                def rfin(e, rtanh_r=rtanh_r):
                    _wait(e, [rtanh_r])
                    w2v = self.stat[:, 0:CPB, 4]
                    w1v = self.stat[:, 0:CPB, 5]
                    e.tensor_scalar(w2v, self.stat[:, 0:CPB, 5], 0.5, 0.5,
                                    op0=ALU.mult, op1=ALU.add)
                    e.drain()
                    e.tensor_scalar(w1v, w2v, -1.0, 1.0, op0=ALU.mult, op1=ALU.add)
                    e.drain()
                    last = None
                    for i in range(CPB):
                        e.tensor_scalar(self.req[:, 0, :], self.rlog[:, i, :],
                                        self.stat[:, i:i + 1, 2], None, op0=ALU.is_equal)
                        e.tensor_scalar(self.req[:, 1, :], self.rlog[:, i, :],
                                        self.stat[:, i:i + 1, 3], None, op0=ALU.is_equal)
                        e.drain()
                        e.tensor_scalar(self.req[:, 0, :], self.req[:, 0, :],
                                        self.stat[:, i:i + 1, 5], None, op0=ALU.mult)
                        e.drain()
                        last = e.scalar_tensor_tensor(self.rw[:, i, :], self.req[:, 1, :],
                                                      self.stat[:, i:i + 1, 4],
                                                      self.req[:, 0, :],
                                                      op0=ALU.mult, op1=ALU.add)
                        e.drain()
                    sV.inc(last)
                if self.do_router:
                    self.V(rfin)
                    router_r = sV.rec()

                def acc0(e, a=lnT_ready, b=rmm_r, c=router_r):
                    _wait(e, [a, b, c])
                    sP.inc(e.memset(u16[:], 0.0))
                self.P(acc0)
                o_add = [sP.rec()] * DCN

                first = (call == 0 and tb == 0)
                for ei in range(self.n_exp):
                    web = ei % 2

                    def wet(e, ei=ei, rr=router_r, wf=we_free[web],
                            wprev=we_ready[1 - web]):
                        _wait(e, [rr, wf, wprev])
                        for i in range(CPB):
                            mm = e.matmul(self.pss.ap()[0:1, i * 128:(i + 1) * 128],
                                     self.rw[:, i, ei:ei + 1], self.identf[:],
                                     start=(i == 0), stop=(i == CPB - 1),
                                     is_transpose=True)
                        sT.inc(mm)
                    self.T(wet)
                    wet_r = sT.rec()

                    def wev(e, wet_r=wet_r):
                        _wait(e, [wet_r])
                        sV.inc(e.tensor_copy(self.w_row[:], self.pss.ap()[0:1, 0:TBS]))
                    self.V(wev)
                    wev_r = sV.rec()

                    def webm(e, wev_r=wev_r):
                        _wait(e, [wev_r])
                        sT.inc(e.matmul(self.pss[:, 0:TBS], self.ones_f[:], self.w_row[:],
                                        start=True, stop=True))
                    self.T(webm)
                    webm_r = sT.rec()

                    def webe(e, web=web, webm_r=webm_r):
                        _wait(e, [webm_r])
                        sV.inc(e.tensor_copy(w_e[:, web, :], self.pss[:, 0:TBS]))
                    self.V(webe)
                    we_ready[web] = sV.rec()

                    def b2fn(dc, psum, group_rec, ei=ei, web=web):
                        def oev(e, dc=dc, psum=psum, gr=group_rec,
                                wer=we_ready[web], pf=o_add[(dc - 2) % DCN]):
                            _wait(e, [gr, wer, pf])
                            sV.inc(e.scalar_tensor_tensor(
                                o_tmp[:, dc % 2, :], psum, self.b2c[:, ei, dc:dc + 1],
                                w_e[:, web, :], op0=ALU.add, op1=ALU.mult))
                        self.V(oev)
                        state["o_evac"][dc] = sV.rec()
                        we_free[web] = sV.rec()

                        def oad(e, dc=dc, er=state["o_evac"][dc]):
                            _wait(e, [er])
                            sP.inc(e.tensor_tensor(u16[:, dc, :], u16[:, dc, :],
                                                   o_tmp[:, dc % 2, :], op=ALU.add))
                        self.P(oad)
                        o_add[dc] = sP.rec()

                    self._mlp_block(
                        lnT, hbuf, w1sl, w2sl, 4, 4,
                        lambda fc, ei=ei: d["w1s"][ei, fc, :, :, :],
                        lambda fc, ei=ei: d["w1b"][ei, fc, :, :, :],
                        self._kcache("w1", ei),
                        lambda fc, ei=ei: self.b1c[:, ei, fc:fc + 1],
                        lambda dc, ei=ei: d["w2s"][ei, dc, :, :, :],
                        lambda dc, ei=ei: d["w2b"][ei, dc, :, :, :],
                        self._kcache("w2", ei),
                        b2fn, first, lnT_ready, state)

                dbg_acc_deps = [o_add[dc] for dc in range(DCN)] + [router_r, we_ready[0]]

                if not self.do_ep:
                    continue
                acc_done = [o_add[dc] for dc in range(DCN)]
                ep_ev = None
                x_upd_prev = None
                epg_prev = None
                for i, tc in enumerate(chunks):
                    def ept(e, i=i, deps=acc_done, prev=ep_ev):
                        _wait(e, deps + [prev])
                        for g in range(2):
                            for j in range(4):
                                dcx = g * 4 + j
                                mm = e.matmul(self.ps2[:, g * 512 + j * 128:g * 512 + (j + 1) * 128],
                                         u16[:, dcx, i * 128:(i + 1) * 128],
                                         self.identf[:], start=(j == 0), stop=(j == 3),
                                         is_transpose=True)
                        sT.inc(mm)
                    self.T(ept)
                    eptr = sT.rec()

                    def epv(e, eptr=eptr, xup=x_upd_prev):
                        _wait(e, [eptr, xup])
                        sV.inc(e.tensor_copy(self.t1_scr[:], self.ps2[:]))
                    self.V(epv)
                    ep_ev = sV.rec()

                    def epa(e, tc=tc, pg=epg_prev, xu=x_upd_prev):
                        _wait(e, [pg, xu])
                        rstd = self.stat[:, tc:tc + 1, 0]
                        nmr = self.stat[:, tc:tc + 1, 1]
                        sV.inc(e.tensor_scalar(self.xa_scr[:], self.x_tm[:, tc, :],
                                               rstd, nmr, op0=ALU.mult, op1=ALU.add))
                    self.V(epa)
                    epar = sV.rec()

                    def epz(e, epar=epar, evr=ep_ev):
                        _wait(e, [epar, evr])
                        sP.inc(e.tensor_tensor(self.z_scr[:], self.xa_scr[:],
                                               self.t1_scr[:], op=ALU.add))
                    self.P(epz)
                    z_r = sP.rec()

                    rstdz = self.stat[:, 8 + i:9 + i, 4]
                    nmrz = self.stat[:, 8 + i:9 + i, 5]
                    st_r = self._ln_stats_chunk(self.z_scr[:], rstdz, nmrz, [z_r],
                                                self.t1_scr[:])

                    def epn(e, st_r=st_r, rstdz=rstdz, nmrz=nmrz):
                        _wait(e, [st_r])
                        sV.inc(e.tensor_scalar(self.t1_scr[:], self.z_scr[:],
                                               rstdz, nmrz, op0=ALU.mult, op1=ALU.add))
                    self.V(epn)
                    epnr = sV.rec()

                    def epg(e, epnr=epnr):
                        _wait(e, [epnr])
                        sV.inc(e.tensor_tensor(self.xa_scr[:], self.t1_scr[:],
                                               self.gfh[:], op=ALU.mult))
                    self.V(epg)
                    epgr = sV.rec()
                    epg_prev = epgr
                    self._prev_epg = epgr

                    def epx(e, tc=tc, epgr=epgr):
                        _wait(e, [epgr])
                        e.tensor_tensor(self.x_tm[:, tc, :], self.x_tm[:, tc, :],
                                        self.xa_scr[:], op=ALU.add)
                        e.drain()
                        sP.inc(e.tensor_tensor(self.x_tm[:, tc, :], self.x_tm[:, tc, :],
                                               self.bfh[:], op=ALU.add))
                    self.P(epx)
                    x_upd_prev = sP.rec()
                    self._prev_xupd = x_upd_prev

                if self.debug and call == 0 and tb == 0 and self.n_tb == 1:
                    def dbg_dump(e, deps=dbg_acc_deps + [x_upd_prev, sA.rec(),
                                                         sV.rec(), sT.rec()]):
                        _wait(e, deps)
                        self.dS.inc(e.dma_start(out=d["dbg_rw"][:, :, :], in_=self.rw[:]), 16)
                        self.dS.inc(e.dma_start(out=d["dbg_acc"][:, :, :], in_=u16[:]), 16)
                        self.dS.inc(e.dma_start(out=d["dbg_we"][:, :], in_=w_e[:, 0, :]), 16)
                        self.dS.inc(e.dma_start(out=d["dbg_stat"][:, :, :], in_=self.stat[:]), 16)
                        self.dS.inc(e.dma_start(out=d["dbg_rlog"][:, :, :], in_=self.rlog[:]), 16)
                    self.Sy(dbg_dump)

    def _kcache(self, kind, ei):
        base = self.w1_cached if kind == "w1" else self.w2_cached
        return base.setdefault(ei, {})

    def _attn_block(self):
        nc, d = self.nc, self.d
        sT, sA, sV, sP = self.sT, self.sA, self.sV, self.sP
        dS, sQ, sWo = self.dS, self.sQ, self.sWo
        u = self.uid
        with ExitStack() as ph:
            sbA = lambda n, sh, dt: ph.enter_context(nc.sbuf_tensor(u(n), sh, dt))
            hcT = sbA("a_hcT", [128, DCN, S], BF16)
            h1v = hcT.ap().rearrange("p dc (tc m) -> p dc tc m", m=128)
            h1_written = [None] * TCN
            hc_last = None

            with ExitStack() as ph2:
                sb2 = lambda n, sh, dt: ph2.enter_context(nc.sbuf_tensor(u(n), sh, dt))
                ln1T = sb2("a_ln1T", [128, DCN, S], BF16)
                qkvT = sb2("a_qkvT", [128, 3, S], BF16)
                qkvw = sb2("a_qkvw", [128, 2, 3, DCN, 128], BF16)
                abuf = sb2("a_abuf", [128, 2, S], BF16)
                atb = sb2("a_atb", [128, 16, 128], BF16)
                vst = sb2("a_vst", [128, 16, HD], BF16)

                ln_ev = None
                prev_tr1 = getattr(self, "_moe_prev_tr1", None)
                for tc in range(TCN):
                    st_r = self._ln_stats_chunk(self.x_tm[:, tc, :],
                                                self.rstd1[:, tc:tc + 1],
                                                self.nmr1[:, tc:tc + 1], [],
                                                self.t1_scr[:])

                    def ap1(e, tc=tc, st_r=st_r, pt=prev_tr1):
                        _wait(e, [st_r, pt])
                        sV.inc(e.tensor_scalar(self.xa_scr[:], self.x_tm[:, tc, :],
                                               self.rstd1[:, tc:tc + 1],
                                               self.nmr1[:, tc:tc + 1],
                                               op0=ALU.mult, op1=ALU.add))
                    self.V(ap1)
                    ap1r = sV.rec()

                    def tr1(e, ap1r=ap1r, prev=ln_ev):
                        _wait(e, [ap1r, prev])
                        for g in range(2):
                            for j in range(4):
                                dcx = g * 4 + j
                                mm = e.matmul(self.ps2[:, g * 512 + j * 128:g * 512 + (j + 1) * 128],
                                         self.xa_scr[:, dcx * 128:(dcx + 1) * 128],
                                         self.identf[:], start=(j == 0), stop=(j == 3),
                                         is_transpose=True)
                        sT.inc(mm)
                    self.T(tr1)
                    tr1r = sT.rec()
                    prev_tr1 = tr1r
                    self._moe_prev_tr1 = tr1r

                    def ev1(e, tc=tc, tr1r=tr1r):
                        _wait(e, [tr1r])
                        for g in range(2):
                            sV.inc(e.tensor_copy(
                                ln1T.ap()[:, g * 4:(g + 1) * 4, tc * 128:(tc + 1) * 128],
                                self.ps2.ap()[:, g * 512:(g + 1) * 512]
                                .rearrange("p (a b) -> p a b", a=4)))
                    self.V(ev1)
                    ln_ev = sV.rec()
                ln1_ready = ln_ev

                qkv_dma = [None, None]
                qkv_pe_done = [None, None]
                at_ev_last = None
                vev_last = None
                bank_gate = [None, None]
                abuf_gate = [None, None]
                for hp in range(8):
                    qb = hp % 2
                    jcs = [hp, 8 + hp, 16 + hp]

                    def qwd(e, jcs=jcs, qb=qb, free=qkv_pe_done[qb]):
                        _wait(e, [free])
                        for si, jc in enumerate(jcs):
                            sQ[qb].inc(e.dma_start(out=qkvw[:, qb, si, :, :],
                                                   in_=d["wqkv"][jc, :, :, :]), 16)
                    self.P(qwd, dr=False)
                    qkv_dma[qb] = sQ[qb].rec()

                    qev_last = None
                    for si in range(3):
                        for tq in range(4):
                            hb = (si * 4 + tq) % 2
                            psum = self.ps4[:, hb * 512:(hb + 1) * 512]

                            def qmm(e, si=si, tq=tq, qb=qb, psum=psum, dr=qkv_dma[qb],
                                    ev=qev_last):
                                _wait(e, [dr, ln1_ready, ev])
                                for dc in range(DCN):
                                    mm = e.matmul(psum, qkvw[:, qb, si, dc, :],
                                             ln1T[:, dc, tq * 512:(tq + 1) * 512],
                                             start=(dc == 0), stop=(dc == DCN - 1))
                                sT.inc(mm)
                            self.T(qmm)
                            qmr = sT.rec()

                            def qev(e, si=si, tq=tq, psum=psum, qmr=qmr, hp=hp):
                                _wait(e, [qmr])
                                col = [hp, 8 + hp, 16 + hp][si]
                                sV.inc(e.tensor_scalar(
                                    qkvT[:, si, tq * 512:(tq + 1) * 512], psum,
                                    0.125 if si == 0 else 1.0,
                                    self.bqkvs[:, col:col + 1],
                                    op0=ALU.mult, op1=ALU.add))
                            self.V(qev)
                            qev_last = sV.rec()
                    qkv_ready = qev_last

                    for hh in range(2):
                        h = 2 * hp + hh
                        s = STRIDES[h]
                        Ls = S // s
                        KC = Ls // 128
                        off = 64 * hh

                        def vtr(e, s=s, off=off, KC=KC, qr=qkv_ready):
                            _wait(e, [qr])
                            for kc in range(KC):
                                src = qkvT.ap()[off:off + 64, 2,
                                                kc * 128 * s:(kc * 128 + 128) * s:s]
                                mm = e.matmul(self.psb[:, kc * 64:(kc + 1) * 64], src,
                                         self.identb.ap()[off:off + 64, off:off + 64],
                                         start=(kc == 0), stop=(kc == KC - 1),
                                         is_transpose=True)
                            sT.inc(mm)
                        self.T(vtr)
                        vtr_r = sT.rec()

                        def vev(e, KC=KC, vtr_r=vtr_r):
                            _wait(e, [vtr_r])
                            sV.inc(e.tensor_copy(vst.ap()[:, 0:KC, :],
                                                 self.psb.ap()[:, 0:KC * 64]
                                                 .rearrange("p (a b) -> p a b", b=64)))
                        self.V(vev)
                        vev_r = sV.rec()
                        vev_last = vev_r

                        pss_gate = None
                        pend = None

                        def emit_tail(qc, qg, par, a2_r):
                            nonlocal pss_gate, at_ev_last, hc_last
                            at_evs = []
                            nbat = (KC + 7) // 8
                            for bi in range(nbat):
                                k0, k1 = bi * 8, min(KC, bi * 8 + 8)

                                def atrb(e, k0=k0, k1=k1, r=a2_r, vr=vev_r,
                                         aev=at_evs[-1] if at_evs else None, par=par):
                                    _wait(e, [r, vr, aev])
                                    for kc in range(k0, k1):
                                        mm = e.matmul(self.psb[:, (kc - k0) * 128:(kc - k0 + 1) * 128],
                                                 abuf[:, par, kc * 128:(kc + 1) * 128],
                                                 self.identb[:],
                                                 start=(kc == k0), stop=(kc == k1 - 1),
                                                 is_transpose=True)
                                    sT.inc(mm)
                                self.T(atrb)
                                atr_r = sT.rec()

                                def atev(e, k0=k0, k1=k1, atr_r=atr_r):
                                    _wait(e, [atr_r])
                                    sV.inc(e.tensor_copy(
                                        atb.ap()[:, k0:k1, :],
                                        self.psb.ap()[:, 0:(k1 - k0) * 128]
                                        .rearrange("p (a b) -> p a b", b=128)))
                                self.V(atev)
                                at_evs.append(sV.rec())
                                at_ev_last = sV.rec()
                            at_ready = at_evs[-1]
                            abuf_gate[par] = sT.rec()

                            def avm(e, qg=qg, KC=KC, atr=at_ready, vr=vev_r,
                                    gate=pss_gate):
                                _wait(e, [atr, vr] + ([gate] if qg == 0 else []))
                                for kc in range(KC):
                                    mm = e.matmul(self.pss.ap()[0:64, qg * 128:(qg + 1) * 128],
                                             vst[:, kc, :], atb[:, kc, :],
                                             start=(qg == 0 and kc == 0),
                                             stop=(qg == 3 and kc == KC - 1))
                                sT.inc(mm)
                            self.T(avm)
                            av_r = sT.rec()

                            if qg == 3:
                                def avev(e, h=h, qc=qc, av_r=av_r):
                                    _wait(e, [av_r])
                                    dst = hcT.ap()[64 * (h % 2):64 * (h % 2) + 64,
                                                   h // 2, (qc - 3) * 128:(qc + 1) * 128]
                                    sA.inc(e.activation(dst, self.pss.ap()[0:64, 0:512],
                                                        AF.Identity))
                                self.A(avev)
                                hc_last = sA.rec()
                                pss_gate = hc_last

                        for qc in range(TCN + 1):
                            if qc < TCN:
                                qg = qc % 4
                                par = qc % 2
                                bpar = (qc % 2) if Ls <= 1024 else 0
                                boff = bpar * 1024
                                nmc = self.sm[:, 0:1, qg]
                                zc = self.sm[:, 1:2, qg]
                                rzc = self.sm[:, 2:3, qg]

                                def smm(e, off=off, qc=qc, s=s, Ls=Ls, qr=qkv_ready,
                                        gate=bank_gate[bpar], boff=boff):
                                    _wait(e, [qr, gate])
                                    nkb = (Ls + 511) // 512
                                    for kb in range(nkb):
                                        n = min(512, Ls - kb * 512)
                                        src = qkvT.ap()[off:off + 64, 1,
                                                        kb * 512 * s:(kb * 512 + n) * s:s]
                                        mm = e.matmul(self.ps4[:, boff + kb * 512:boff + kb * 512 + n],
                                                 qkvT.ap()[off:off + 64, 0,
                                                           qc * 128:(qc + 1) * 128],
                                                 src, start=True, stop=True)
                                    sT.inc(mm)
                                self.T(smm)
                                smm_r = sT.rec()

                                def smax(e, Ls=Ls, nmc=nmc, smm_r=smm_r, boff=boff):
                                    _wait(e, [smm_r])
                                    sV.inc(e.reduce_max(nmc, self.ps4[:, boff:boff + Ls],
                                                        axis=AX.X, negate=True))
                                self.V(smax)
                                smax_r = sV.rec()

                                def sexp1(e, Ls=Ls, nmc=nmc, zc=zc, smax_r=smax_r,
                                          gate=abuf_gate[par], boff=boff, par=par):
                                    _wait(e, [smax_r, gate])
                                    sA.inc(e.activation(abuf[:, par, 0:Ls],
                                                        self.ps4[:, boff:boff + Ls],
                                                        AF.Exp, bias=nmc, accum_out=zc))
                                self.A(sexp1)
                                sexp1_r = sA.rec()
                                bank_gate[bpar] = sexp1_r

                                def srz(e, zc=zc, rzc=rzc, r=sexp1_r):
                                    _wait(e, [r])
                                    sV.inc(e.reciprocal(rzc, zc))
                                self.V(srz)
                                srz_r = sV.rec()

                                def sscale(e, Ls=Ls, rzc=rzc, par=par, r=srz_r):
                                    _wait(e, [r])
                                    sV.inc(e.tensor_scalar(abuf[:, par, 0:Ls],
                                                           abuf[:, par, 0:Ls],
                                                           rzc, None, op0=ALU.mult))
                                self.V(sscale)
                                a2_r = sV.rec()
                            if pend is not None:
                                emit_tail(*pend)
                            pend = (qc, qg, par, a2_r) if qc < TCN else None

                    qkv_pe_done[qb] = sT.rec()

            if self.debug:
                def dump_hc(e, hl=hc_last, qd=qkv_pe_done[1], aev=at_ev_last,
                            vv=vev_last):
                    _wait(e, [hl, qd, aev, vv])
                    self.dS.inc(e.dma_start(out=d["dbg_hc"][:, :, :], in_=hcT[:]), 16)
                    self.dS.inc(e.dma_start(out=d["dbg_a"][:, :], in_=abuf.ap()[:, 1, :]), 16)
                    self.dS.inc(e.dma_start(out=d["dbg_qkv"][:, :, :], in_=qkvT[:]), 16)
                    self.dS.inc(e.dma_start(out=d["dbg_at"][:, :, :], in_=atb[:]), 16)
                    self.dS.inc(e.dma_start(out=d["dbg_vs"][:, :, :], in_=vst[:]), 16)
                self.Sy(dump_hc)
            self.barrier()
            with ExitStack() as ph3:
                sb3 = lambda n, sh, dt: ph3.enter_context(nc.sbuf_tensor(u(n), sh, dt))
                wout_sb = sb3("a_wout", [128, DCN, D], BF16)

                def wod(e):
                    sWo.inc(e.dma_start(out=wout_sb[:], in_=d["wout"]
                                        .rearrange("(dc p) n -> p dc n", p=128)), 16)
                self.P(wod, dr=False)
                wout_r = sWo.rec()

                z_prev = None
                wog_prev = None
                h1w_prev = None
                for tc in range(TCN):
                    def womm(e, tc=tc, wr=wout_r, hca=hc_last, zp=z_prev):
                        _wait(e, [wr, hca, zp])
                        for half in range(2):
                            for dc in range(DCN):
                                e.matmul(self.ps2[:, half * 512:(half + 1) * 512],
                                         hcT.ap()[:, dc, tc * 128:(tc + 1) * 128],
                                         wout_sb[:, dc, half * 512:(half + 1) * 512],
                                         start=(dc == 0), stop=False)
                            mm = e.matmul(self.ps2[:, half * 512:(half + 1) * 512],
                                     self.ones_b[:],
                                     self.bout_bf.ap()[:, half * 512:(half + 1) * 512],
                                     start=False, stop=True)
                        sT.inc(mm)
                    self.T(womm)
                    womm_r = sT.rec()

                    def woxa(e, tc=tc, wp=wog_prev):
                        _wait(e, [wp])
                        sV.inc(e.tensor_scalar(self.xa_scr[:], self.x_tm[:, tc, :],
                                               self.rstd1[:, tc:tc + 1],
                                               self.nmr1[:, tc:tc + 1],
                                               op0=ALU.mult, op1=ALU.add))
                    self.V(woxa)
                    woxa_r = sV.rec()

                    def woz(e, womm_r=womm_r, woxa_r=woxa_r):
                        _wait(e, [womm_r, woxa_r])
                        sV.inc(e.tensor_tensor(self.z_scr[:], self.xa_scr[:],
                                               self.ps2[:], op=ALU.add))
                    self.V(woz)
                    z_prev = sV.rec()

                    rstdh = self.sm[:, 4:5, 0]
                    nmrh = self.sm[:, 5:6, 0]
                    st_r = self._ln_stats_chunk(self.z_scr[:], rstdh, nmrh,
                                                [z_prev, h1w_prev], self.t1_scr[:])

                    def woh(e, st_r=st_r, rstdh=rstdh, nmrh=nmrh):
                        _wait(e, [st_r])
                        sV.inc(e.tensor_scalar(self.xa_scr[:], self.z_scr[:],
                                               rstdh, nmrh, op0=ALU.mult, op1=ALU.add))
                    self.V(woh)
                    woh_r = sV.rec()

                    def wog(e, woh_r=woh_r):
                        _wait(e, [woh_r])
                        sV.inc(e.tensor_tensor(self.t1_scr[:], self.xa_scr[:],
                                               self.gab[:], op=ALU.mult))
                    self.V(wog)
                    wog_r = sV.rec()
                    wog_prev = wog_r

                    def woh1(e, tc=tc, wog_r=wog_r, hp_=h1w_prev):
                        _wait(e, [wog_r, hp_])
                        sP.inc(e.tensor_tensor(
                            h1v[:, :, tc, :],
                            self.t1_scr.ap().rearrange("p (a b) -> p a b", b=128),
                            self.bab.ap().rearrange("p (a b) -> p a b", b=128),
                            op=ALU.add))
                    self.P(woh1)
                    h1_written[tc] = sP.rec()
                    h1w_prev = h1_written[tc]
            self.barrier()

            with ExitStack() as ph4:
                sb4 = lambda n, sh, dt: ph4.enter_context(nc.sbuf_tensor(u(n), sh, dt))
                lnT = sb4("f_lnT", [128, DCN, TBS], BF16)
                u16 = sb4("f_u16", [128, DCN, TBS], F32)
                hbuf = sb4("f_h", [128, FCN, TBS], BF16)
                w1sl = sb4("f_w1sl", [128, 4, DCN, 128], BF16)
                w2sl = sb4("f_w2sl", [128, 1, FCN, 128], BF16)

                state = dict(
                    h_evac=[None] * FCN, pe_w1=[None] * FCN,
                    o_evac=[None] * DCN, pe_w2=[None] * DCN,
                    w1buf_free=[[] for _ in range(4)],
                    w2buf_free=[[]],
                )
                o_guard = [None] * DCN

                for tb in range(NTB):
                    chunks = [tb * CPB + i for i in range(CPB)]
                    first = (tb == 0)
                    lnq = None
                    for i, tc in enumerate(chunks):
                        def htr(e, tc=tc, h1r=h1_written[tc], prev=lnq):
                            _wait(e, [h1r, prev])
                            for dcx in range(DCN):
                                mm = e.matmul(self.psb[:, dcx * 128:(dcx + 1) * 128],
                                         h1v[:, dcx, tc, :], self.identb[:],
                                         start=(dcx == 0), stop=(dcx == DCN - 1),
                                         is_transpose=True)
                            sT.inc(mm)
                        self.T(htr)
                        htr_r = sT.rec()

                        def hev2(e, i=i, htr_r=htr_r):
                            _wait(e, [htr_r])
                            sV.inc(e.tensor_copy(lnT.ap()[:, :, i * 128:(i + 1) * 128],
                                                 self.psb.ap()[:, 0:1024]
                                                 .rearrange("p (a b) -> p a b", a=8)))
                        self.V(hev2)
                        lnq = sV.rec()
                    lnT_ready = lnq

                    def b2fn(dc, psum, group_rec):
                        def oev(e, dc=dc, psum=psum, gr=group_rec, og=o_guard[dc]):
                            _wait(e, [gr, og])
                            sV.inc(e.tensor_scalar(u16[:, dc, :], psum,
                                                   self.b2ac[:, dc:dc + 1], None,
                                                   op0=ALU.add))
                        self.V(oev)
                        state["o_evac"][dc] = sV.rec()

                    self._mlp_block(
                        lnT, hbuf, w1sl, w2sl, 4, 1,
                        lambda fc: d["w1a"][fc, :, :, :],
                        lambda fc: d["w1ab"][fc, :, :, :],
                        self.w1a_cached,
                        lambda fc: self.b1ac[:, fc:fc + 1],
                        lambda dc: d["w2a"][dc, :, :, :],
                        lambda dc: d["w2ab"][dc, :, :, :],
                        self.w2a_cached,
                        b2fn, first, lnT_ready, state)

                    ffn_done = [state["o_evac"][dc] for dc in range(DCN)]
                    ep_ev = None
                    x_upd_prev = None
                    ffn_epn_prev = getattr(self, "_ffn_epn_prev", [None])
                    self._ffn_epn_prev = ffn_epn_prev
                    for i, tc in enumerate(chunks):
                        def ept(e, i=i, deps=ffn_done, prev=ep_ev):
                            _wait(e, deps + [prev])
                            for g in range(2):
                                for j in range(4):
                                    dcx = g * 4 + j
                                    mm = e.matmul(self.ps2[:, g * 512 + j * 128:g * 512 + (j + 1) * 128],
                                             u16[:, dcx, i * 128:(i + 1) * 128],
                                             self.identf[:], start=(j == 0), stop=(j == 3),
                                             is_transpose=True)
                            sT.inc(mm)
                        self.T(ept)
                        eptr = sT.rec()
                        o_guard = [eptr] * DCN

                        def epv(e, eptr=eptr, xup=x_upd_prev):
                            _wait(e, [eptr, xup])
                            sV.inc(e.tensor_copy(self.t1_scr[:], self.ps2[:]))
                        self.V(epv)
                        ep_ev = sV.rec()

                        def epz(e, tc=tc, evr=ep_ev, h1r=h1_written[tc],
                                pn=ffn_epn_prev[0]):
                            _wait(e, [evr, h1r, pn])
                            sP.inc(e.tensor_tensor(
                                self.z_scr.ap().rearrange("p (a b) -> p a b", b=128),
                                self.t1_scr.ap().rearrange("p (a b) -> p a b", b=128),
                                h1v[:, :, tc, :], op=ALU.add))
                        self.P(epz)
                        z_r = sP.rec()

                        rstdz = self.sm[:, 6:7, 0]
                        nmrz = self.sm[:, 7:8, 0]
                        st_r = self._ln_stats_chunk(self.z_scr[:], rstdz, nmrz, [z_r],
                                                    self.t1_scr[:])

                        def epn(e, st_r=st_r, rstdz=rstdz, nmrz=nmrz):
                            _wait(e, [st_r])
                            sV.inc(e.tensor_scalar(self.t1_scr[:], self.z_scr[:],
                                                   rstdz, nmrz, op0=ALU.mult, op1=ALU.add))
                        self.V(epn)
                        epnr = sV.rec()
                        ffn_epn_prev[0] = epnr

                        def epg(e, epnr=epnr):
                            _wait(e, [epnr])
                            sV.inc(e.tensor_tensor(self.xa_scr[:], self.t1_scr[:],
                                                   self.gab[:], op=ALU.mult))
                        self.V(epg)
                        epgr = sV.rec()

                        def epx(e, tc=tc, epgr=epgr):
                            _wait(e, [epgr])
                            e.tensor_tensor(self.x_tm[:, tc, :], self.x_tm[:, tc, :],
                                            self.xa_scr[:], op=ALU.add)
                            e.drain()
                            sP.inc(e.tensor_tensor(self.x_tm[:, tc, :],
                                                   self.x_tm[:, tc, :],
                                                   self.bab[:], op=ALU.add))
                        self.P(epx)
                        x_upd_prev = sP.rec()


def _prep_weights(ins):
    g = lambda k: np.ascontiguousarray(np.asarray(ins[k]))
    W1 = g("L0_ffn_W1")
    W2 = g("L0_ffn_W2")
    W1a = g("L0_attn_W1")
    W2a = g("L0_attn_W2")
    Wqkv = g("L0_attn_Wqkv")
    com = {}
    com["w1s"] = np.ascontiguousarray(
        W1.reshape(NE, FCN, 128, DCN, 128).transpose(0, 1, 4, 3, 2))
    com["w2s"] = np.ascontiguousarray(
        W2.reshape(NE, DCN, 128, FCN, 128).transpose(0, 1, 4, 3, 2))
    com["w1a"] = np.ascontiguousarray(
        W1a.reshape(DCN, 128, FCN, 128).transpose(2, 1, 0, 3))
    com["w2a"] = np.ascontiguousarray(
        W2a.reshape(FCN, 128, DCN, 128).transpose(2, 1, 0, 3))
    com["wqkv"] = np.ascontiguousarray(
        Wqkv.reshape(DCN, 128, 24, 128).transpose(2, 1, 0, 3))
    com["wout"] = g("L0_attn_Wout")
    com["wr"] = np.ascontiguousarray(
        g("L0_ffn_Wr").T.reshape(DCN, 128, NE).transpose(1, 0, 2))
    com["b1c"] = np.ascontiguousarray(
        g("L0_ffn_b1").reshape(NE, FCN, 128).transpose(2, 0, 1))
    com["b2c"] = np.ascontiguousarray(
        g("L0_ffn_b2").reshape(NE, DCN, 128).transpose(2, 0, 1))
    com["b1ac"] = np.ascontiguousarray(g("L0_attn_b1").reshape(FCN, 128).T)
    com["b2ac"] = np.ascontiguousarray(g("L0_attn_b2").reshape(DCN, 128).T)
    com["bqkvc"] = np.ascontiguousarray(g("L0_attn_bqkv").reshape(24, 128).T)
    com["br_b"] = np.ascontiguousarray(np.broadcast_to(g("L0_ffn_br"), (128, NE)))
    com["g_f"] = np.ascontiguousarray(np.broadcast_to(g("L0_ffn_gamma"), (128, D)))
    com["be_f"] = np.ascontiguousarray(np.broadcast_to(g("L0_ffn_beta"), (128, D)))
    com["g_a"] = np.ascontiguousarray(np.broadcast_to(g("L0_attn_gamma"), (128, D)))
    com["be_a"] = np.ascontiguousarray(np.broadcast_to(g("L0_attn_beta"), (128, D)))
    com["bout_r"] = np.ascontiguousarray(g("L0_attn_bout").reshape(1, D))
    com["identf"] = np.eye(128, dtype=np.float32)
    com["ones1"] = np.ones((1, 128), dtype=np.float32)
    return com


def kernel(**inputs):
    com = _prep_weights(inputs)
    xs = np.ascontiguousarray(np.asarray(inputs["L0_attn_x"]))
    nc = Builder(debug=False).build()
    in_maps = []
    for b in range(8):
        m = dict(com)
        m["x"] = np.ascontiguousarray(xs[b])
        in_maps.append(m)
    res = run_bass_kernel_spmd(nc, in_maps, core_ids=list(range(8)))
    return np.stack([np.asarray(r["y"]) for r in res.results], axis=0)


# version 05
# speedup vs baseline: 2.0347x; 1.0865x over previous
import numpy as np
import ml_dtypes
from contextlib import ExitStack

import concourse.bass as bass
import concourse.mybir as mybir
from concourse.bass_utils import run_bass_kernel_spmd

F32 = mybir.dt.float32
BF16 = mybir.dt.bfloat16
AF = mybir.ActivationFunctionType
ALU = mybir.AluOpType
AX = mybir.AxisListType

S = 2048
TCN = 16
D = 1024
DCN = 8
FF = 4096
FCN = 32
NE = 8
TBS = 512
NTB = S // TBS
CPB = TBS // 128
NH = 16
HD = 64
STRIDES = [1, 1, 2, 2, 2, 2, 4, 4, 4, 4, 8, 8, 8, 8, 16, 16]
EPS = 1e-5


class Sem:
    def __init__(self, ctx, nc, name):
        self.h = ctx.enter_context(nc.semaphore(name))
        self.n = 0

    def inc(self, instr, k=1):
        instr.then_inc(self.h, k)
        self.n += k
        return self.n

    def rec(self):
        return (self, self.n)


def _wait(e, recs):
    seen = {}
    for r in recs:
        if r is None:
            continue
        s, v = r
        if v > 0 and seen.get(id(s), -1) < v:
            seen[id(s)] = v
            e.wait_ge(s.h, v)


class Builder:
    def __init__(self, debug=False, stop_after="full", n_tb=NTB, n_exp=NE,
                 do_router=True, do_ep=True):
        self.debug = debug
        self.stop_after = stop_after
        self.n_tb = n_tb
        self.n_exp = n_exp
        self.do_router = do_router
        self.do_ep = do_ep
        self.nc = bass.Bass("TRN2", target_bir_lowering=False)
        self._uid = 0

    def uid(self, p):
        self._uid += 1
        return f"{p}{self._uid}"

    def T(self, fn): self.blk.tensor(fn)

    def _fence(self, sem, fn):
        n = sem.n
        def g(e):
            if n > 0:
                e.wait_ge(sem.h, n)
            fn(e)
        return g

    def A(self, fn, dr=True):
        self.blk.scalar(self._fence(self.sA, fn) if dr else fn)

    def V(self, fn, dr=True):
        self.blk.vector(self._fence(self.sV, fn) if dr else fn)

    def P(self, fn, dr=True):
        self.blk.gpsimd(self._fence(self.sP, fn) if dr else fn)

    def Sy(self, fn): self.blk.sync(fn)

    def barrier(self):
        sems = [self.sT, self.sA, self.sV, self.sP, self.dS, self.sIc,
                self.sWo] + self.sW1 + self.sW1c + self.sW2 + self.sW2c + \
            self.sQ + self.sW1x + self.sW2x
        recs = [s.rec() for s in sems]
        for em in (self.T, self.A, self.V, self.P, self.Sy):
            em(lambda e: _wait(e, recs))

    def build(self):
        nc = self.nc
        d = {}
        d["x"] = nc.dram_tensor("x", [S, D], F32, kind="ExternalInput").ap()
        d["w1s"] = nc.dram_tensor("w1s", [NE, FCN, 128, DCN, 128], F32, kind="ExternalInput").ap()
        d["w2s"] = nc.dram_tensor("w2s", [NE, DCN, 128, FCN, 128], F32, kind="ExternalInput").ap()
        d["w1a"] = nc.dram_tensor("w1a", [FCN, 128, DCN, 128], F32, kind="ExternalInput").ap()
        d["w2a"] = nc.dram_tensor("w2a", [DCN, 128, FCN, 128], F32, kind="ExternalInput").ap()
        d["wqkv"] = nc.dram_tensor("wqkv", [24, 128, DCN, 128], F32, kind="ExternalInput").ap()
        d["wout"] = nc.dram_tensor("wout", [D, D], F32, kind="ExternalInput").ap()
        d["wr"] = nc.dram_tensor("wr", [128, DCN, NE], F32, kind="ExternalInput").ap()
        d["b1c"] = nc.dram_tensor("b1c", [128, NE, FCN], F32, kind="ExternalInput").ap()
        d["b2c"] = nc.dram_tensor("b2c", [128, NE, DCN], F32, kind="ExternalInput").ap()
        d["b1ac"] = nc.dram_tensor("b1ac", [128, FCN], F32, kind="ExternalInput").ap()
        d["b2ac"] = nc.dram_tensor("b2ac", [128, DCN], F32, kind="ExternalInput").ap()
        d["bqkvc"] = nc.dram_tensor("bqkvc", [128, 24], F32, kind="ExternalInput").ap()
        d["br_b"] = nc.dram_tensor("br_b", [128, NE], F32, kind="ExternalInput").ap()
        d["g_f"] = nc.dram_tensor("g_f", [128, D], F32, kind="ExternalInput").ap()
        d["be_f"] = nc.dram_tensor("be_f", [128, D], F32, kind="ExternalInput").ap()
        d["g_a"] = nc.dram_tensor("g_a", [128, D], F32, kind="ExternalInput").ap()
        d["be_a"] = nc.dram_tensor("be_a", [128, D], F32, kind="ExternalInput").ap()
        d["bout_r"] = nc.dram_tensor("bout_r", [1, D], F32, kind="ExternalInput").ap()
        d["identf"] = nc.dram_tensor("identf", [128, 128], F32, kind="ExternalInput").ap()
        d["ones1"] = nc.dram_tensor("ones1", [1, 128], F32, kind="ExternalInput").ap()
        d["y"] = nc.dram_tensor("y", [S, D], F32, kind="ExternalOutput").ap()
        d["w1b"] = nc.dram_tensor("w1b", [NE, FCN, 128, DCN, 128], BF16, kind="Internal").ap()
        d["w2b"] = nc.dram_tensor("w2b", [NE, DCN, 128, FCN, 128], BF16, kind="Internal").ap()
        d["w1ab"] = nc.dram_tensor("w1ab", [FCN, 128, DCN, 128], BF16, kind="Internal").ap()
        d["w2ab"] = nc.dram_tensor("w2ab", [DCN, 128, FCN, 128], BF16, kind="Internal").ap()
        if self.debug:
            d["dbg_x1"] = nc.dram_tensor("dbg_x1", [S, D], F32, kind="ExternalOutput").ap()
            d["dbg_x2"] = nc.dram_tensor("dbg_x2", [S, D], F32, kind="ExternalOutput").ap()
            d["dbg_rw"] = nc.dram_tensor("dbg_rw", [128, CPB, NE], F32, kind="ExternalOutput").ap()
            d["dbg_acc"] = nc.dram_tensor("dbg_acc", [128, DCN, TBS], F32, kind="ExternalOutput").ap()
            d["dbg_we"] = nc.dram_tensor("dbg_we", [128, TBS], F32, kind="ExternalOutput").ap()
            d["dbg_stat"] = nc.dram_tensor("dbg_stat", [128, 16, 8], F32, kind="ExternalOutput").ap()
            d["dbg_hc"] = nc.dram_tensor("dbg_hc", [128, DCN, S], BF16, kind="ExternalOutput").ap()
            d["dbg_a"] = nc.dram_tensor("dbg_a", [128, S], BF16, kind="ExternalOutput").ap()
            d["dbg_qkv"] = nc.dram_tensor("dbg_qkv", [128, 3, S], BF16, kind="ExternalOutput").ap()
            d["dbg_at"] = nc.dram_tensor("dbg_at", [128, 16, 128], BF16, kind="ExternalOutput").ap()
            d["dbg_vs"] = nc.dram_tensor("dbg_vs", [128, 16, HD], BF16, kind="ExternalOutput").ap()
            d["dbg_rlog"] = nc.dram_tensor("dbg_rlog", [128, CPB, NE], F32, kind="ExternalOutput").ap()
        self.d = d

        with ExitStack() as ctx:
            self.ctx = ctx
            self._alloc_outer()
            self.blk = ctx.enter_context(nc.Block())
            self._emit()
        return nc

    def _alloc_outer(self):
        nc, ctx = self.nc, self.ctx
        sb = lambda n, sh, dt: ctx.enter_context(nc.sbuf_tensor(n, sh, dt))
        self.x_tm = sb("s_x", [128, TCN, D], F32)
        self.gfh = sb("s_gfh", [128, D], F32)
        self.bfh = sb("s_bfh", [128, D], F32)
        self.gab = sb("s_gab", [128, D], F32)
        self.bab = sb("s_bab", [128, D], F32)
        self.wr_sb = sb("s_wr", [128, DCN, NE], F32)
        self.b1c = sb("s_b1c", [128, NE, FCN], F32)
        self.b2c = sb("s_b2c", [128, NE, DCN], F32)
        self.b1ac = sb("s_b1ac", [128, FCN], F32)
        self.b2ac = sb("s_b2ac", [128, DCN], F32)
        self.bqkvs = sb("s_bqkvs", [128, 24], F32)
        self.br_b = sb("s_brb", [128, NE], F32)
        self.bout_bf = sb("s_boutbf", [1, D], BF16)
        self.identf = sb("s_idf", [128, 128], F32)
        self.identb = sb("s_idb", [128, 128], BF16)
        self.ones_f = sb("s_onesf", [1, 128], F32)
        self.ones_b = sb("s_onesb", [1, 128], BF16)
        self.stat = sb("s_stat", [128, 16, 8], F32)
        self.rstd1 = sb("s_rstd1", [128, TCN], F32)
        self.nmr1 = sb("s_nmr1", [128, TCN], F32)
        self.rlog = sb("s_rlog", [128, CPB, NE], F32)
        self.rw = sb("s_rw", [128, CPB, NE], F32)
        self.req = sb("s_req", [128, 2, NE], F32)
        self.w_row = sb("s_wrow", [1, TBS], F32)
        self.sm = sb("s_sm", [128, 8, 4], F32)
        self.xa_scr = sb("s_xa", [128, D], F32)
        self.t1_scr = sb("s_t1", [128, D], F32)
        self.z_scr = sb("s_z", [128, D], F32)
        self.ps4 = ctx.enter_context(nc.psum_tensor("p_ps4", [128, 2048], F32))
        self.ps2 = ctx.enter_context(nc.psum_tensor("p_ps2", [128, 1024], F32))
        self.psb = ctx.enter_context(nc.psum_tensor("p_psb", [128, 1024], BF16))
        self.pss = ctx.enter_context(nc.psum_tensor("p_pss", [128, 512], F32))
        self.sT = Sem(ctx, nc, "sT")
        self.sA = Sem(ctx, nc, "sA")
        self.sV = Sem(ctx, nc, "sV")
        self.sP = Sem(ctx, nc, "sP")
        self.dS = Sem(ctx, nc, "dS")
        self.sIc = Sem(ctx, nc, "sIc")
        self.sW1 = [Sem(ctx, nc, f"sW1_{i}") for i in range(6)]
        self.sW1x = [Sem(ctx, nc, f"sW1x_{i}") for i in range(6)]
        self.sW2x = [Sem(ctx, nc, f"sW2x_{i}") for i in range(4)]
        self.sW1c = [Sem(ctx, nc, f"sW1c_{i}") for i in range(6)]
        self.sW2 = [Sem(ctx, nc, f"sW2_{i}") for i in range(4)]
        self.sW2c = [Sem(ctx, nc, f"sW2c_{i}") for i in range(4)]
        self.sQ = [Sem(ctx, nc, f"sQ_{i}") for i in range(2)]
        self.sWo = Sem(ctx, nc, "sWo")

    def _emit(self):
        d = self.d
        dS, sIc, sA = self.dS, self.sIc, self.sA

        def init_dmas(e):
            x_r = d["x"].rearrange("(tc p) dd -> p tc dd", p=128)
            for i in range(4):
                dS.inc(e.dma_start(out=self.x_tm[:, 4 * i:4 * i + 4, :],
                                   in_=x_r[:, 4 * i:4 * i + 4, :]), 16)
            for dst, src in ((self.wr_sb[:], d["wr"][:, :, :]),
                             (self.b1c[:], d["b1c"][:, :, :]),
                             (self.b2c[:], d["b2c"][:, :, :]),
                             (self.b1ac[:], d["b1ac"][:, :]),
                             (self.b2ac[:], d["b2ac"][:, :]),
                             (self.bqkvs[:], d["bqkvc"][:, :]),
                             (self.br_b[:], d["br_b"][:, :]),
                             (self.identf[:], d["identf"][:, :]),
                             (self.ones_f[:], d["ones1"][:, :]),
                             (self.gab[:], d["g_a"][:, :]),
                             (self.bab[:], d["be_a"][:, :]),
                             (self.gfh[:], d["g_f"][:, :]),
                             (self.bfh[:], d["be_f"][:, :])):
                dS.inc(e.dma_start(out=dst, in_=src), 16)
        self.Sy(init_dmas)
        init_dS = dS.rec()

        def init_casts(e):
            sIc.inc(e.dma_start(out=self.identb[:], in_=d["identf"][:, :]), 16)
            sIc.inc(e.dma_start(out=self.ones_b[:], in_=d["ones1"][:, :]), 16)
            sIc.inc(e.dma_start(out=self.bout_bf[:], in_=d["bout_r"][:, :]), 16)
        self.P(init_casts, dr=False)

        def init_zero(e):
            e.memset(self.stat[:], 0.0)
            e.memset(self.sm[:], 0.0)
            e.memset(self.rlog[:], 0.0)
            e.memset(self.req[:], 0.0)
            e.memset(self.rw[:], 0.0)
            self.sV.inc(e.memset(self.w_row[:], 0.0))
        self.V(init_zero, dr=False)
        init_zero_r = self.sV.rec()

        def init_scale(e):
            _wait(e, [init_dS])
            e.activation(self.xa_scr[:], self.gfh[:], AF.Identity, scale=0.5)
            e.drain()
            e.activation(self.gfh[:], self.xa_scr[:], AF.Identity)
            e.drain()
            e.activation(self.xa_scr[:], self.bfh[:], AF.Identity, scale=0.5)
            e.drain()
            e.activation(self.bfh[:], self.xa_scr[:], AF.Identity)
            e.drain()
            e.activation(self.xa_scr[:, 0:8], self.bqkvs[:, 0:8], AF.Identity, scale=0.125)
            e.drain()
            sA.inc(e.activation(self.bqkvs[:, 0:8], self.xa_scr[:, 0:8], AF.Identity))
        self.A(init_scale)
        self.x_ready = [init_dS, sA.rec()]
        self.w1_cached = {}
        self.w2_cached = {}
        self.w1a_cached = {}
        self.w2a_cached = {}

        self.barrier()
        if self.stop_after != "init":
            self._moe_call(0)
            self.barrier()
            if self.debug:
                self._dump_x("dbg_x1")
                self.barrier()
        if self.stop_after not in ("init", "moe0"):
            self._attn_block()
            self.barrier()
            if self.debug:
                self._dump_x("dbg_x2")
                self.barrier()
        if self.stop_after not in ("init", "moe0", "attn"):
            self._moe_call(1)
            self.barrier()

        def out_dma(e):
            y_r = d["y"].rearrange("(tc p) dd -> p tc dd", p=128)
            for i in range(4):
                self.dS.inc(e.dma_start(out=y_r[:, 4 * i:4 * i + 4, :],
                                        in_=self.x_tm[:, 4 * i:4 * i + 4, :]), 16)
            e.wait_ge(self.dS.h, self.dS.n)
        self.Sy(out_dma)

    def _dump_x(self, name):
        def f(e):
            y_r = self.d[name].rearrange("(tc p) dd -> p tc dd", p=128)
            for i in range(4):
                self.dS.inc(e.dma_start(out=y_r[:, 4 * i:4 * i + 4, :],
                                        in_=self.x_tm[:, 4 * i:4 * i + 4, :]), 16)
        self.Sy(f)

    def _ln_stats_chunk(self, src_ap, rstd_col, nmr_col, dep_recs, sq_scr):
        sA, sV = self.sA, self.sV
        ssq = self.stat[:, 0:1, 6]
        ssm = self.stat[:, 1:2, 6]
        mu = self.stat[:, 2:3, 6]
        var = self.stat[:, 0:1, 7]
        t2 = self.stat[:, 1:2, 7]
        lnv = self.stat[:, 2:3, 7]

        def v1(e):
            _wait(e, dep_recs)
            e.scalar_tensor_tensor(sq_scr, src_ap, 1.0, src_ap,
                                   op0=ALU.mult, op1=ALU.mult, accum_out=ssq)
            e.reduce_sum(ssm, src_ap, axis=AX.X)
            e.drain()
            e.tensor_scalar(mu, ssm, 1.0 / D, None, op0=ALU.mult)
            e.tensor_scalar(var, ssq, 1.0 / D, None, op0=ALU.mult)
            e.drain()
            e.tensor_tensor(t2, mu, mu, op=ALU.mult)
            e.drain()
            sV.inc(e.scalar_tensor_tensor(var, var, EPS, t2,
                                          op0=ALU.add, op1=ALU.subtract))
        self.V(v1)
        v1r = sV.rec()

        def a2(e):
            _wait(e, [v1r])
            e.activation(lnv, var, AF.Ln)
            e.drain()
            sA.inc(e.activation(rstd_col, lnv, AF.Exp, scale=-0.5))
        self.A(a2)
        a2r = sA.rec()

        def v2(e):
            _wait(e, [a2r])
            sV.inc(e.scalar_tensor_tensor(nmr_col, mu, -1.0, rstd_col,
                                          op0=ALU.mult, op1=ALU.mult))
        self.V(v2)
        return sV.rec()

    def _mlp_block(self, lnT, hbuf, w1sl, w2sl, nw1buf, nw2buf,
                   w1_src, w1_cache, w1_cached, b1col,
                   w2_src, w2_cache, w2_cached, b2fn,
                   first, lnT_ready, state):
        sT, sA, sV = self.sT, self.sA, self.sV
        sW1, sW1c = self.sW1, self.sW1c
        sW2, sW2c = self.sW2, self.sW2c

        for fc in range(FCN):
            buf = fc % nw1buf
            cache_rec = None
            if first:
                def wd(e, fc=fc, buf=buf):
                    _wait(e, state["w1buf_free"][buf])
                    self.sW1x[buf].inc(e.dma_start(out=w1sl[:, buf, :, :], in_=w1_src(fc)), 16)
                self.P(wd, dr=False)
                dmar = self.sW1x[buf].rec()

                def wc(e, fc=fc, buf=buf, dmar=dmar):
                    _wait(e, [dmar])
                    sW1c[buf].inc(e.dma_start(out=w1_cache(fc), in_=w1sl[:, buf, :, :]), 16)
                self.Sy(wc)
                cache_rec = sW1c[buf].rec()
            else:
                def wd(e, fc=fc, buf=buf):
                    _wait(e, state["w1buf_free"][buf] + w1_cached["all"])
                    sW1[buf].inc(e.dma_start(out=w1sl[:, buf, :, :], in_=w1_cache(fc)), 16)
                self.Sy(wd)
                dmar = sW1[buf].rec()

            hb = fc % 2
            psum = self.ps4[:, hb * 512:(hb + 1) * 512]

            def w1mm(e, fc=fc, buf=buf, psum=psum, dmar=dmar):
                _wait(e, [dmar, lnT_ready if fc < 2 else None,
                          state["h_evac"][fc - 2]])
                for dc in range(DCN):
                    mm = e.matmul(psum, w1sl[:, buf, dc, :], lnT[:, dc, :],
                             start=(dc == 0), stop=(dc == DCN - 1))
                sT.inc(mm)
            self.T(w1mm)
            state["pe_w1"][fc] = sT.rec()
            state["w1buf_free"][buf] = [state["pe_w1"][fc]] + \
                ([cache_rec] if cache_rec else [])

            def hev(e, fc=fc, psum=psum, gr=state["pe_w1"][fc]):
                _wait(e, [gr])
                sV.inc(e.tensor_scalar(hbuf[:, fc, :], psum, b1col(fc), 0.0,
                                       op0=ALU.add, op1=ALU.max))
            self.V(hev)
            state["h_evac"][fc] = sV.rec()

        if first:
            w1_cached["all"] = [s.rec() for s in sW1c]
        h_all = state["h_evac"][FCN - 1]
        for dc in range(DCN):
            buf = dc % nw2buf
            cache_rec2 = None
            if first:
                def wd2(e, dc=dc, buf=buf):
                    _wait(e, state["w2buf_free"][buf])
                    self.sW2x[buf].inc(e.dma_start(out=w2sl[:, buf, :, :], in_=w2_src(dc)), 16)
                self.P(wd2, dr=False)
                dmar2 = self.sW2x[buf].rec()

                def wc2(e, dc=dc, buf=buf, dmar2=dmar2):
                    _wait(e, [dmar2])
                    sW2c[buf].inc(e.dma_start(out=w2_cache(dc), in_=w2sl[:, buf, :, :]), 16)
                self.Sy(wc2)
                cache_rec2 = sW2c[buf].rec()
            else:
                def wd2(e, dc=dc, buf=buf):
                    _wait(e, state["w2buf_free"][buf] + w2_cached["all"])
                    sW2[buf].inc(e.dma_start(out=w2sl[:, buf, :, :], in_=w2_cache(dc)), 16)
                self.A(wd2, dr=False)
                dmar2 = sW2[buf].rec()

            ob = dc % 2
            psum = self.ps4[:, 1024 + ob * 512:1024 + (ob + 1) * 512]

            def w2mm(e, dc=dc, buf=buf, psum=psum, dmar2=dmar2,
                     pfree=state["o_evac"][dc - 2]):
                _wait(e, [dmar2, h_all if dc < 2 else None, pfree])
                for fc in range(FCN):
                    mm = e.matmul(psum, w2sl[:, buf, fc, :], hbuf[:, fc, :],
                             start=(fc == 0), stop=(fc == FCN - 1))
                sT.inc(mm)
            self.T(w2mm)
            state["pe_w2"][dc] = sT.rec()
            state["w2buf_free"][buf] = [state["pe_w2"][dc]] + \
                ([cache_rec2] if cache_rec2 else [])

            b2fn(dc, psum, state["pe_w2"][dc])
        if first:
            w2_cached["all"] = [s.rec() for s in sW2c]

    def _moe_call(self, call):
        nc, d = self.nc, self.d
        sT, sA, sV, sP = self.sT, self.sA, self.sV, self.sP
        u = self.uid
        with ExitStack() as ph:
            sb = lambda n, sh, dt: ph.enter_context(nc.sbuf_tensor(u(n), sh, dt))
            lnT = sb("m_lnT", [128, DCN, TBS], BF16)
            u16 = sb("m_u16", [128, DCN, TBS], F32)
            hbuf = sb("m_h", [128, FCN, TBS], BF16)
            w1sl = sb("m_w1sl", [128, 5, DCN, 128], BF16)
            w2sl = sb("m_w2sl", [128, 4, FCN, 128], BF16)
            w_e = sb("m_we", [128, 2, TBS], F32)
            o_tmp = sb("m_ot", [128, 2, TBS], F32)

            state = dict(
                h_evac=[None] * FCN, pe_w1=[None] * FCN,
                o_evac=[None] * DCN, pe_w2=[None] * DCN,
                w1buf_free=[[] for _ in range(5)],
                w2buf_free=[[] for _ in range(4)],
            )
            o_add = [None] * DCN
            we_ready = [None, None]
            we_free = [None, None]

            for tb in range(self.n_tb):
                chunks = [tb * CPB + i for i in range(CPB)]
                lnF = u16
                lnF_ready = None
                prev_tr1 = getattr(self, "_moe_prev_tr1", None)
                for i, tc in enumerate(chunks):
                    rstd = self.stat[:, tc:tc + 1, 0]
                    nmr = self.stat[:, tc:tc + 1, 1]
                    dep = (self.x_ready if call == 0 else []) + \
                        [getattr(self, "_prev_epg", None)]
                    st_r = self._ln_stats_chunk(self.x_tm[:, tc, :], rstd, nmr,
                                                dep, self.t1_scr[:])

                    def ap1(e, tc=tc, rstd=rstd, nmr=nmr, st_r=st_r, pt=prev_tr1,
                            xu=getattr(self, "_prev_xupd", None)):
                        _wait(e, [st_r, pt, xu])
                        sV.inc(e.tensor_scalar(self.xa_scr[:], self.x_tm[:, tc, :],
                                               rstd, nmr, op0=ALU.mult, op1=ALU.add))
                    self.V(ap1)
                    ap1r = sV.rec()

                    def tr1(e, ap1r=ap1r, prev=lnF_ready):
                        _wait(e, [ap1r, prev])
                        for g in range(2):
                            for j in range(4):
                                dcx = g * 4 + j
                                mm = e.matmul(self.ps2[:, g * 512 + j * 128:g * 512 + (j + 1) * 128],
                                         self.xa_scr[:, dcx * 128:(dcx + 1) * 128],
                                         self.identf[:], start=(j == 0), stop=(j == 3),
                                         is_transpose=True)
                        sT.inc(mm)
                    self.T(tr1)
                    tr1r = sT.rec()
                    prev_tr1 = tr1r
                    self._moe_prev_tr1 = tr1r

                    def ev1(e, i=i, tr1r=tr1r):
                        _wait(e, [tr1r])
                        for g in range(2):
                            sV.inc(e.tensor_copy(
                                lnF.ap()[:, g * 4:(g + 1) * 4, i * 128:(i + 1) * 128],
                                self.ps2.ap()[:, g * 512:(g + 1) * 512]
                                .rearrange("p (a b) -> p a b", a=4)))
                    self.V(ev1)
                    lnF_ready = sV.rec()

                def cast_ln(e, lnF_ready=lnF_ready):
                    _wait(e, [lnF_ready])
                    sA.inc(e.copy(lnT[:], lnF[:]))
                self.A(cast_ln)
                lnT_ready = sA.rec()

                rmm_r = None
                router_r = None
                if not self.do_router:
                    pass
                def rmm(e, lnF_ready=lnF_ready):
                    _wait(e, [lnF_ready])
                    for i in range(CPB):
                        for dc in range(DCN):
                            mm = e.matmul(self.pss[:, i * 8:(i + 1) * 8],
                                     lnF[:, dc, i * 128:(i + 1) * 128],
                                     self.wr_sb[:, dc, :],
                                     start=(i == 0 and dc == 0),
                                     stop=(i == CPB - 1 and dc == DCN - 1))
                    sT.inc(mm)
                if self.do_router:
                    self.T(rmm)
                    rmm_r = sT.rec()

                def rsoft(e, rmm_r=rmm_r):
                    _wait(e, [rmm_r])
                    for i in range(CPB):
                        e.tensor_tensor(self.rlog[:, i, :], self.pss[:, i * 8:(i + 1) * 8],
                                        self.br_b[:], op=ALU.add)
                    e.drain()
                    m1 = self.stat[:, 0:CPB, 2]
                    e.reduce_max(m1, self.rlog[:], axis=AX.X)
                    e.drain()
                    for i in range(CPB):
                        e.tensor_scalar(self.req[:, 0, :], self.rlog[:, i, :],
                                        self.stat[:, i:i + 1, 2], None, op0=ALU.is_equal)
                        e.drain()
                        e.scalar_tensor_tensor(self.req[:, 1, :], self.req[:, 0, :], -1e30,
                                               self.rlog[:, i, :], op0=ALU.mult, op1=ALU.add)
                        e.drain()
                        e.reduce_max(self.stat[:, i:i + 1, 3], self.req[:, 1, :], axis=AX.X)
                        e.drain()
                    sV.inc(e.tensor_tensor(self.stat[:, 0:CPB, 4],
                                           self.stat[:, 0:CPB, 3],
                                           self.stat[:, 0:CPB, 2], op=ALU.subtract))
                if self.do_router:
                    self.V(rsoft)
                rsoft_r = sV.rec() if self.do_router else None

                def rtanh(e, rsoft_r=rsoft_r):
                    _wait(e, [rsoft_r])
                    sA.inc(e.activation(self.stat[:, 0:CPB, 5], self.stat[:, 0:CPB, 4],
                                        AF.Tanh, scale=0.5))
                if self.do_router:
                    self.A(rtanh)
                rtanh_r = sA.rec() if self.do_router else None

                def rfin(e, rtanh_r=rtanh_r):
                    _wait(e, [rtanh_r])
                    w2v = self.stat[:, 0:CPB, 4]
                    w1v = self.stat[:, 0:CPB, 5]
                    e.tensor_scalar(w2v, self.stat[:, 0:CPB, 5], 0.5, 0.5,
                                    op0=ALU.mult, op1=ALU.add)
                    e.drain()
                    e.tensor_scalar(w1v, w2v, -1.0, 1.0, op0=ALU.mult, op1=ALU.add)
                    e.drain()
                    last = None
                    for i in range(CPB):
                        e.tensor_scalar(self.req[:, 0, :], self.rlog[:, i, :],
                                        self.stat[:, i:i + 1, 2], None, op0=ALU.is_equal)
                        e.tensor_scalar(self.req[:, 1, :], self.rlog[:, i, :],
                                        self.stat[:, i:i + 1, 3], None, op0=ALU.is_equal)
                        e.drain()
                        e.tensor_scalar(self.req[:, 0, :], self.req[:, 0, :],
                                        self.stat[:, i:i + 1, 5], None, op0=ALU.mult)
                        e.drain()
                        last = e.scalar_tensor_tensor(self.rw[:, i, :], self.req[:, 1, :],
                                                      self.stat[:, i:i + 1, 4],
                                                      self.req[:, 0, :],
                                                      op0=ALU.mult, op1=ALU.add)
                        e.drain()
                    sV.inc(last)
                if self.do_router:
                    self.V(rfin)
                    router_r = sV.rec()

                u16_free = getattr(self, "_u16_ept_last", None)
                o_add = [None] * DCN

                first = (call == 0 and tb == 0)
                for ei in range(self.n_exp):
                    web = ei % 2

                    def wet(e, ei=ei, rr=router_r, wf=we_free[web],
                            wprev=we_ready[1 - web]):
                        _wait(e, [rr, wf, wprev])
                        for i in range(CPB):
                            mm = e.matmul(self.pss.ap()[0:1, i * 128:(i + 1) * 128],
                                     self.rw[:, i, ei:ei + 1], self.identf[:],
                                     start=(i == 0), stop=(i == CPB - 1),
                                     is_transpose=True)
                        sT.inc(mm)
                    self.T(wet)
                    wet_r = sT.rec()

                    def wev(e, wet_r=wet_r):
                        _wait(e, [wet_r])
                        sV.inc(e.tensor_copy(self.w_row[:], self.pss.ap()[0:1, 0:TBS]))
                    self.V(wev)
                    wev_r = sV.rec()

                    def webm(e, wev_r=wev_r):
                        _wait(e, [wev_r])
                        sT.inc(e.matmul(self.pss[:, 0:TBS], self.ones_f[:], self.w_row[:],
                                        start=True, stop=True))
                    self.T(webm)
                    webm_r = sT.rec()

                    def webe(e, web=web, webm_r=webm_r):
                        _wait(e, [webm_r])
                        sV.inc(e.tensor_copy(w_e[:, web, :], self.pss[:, 0:TBS]))
                    self.V(webe)
                    we_ready[web] = sV.rec()

                    def b2fn(dc, psum, group_rec, ei=ei, web=web):
                        def oev(e, dc=dc, psum=psum, gr=group_rec,
                                wer=we_ready[web], pf=o_add[(dc - 2) % DCN]):
                            _wait(e, [gr, wer, pf])
                            sV.inc(e.scalar_tensor_tensor(
                                o_tmp[:, dc % 2, :], psum, self.b2c[:, ei, dc:dc + 1],
                                w_e[:, web, :], op0=ALU.add, op1=ALU.mult))
                        self.V(oev)
                        state["o_evac"][dc] = sV.rec()
                        we_free[web] = sV.rec()

                        if ei == 0:
                            def oad(e, dc=dc, er=state["o_evac"][dc],
                                    uf=u16_free):
                                _wait(e, [er, uf])
                                sV.inc(e.tensor_copy(u16[:, dc, :],
                                                     o_tmp[:, dc % 2, :]))
                        else:
                            def oad(e, dc=dc, er=state["o_evac"][dc]):
                                _wait(e, [er])
                                sV.inc(e.tensor_tensor(u16[:, dc, :],
                                                       u16[:, dc, :],
                                                       o_tmp[:, dc % 2, :],
                                                       op=ALU.add))
                        self.V(oad)
                        o_add[dc] = sV.rec()

                    self._mlp_block(
                        lnT, hbuf, w1sl, w2sl, 5, 4,
                        lambda fc, ei=ei: d["w1s"][ei, fc, :, :, :],
                        lambda fc, ei=ei: d["w1b"][ei, fc, :, :, :],
                        self._kcache("w1", ei),
                        lambda fc, ei=ei: self.b1c[:, ei, fc:fc + 1],
                        lambda dc, ei=ei: d["w2s"][ei, dc, :, :, :],
                        lambda dc, ei=ei: d["w2b"][ei, dc, :, :, :],
                        self._kcache("w2", ei),
                        b2fn, first, lnT_ready, state)

                dbg_acc_deps = [o_add[dc] for dc in range(DCN)] + [router_r, we_ready[0]]

                if not self.do_ep:
                    continue
                acc_done = [o_add[dc] for dc in range(DCN)]
                ep_ev = None
                x_upd_prev = None
                epg_prev = None
                for i, tc in enumerate(chunks):
                    def ept(e, i=i, deps=acc_done, prev=ep_ev):
                        _wait(e, deps + [prev])
                        for g in range(2):
                            for j in range(4):
                                dcx = g * 4 + j
                                mm = e.matmul(self.ps2[:, g * 512 + j * 128:g * 512 + (j + 1) * 128],
                                         u16[:, dcx, i * 128:(i + 1) * 128],
                                         self.identf[:], start=(j == 0), stop=(j == 3),
                                         is_transpose=True)
                        sT.inc(mm)
                    self.T(ept)
                    eptr = sT.rec()
                    self._u16_ept_last = eptr

                    def epv(e, eptr=eptr, xup=x_upd_prev):
                        _wait(e, [eptr, xup])
                        sV.inc(e.tensor_copy(self.t1_scr[:], self.ps2[:]))
                    self.V(epv)
                    ep_ev = sV.rec()

                    def epa(e, tc=tc, pg=epg_prev, xu=x_upd_prev):
                        _wait(e, [pg, xu])
                        rstd = self.stat[:, tc:tc + 1, 0]
                        nmr = self.stat[:, tc:tc + 1, 1]
                        sV.inc(e.tensor_scalar(self.xa_scr[:], self.x_tm[:, tc, :],
                                               rstd, nmr, op0=ALU.mult, op1=ALU.add))
                    self.V(epa)
                    epar = sV.rec()

                    def epz(e, epar=epar, evr=ep_ev):
                        _wait(e, [epar, evr])
                        sP.inc(e.tensor_tensor(self.z_scr[:], self.xa_scr[:],
                                               self.t1_scr[:], op=ALU.add))
                    self.P(epz)
                    z_r = sP.rec()

                    rstdz = self.stat[:, 8 + i:9 + i, 4]
                    nmrz = self.stat[:, 8 + i:9 + i, 5]
                    st_r = self._ln_stats_chunk(self.z_scr[:], rstdz, nmrz, [z_r],
                                                self.t1_scr[:])

                    def epn(e, st_r=st_r, rstdz=rstdz, nmrz=nmrz):
                        _wait(e, [st_r])
                        sV.inc(e.tensor_scalar(self.t1_scr[:], self.z_scr[:],
                                               rstdz, nmrz, op0=ALU.mult, op1=ALU.add))
                    self.V(epn)
                    epnr = sV.rec()

                    def epg(e, epnr=epnr):
                        _wait(e, [epnr])
                        sV.inc(e.tensor_tensor(self.xa_scr[:], self.t1_scr[:],
                                               self.gfh[:], op=ALU.mult))
                    self.V(epg)
                    epgr = sV.rec()
                    epg_prev = epgr
                    self._prev_epg = epgr

                    def epx(e, tc=tc, epgr=epgr):
                        _wait(e, [epgr])
                        e.tensor_tensor(self.x_tm[:, tc, :], self.x_tm[:, tc, :],
                                        self.xa_scr[:], op=ALU.add)
                        e.drain()
                        sP.inc(e.tensor_tensor(self.x_tm[:, tc, :], self.x_tm[:, tc, :],
                                               self.bfh[:], op=ALU.add))
                    self.P(epx)
                    x_upd_prev = sP.rec()
                    self._prev_xupd = x_upd_prev

                if self.debug and call == 0 and tb == 0 and self.n_tb == 1:
                    def dbg_dump(e, deps=dbg_acc_deps + [x_upd_prev, sA.rec(),
                                                         sV.rec(), sT.rec()]):
                        _wait(e, deps)
                        self.dS.inc(e.dma_start(out=d["dbg_rw"][:, :, :], in_=self.rw[:]), 16)
                        self.dS.inc(e.dma_start(out=d["dbg_acc"][:, :, :], in_=u16[:]), 16)
                        self.dS.inc(e.dma_start(out=d["dbg_we"][:, :], in_=w_e[:, 0, :]), 16)
                        self.dS.inc(e.dma_start(out=d["dbg_stat"][:, :, :], in_=self.stat[:]), 16)
                        self.dS.inc(e.dma_start(out=d["dbg_rlog"][:, :, :], in_=self.rlog[:]), 16)
                    self.Sy(dbg_dump)

    def _kcache(self, kind, ei):
        base = self.w1_cached if kind == "w1" else self.w2_cached
        return base.setdefault(ei, {})

    def _attn_block(self):
        nc, d = self.nc, self.d
        sT, sA, sV, sP = self.sT, self.sA, self.sV, self.sP
        dS, sQ, sWo = self.dS, self.sQ, self.sWo
        u = self.uid
        with ExitStack() as ph:
            sbA = lambda n, sh, dt: ph.enter_context(nc.sbuf_tensor(u(n), sh, dt))
            hcT = sbA("a_hcT", [128, DCN, S], BF16)
            h1v = hcT.ap().rearrange("p dc (tc m) -> p dc tc m", m=128)
            h1_written = [None] * TCN
            hc_last = None

            with ExitStack() as ph2:
                sb2 = lambda n, sh, dt: ph2.enter_context(nc.sbuf_tensor(u(n), sh, dt))
                ln1T = sb2("a_ln1T", [128, DCN, S], BF16)
                qkvT = sb2("a_qkvT", [128, 3, S], BF16)
                qkvw = sb2("a_qkvw", [128, 2, 3, DCN, 128], BF16)
                abuf = sb2("a_abuf", [128, 2, S], BF16)
                atb = sb2("a_atb", [128, 16, 128], BF16)
                vst = sb2("a_vst", [128, 16, HD], BF16)

                ln_ev = None
                prev_tr1 = getattr(self, "_moe_prev_tr1", None)
                for tc in range(TCN):
                    st_r = self._ln_stats_chunk(self.x_tm[:, tc, :],
                                                self.rstd1[:, tc:tc + 1],
                                                self.nmr1[:, tc:tc + 1], [],
                                                self.t1_scr[:])

                    def ap1(e, tc=tc, st_r=st_r, pt=prev_tr1):
                        _wait(e, [st_r, pt])
                        sV.inc(e.tensor_scalar(self.xa_scr[:], self.x_tm[:, tc, :],
                                               self.rstd1[:, tc:tc + 1],
                                               self.nmr1[:, tc:tc + 1],
                                               op0=ALU.mult, op1=ALU.add))
                    self.V(ap1)
                    ap1r = sV.rec()

                    def tr1(e, ap1r=ap1r, prev=ln_ev):
                        _wait(e, [ap1r, prev])
                        for g in range(2):
                            for j in range(4):
                                dcx = g * 4 + j
                                mm = e.matmul(self.ps2[:, g * 512 + j * 128:g * 512 + (j + 1) * 128],
                                         self.xa_scr[:, dcx * 128:(dcx + 1) * 128],
                                         self.identf[:], start=(j == 0), stop=(j == 3),
                                         is_transpose=True)
                        sT.inc(mm)
                    self.T(tr1)
                    tr1r = sT.rec()
                    prev_tr1 = tr1r
                    self._moe_prev_tr1 = tr1r

                    def ev1(e, tc=tc, tr1r=tr1r):
                        _wait(e, [tr1r])
                        for g in range(2):
                            sV.inc(e.tensor_copy(
                                ln1T.ap()[:, g * 4:(g + 1) * 4, tc * 128:(tc + 1) * 128],
                                self.ps2.ap()[:, g * 512:(g + 1) * 512]
                                .rearrange("p (a b) -> p a b", a=4)))
                    self.V(ev1)
                    ln_ev = sV.rec()
                ln1_ready = ln_ev

                qkv_dma = [None, None]
                qkv_pe_done = [None, None]
                at_ev_last = None
                vev_last = None
                bank_gate = [None, None]
                abuf_gate = [None, None]
                for hp in range(8):
                    qb = hp % 2
                    jcs = [hp, 8 + hp, 16 + hp]

                    def qwd(e, jcs=jcs, qb=qb, free=qkv_pe_done[qb]):
                        _wait(e, [free])
                        for si, jc in enumerate(jcs):
                            sQ[qb].inc(e.dma_start(out=qkvw[:, qb, si, :, :],
                                                   in_=d["wqkv"][jc, :, :, :]), 16)
                    self.P(qwd, dr=False)
                    qkv_dma[qb] = sQ[qb].rec()

                    qev_last = None
                    for si in range(3):
                        for tq in range(4):
                            hb = (si * 4 + tq) % 2
                            psum = self.ps4[:, hb * 512:(hb + 1) * 512]

                            def qmm(e, si=si, tq=tq, qb=qb, psum=psum, dr=qkv_dma[qb],
                                    ev=qev_last):
                                _wait(e, [dr, ln1_ready, ev])
                                for dc in range(DCN):
                                    mm = e.matmul(psum, qkvw[:, qb, si, dc, :],
                                             ln1T[:, dc, tq * 512:(tq + 1) * 512],
                                             start=(dc == 0), stop=(dc == DCN - 1))
                                sT.inc(mm)
                            self.T(qmm)
                            qmr = sT.rec()

                            def qev(e, si=si, tq=tq, psum=psum, qmr=qmr, hp=hp):
                                _wait(e, [qmr])
                                col = [hp, 8 + hp, 16 + hp][si]
                                sV.inc(e.tensor_scalar(
                                    qkvT[:, si, tq * 512:(tq + 1) * 512], psum,
                                    0.125 if si == 0 else 1.0,
                                    self.bqkvs[:, col:col + 1],
                                    op0=ALU.mult, op1=ALU.add))
                            self.V(qev)
                            qev_last = sV.rec()
                    qkv_ready = qev_last

                    for hh in range(2):
                        h = 2 * hp + hh
                        s = STRIDES[h]
                        Ls = S // s
                        KC = Ls // 128
                        off = 64 * hh

                        def vtr(e, s=s, off=off, KC=KC, qr=qkv_ready):
                            _wait(e, [qr])
                            for kc in range(KC):
                                src = qkvT.ap()[off:off + 64, 2,
                                                kc * 128 * s:(kc * 128 + 128) * s:s]
                                mm = e.matmul(self.psb[:, kc * 64:(kc + 1) * 64], src,
                                         self.identb.ap()[off:off + 64, off:off + 64],
                                         start=(kc == 0), stop=(kc == KC - 1),
                                         is_transpose=True)
                            sT.inc(mm)
                        self.T(vtr)
                        vtr_r = sT.rec()

                        def vev(e, KC=KC, vtr_r=vtr_r):
                            _wait(e, [vtr_r])
                            sV.inc(e.tensor_copy(vst.ap()[:, 0:KC, :],
                                                 self.psb.ap()[:, 0:KC * 64]
                                                 .rearrange("p (a b) -> p a b", b=64)))
                        self.V(vev)
                        vev_r = sV.rec()
                        vev_last = vev_r

                        pss_gate = None
                        pend = None

                        def emit_tail(qc, qg, par, a2_r):
                            nonlocal pss_gate, at_ev_last, hc_last
                            at_evs = []
                            nbat = (KC + 7) // 8
                            for bi in range(nbat):
                                k0, k1 = bi * 8, min(KC, bi * 8 + 8)

                                def atrb(e, k0=k0, k1=k1, r=a2_r, vr=vev_r,
                                         aev=at_evs[-1] if at_evs else None, par=par):
                                    _wait(e, [r, vr, aev])
                                    for kc in range(k0, k1):
                                        mm = e.matmul(self.psb[:, (kc - k0) * 128:(kc - k0 + 1) * 128],
                                                 abuf[:, par, kc * 128:(kc + 1) * 128],
                                                 self.identb[:],
                                                 start=(kc == k0), stop=(kc == k1 - 1),
                                                 is_transpose=True)
                                    sT.inc(mm)
                                self.T(atrb)
                                atr_r = sT.rec()

                                def atev(e, k0=k0, k1=k1, atr_r=atr_r):
                                    _wait(e, [atr_r])
                                    sV.inc(e.tensor_copy(
                                        atb.ap()[:, k0:k1, :],
                                        self.psb.ap()[:, 0:(k1 - k0) * 128]
                                        .rearrange("p (a b) -> p a b", b=128)))
                                self.V(atev)
                                at_evs.append(sV.rec())
                                at_ev_last = sV.rec()
                            at_ready = at_evs[-1]
                            abuf_gate[par] = sT.rec()

                            def avm(e, qg=qg, KC=KC, atr=at_ready, vr=vev_r,
                                    gate=pss_gate):
                                _wait(e, [atr, vr] + ([gate] if qg == 0 else []))
                                for kc in range(KC):
                                    mm = e.matmul(self.pss.ap()[0:64, qg * 128:(qg + 1) * 128],
                                             vst[:, kc, :], atb[:, kc, :],
                                             start=(qg == 0 and kc == 0),
                                             stop=(qg == 3 and kc == KC - 1))
                                sT.inc(mm)
                            self.T(avm)
                            av_r = sT.rec()

                            if qg == 3:
                                def avev(e, h=h, qc=qc, av_r=av_r):
                                    _wait(e, [av_r])
                                    dst = hcT.ap()[64 * (h % 2):64 * (h % 2) + 64,
                                                   h // 2, (qc - 3) * 128:(qc + 1) * 128]
                                    sA.inc(e.activation(dst, self.pss.ap()[0:64, 0:512],
                                                        AF.Identity))
                                self.A(avev)
                                hc_last = sA.rec()
                                pss_gate = hc_last

                        for qc in range(TCN + 1):
                            if qc < TCN:
                                qg = qc % 4
                                par = qc % 2
                                bpar = (qc % 2) if Ls <= 1024 else 0
                                boff = bpar * 1024
                                nmc = self.sm[:, 0:1, qg]
                                zc = self.sm[:, 1:2, qg]
                                rzc = self.sm[:, 2:3, qg]

                                def smm(e, off=off, qc=qc, s=s, Ls=Ls, qr=qkv_ready,
                                        gate=bank_gate[bpar], boff=boff):
                                    _wait(e, [qr, gate])
                                    nkb = (Ls + 511) // 512
                                    for kb in range(nkb):
                                        n = min(512, Ls - kb * 512)
                                        src = qkvT.ap()[off:off + 64, 1,
                                                        kb * 512 * s:(kb * 512 + n) * s:s]
                                        mm = e.matmul(self.ps4[:, boff + kb * 512:boff + kb * 512 + n],
                                                 qkvT.ap()[off:off + 64, 0,
                                                           qc * 128:(qc + 1) * 128],
                                                 src, start=True, stop=True)
                                    sT.inc(mm)
                                self.T(smm)
                                smm_r = sT.rec()

                                def smax(e, Ls=Ls, nmc=nmc, smm_r=smm_r, boff=boff):
                                    _wait(e, [smm_r])
                                    sV.inc(e.reduce_max(nmc, self.ps4[:, boff:boff + Ls],
                                                        axis=AX.X, negate=True))
                                self.V(smax)
                                smax_r = sV.rec()

                                def sexp1(e, Ls=Ls, nmc=nmc, zc=zc, smax_r=smax_r,
                                          gate=abuf_gate[par], boff=boff, par=par):
                                    _wait(e, [smax_r, gate])
                                    sA.inc(e.activation(abuf[:, par, 0:Ls],
                                                        self.ps4[:, boff:boff + Ls],
                                                        AF.Exp, bias=nmc, accum_out=zc))
                                self.A(sexp1)
                                sexp1_r = sA.rec()
                                bank_gate[bpar] = sexp1_r

                                def srz(e, zc=zc, rzc=rzc, r=sexp1_r):
                                    _wait(e, [r])
                                    sV.inc(e.reciprocal(rzc, zc))
                                self.V(srz)
                                srz_r = sV.rec()

                                def sscale(e, Ls=Ls, rzc=rzc, par=par, r=srz_r):
                                    _wait(e, [r])
                                    sV.inc(e.tensor_scalar(abuf[:, par, 0:Ls],
                                                           abuf[:, par, 0:Ls],
                                                           rzc, None, op0=ALU.mult))
                                self.V(sscale)
                                a2_r = sV.rec()
                            if pend is not None:
                                emit_tail(*pend)
                            pend = (qc, qg, par, a2_r) if qc < TCN else None

                    qkv_pe_done[qb] = sT.rec()

            if self.debug:
                def dump_hc(e, hl=hc_last, qd=qkv_pe_done[1], aev=at_ev_last,
                            vv=vev_last):
                    _wait(e, [hl, qd, aev, vv])
                    self.dS.inc(e.dma_start(out=d["dbg_hc"][:, :, :], in_=hcT[:]), 16)
                    self.dS.inc(e.dma_start(out=d["dbg_a"][:, :], in_=abuf.ap()[:, 1, :]), 16)
                    self.dS.inc(e.dma_start(out=d["dbg_qkv"][:, :, :], in_=qkvT[:]), 16)
                    self.dS.inc(e.dma_start(out=d["dbg_at"][:, :, :], in_=atb[:]), 16)
                    self.dS.inc(e.dma_start(out=d["dbg_vs"][:, :, :], in_=vst[:]), 16)
                self.Sy(dump_hc)
            self.barrier()
            with ExitStack() as ph3:
                sb3 = lambda n, sh, dt: ph3.enter_context(nc.sbuf_tensor(u(n), sh, dt))
                wout_sb = sb3("a_wout", [128, DCN, D], BF16)

                def wod(e):
                    sWo.inc(e.dma_start(out=wout_sb[:], in_=d["wout"]
                                        .rearrange("(dc p) n -> p dc n", p=128)), 16)
                self.P(wod, dr=False)
                wout_r = sWo.rec()

                z_prev = None
                wog_prev = None
                h1w_prev = None
                for tc in range(TCN):
                    def womm(e, tc=tc, wr=wout_r, hca=hc_last, zp=z_prev):
                        _wait(e, [wr, hca, zp])
                        for half in range(2):
                            for dc in range(DCN):
                                e.matmul(self.ps2[:, half * 512:(half + 1) * 512],
                                         hcT.ap()[:, dc, tc * 128:(tc + 1) * 128],
                                         wout_sb[:, dc, half * 512:(half + 1) * 512],
                                         start=(dc == 0), stop=False)
                            mm = e.matmul(self.ps2[:, half * 512:(half + 1) * 512],
                                     self.ones_b[:],
                                     self.bout_bf.ap()[:, half * 512:(half + 1) * 512],
                                     start=False, stop=True)
                        sT.inc(mm)
                    self.T(womm)
                    womm_r = sT.rec()

                    def woxa(e, tc=tc, wp=wog_prev):
                        _wait(e, [wp])
                        sV.inc(e.tensor_scalar(self.xa_scr[:], self.x_tm[:, tc, :],
                                               self.rstd1[:, tc:tc + 1],
                                               self.nmr1[:, tc:tc + 1],
                                               op0=ALU.mult, op1=ALU.add))
                    self.V(woxa)
                    woxa_r = sV.rec()

                    def woz(e, womm_r=womm_r, woxa_r=woxa_r):
                        _wait(e, [womm_r, woxa_r])
                        sV.inc(e.tensor_tensor(self.z_scr[:], self.xa_scr[:],
                                               self.ps2[:], op=ALU.add))
                    self.V(woz)
                    z_prev = sV.rec()

                    rstdh = self.sm[:, 4:5, 0]
                    nmrh = self.sm[:, 5:6, 0]
                    st_r = self._ln_stats_chunk(self.z_scr[:], rstdh, nmrh,
                                                [z_prev, h1w_prev], self.t1_scr[:])

                    def woh(e, st_r=st_r, rstdh=rstdh, nmrh=nmrh):
                        _wait(e, [st_r])
                        sV.inc(e.tensor_scalar(self.xa_scr[:], self.z_scr[:],
                                               rstdh, nmrh, op0=ALU.mult, op1=ALU.add))
                    self.V(woh)
                    woh_r = sV.rec()

                    def wog(e, woh_r=woh_r):
                        _wait(e, [woh_r])
                        sV.inc(e.tensor_tensor(self.t1_scr[:], self.xa_scr[:],
                                               self.gab[:], op=ALU.mult))
                    self.V(wog)
                    wog_r = sV.rec()
                    wog_prev = wog_r

                    def woh1(e, tc=tc, wog_r=wog_r, hp_=h1w_prev):
                        _wait(e, [wog_r, hp_])
                        sP.inc(e.tensor_tensor(
                            h1v[:, :, tc, :],
                            self.t1_scr.ap().rearrange("p (a b) -> p a b", b=128),
                            self.bab.ap().rearrange("p (a b) -> p a b", b=128),
                            op=ALU.add))
                    self.P(woh1)
                    h1_written[tc] = sP.rec()
                    h1w_prev = h1_written[tc]
            self.barrier()

            with ExitStack() as ph4:
                sb4 = lambda n, sh, dt: ph4.enter_context(nc.sbuf_tensor(u(n), sh, dt))
                lnT = sb4("f_lnT", [128, DCN, TBS], BF16)
                u16 = sb4("f_u16", [128, DCN, TBS], F32)
                hbuf = sb4("f_h", [128, FCN, TBS], BF16)
                w1sl = sb4("f_w1sl", [128, 4, DCN, 128], BF16)
                w2sl = sb4("f_w2sl", [128, 1, FCN, 128], BF16)

                state = dict(
                    h_evac=[None] * FCN, pe_w1=[None] * FCN,
                    o_evac=[None] * DCN, pe_w2=[None] * DCN,
                    w1buf_free=[[] for _ in range(4)],
                    w2buf_free=[[]],
                )
                o_guard = [None] * DCN

                for tb in range(NTB):
                    chunks = [tb * CPB + i for i in range(CPB)]
                    first = (tb == 0)
                    lnq = None
                    for i, tc in enumerate(chunks):
                        def htr(e, tc=tc, h1r=h1_written[tc], prev=lnq):
                            _wait(e, [h1r, prev])
                            for dcx in range(DCN):
                                mm = e.matmul(self.psb[:, dcx * 128:(dcx + 1) * 128],
                                         h1v[:, dcx, tc, :], self.identb[:],
                                         start=(dcx == 0), stop=(dcx == DCN - 1),
                                         is_transpose=True)
                            sT.inc(mm)
                        self.T(htr)
                        htr_r = sT.rec()

                        def hev2(e, i=i, htr_r=htr_r):
                            _wait(e, [htr_r])
                            sV.inc(e.tensor_copy(lnT.ap()[:, :, i * 128:(i + 1) * 128],
                                                 self.psb.ap()[:, 0:1024]
                                                 .rearrange("p (a b) -> p a b", a=8)))
                        self.V(hev2)
                        lnq = sV.rec()
                    lnT_ready = lnq

                    def b2fn(dc, psum, group_rec):
                        def oev(e, dc=dc, psum=psum, gr=group_rec, og=o_guard[dc]):
                            _wait(e, [gr, og])
                            sV.inc(e.tensor_scalar(u16[:, dc, :], psum,
                                                   self.b2ac[:, dc:dc + 1], None,
                                                   op0=ALU.add))
                        self.V(oev)
                        state["o_evac"][dc] = sV.rec()

                    self._mlp_block(
                        lnT, hbuf, w1sl, w2sl, 4, 1,
                        lambda fc: d["w1a"][fc, :, :, :],
                        lambda fc: d["w1ab"][fc, :, :, :],
                        self.w1a_cached,
                        lambda fc: self.b1ac[:, fc:fc + 1],
                        lambda dc: d["w2a"][dc, :, :, :],
                        lambda dc: d["w2ab"][dc, :, :, :],
                        self.w2a_cached,
                        b2fn, first, lnT_ready, state)

                    ffn_done = [state["o_evac"][dc] for dc in range(DCN)]
                    ep_ev = None
                    x_upd_prev = None
                    ffn_epn_prev = getattr(self, "_ffn_epn_prev", [None])
                    self._ffn_epn_prev = ffn_epn_prev
                    for i, tc in enumerate(chunks):
                        def ept(e, i=i, deps=ffn_done, prev=ep_ev):
                            _wait(e, deps + [prev])
                            for g in range(2):
                                for j in range(4):
                                    dcx = g * 4 + j
                                    mm = e.matmul(self.ps2[:, g * 512 + j * 128:g * 512 + (j + 1) * 128],
                                             u16[:, dcx, i * 128:(i + 1) * 128],
                                             self.identf[:], start=(j == 0), stop=(j == 3),
                                             is_transpose=True)
                            sT.inc(mm)
                        self.T(ept)
                        eptr = sT.rec()
                        o_guard = [eptr] * DCN

                        def epv(e, eptr=eptr, xup=x_upd_prev):
                            _wait(e, [eptr, xup])
                            sV.inc(e.tensor_copy(self.t1_scr[:], self.ps2[:]))
                        self.V(epv)
                        ep_ev = sV.rec()

                        def epz(e, tc=tc, evr=ep_ev, h1r=h1_written[tc],
                                pn=ffn_epn_prev[0]):
                            _wait(e, [evr, h1r, pn])
                            sP.inc(e.tensor_tensor(
                                self.z_scr.ap().rearrange("p (a b) -> p a b", b=128),
                                self.t1_scr.ap().rearrange("p (a b) -> p a b", b=128),
                                h1v[:, :, tc, :], op=ALU.add))
                        self.P(epz)
                        z_r = sP.rec()

                        rstdz = self.sm[:, 6:7, 0]
                        nmrz = self.sm[:, 7:8, 0]
                        st_r = self._ln_stats_chunk(self.z_scr[:], rstdz, nmrz, [z_r],
                                                    self.t1_scr[:])

                        def epn(e, st_r=st_r, rstdz=rstdz, nmrz=nmrz):
                            _wait(e, [st_r])
                            sV.inc(e.tensor_scalar(self.t1_scr[:], self.z_scr[:],
                                                   rstdz, nmrz, op0=ALU.mult, op1=ALU.add))
                        self.V(epn)
                        epnr = sV.rec()
                        ffn_epn_prev[0] = epnr

                        def epg(e, epnr=epnr):
                            _wait(e, [epnr])
                            sV.inc(e.tensor_tensor(self.xa_scr[:], self.t1_scr[:],
                                                   self.gab[:], op=ALU.mult))
                        self.V(epg)
                        epgr = sV.rec()

                        def epx(e, tc=tc, epgr=epgr):
                            _wait(e, [epgr])
                            e.tensor_tensor(self.x_tm[:, tc, :], self.x_tm[:, tc, :],
                                            self.xa_scr[:], op=ALU.add)
                            e.drain()
                            sP.inc(e.tensor_tensor(self.x_tm[:, tc, :],
                                                   self.x_tm[:, tc, :],
                                                   self.bab[:], op=ALU.add))
                        self.P(epx)
                        x_upd_prev = sP.rec()


def _prep_weights(ins):
    g = lambda k: np.ascontiguousarray(np.asarray(ins[k]))
    W1 = g("L0_ffn_W1")
    W2 = g("L0_ffn_W2")
    W1a = g("L0_attn_W1")
    W2a = g("L0_attn_W2")
    Wqkv = g("L0_attn_Wqkv")
    com = {}
    com["w1s"] = np.ascontiguousarray(
        W1.reshape(NE, FCN, 128, DCN, 128).transpose(0, 1, 4, 3, 2))
    com["w2s"] = np.ascontiguousarray(
        W2.reshape(NE, DCN, 128, FCN, 128).transpose(0, 1, 4, 3, 2))
    com["w1a"] = np.ascontiguousarray(
        W1a.reshape(DCN, 128, FCN, 128).transpose(2, 1, 0, 3))
    com["w2a"] = np.ascontiguousarray(
        W2a.reshape(FCN, 128, DCN, 128).transpose(2, 1, 0, 3))
    com["wqkv"] = np.ascontiguousarray(
        Wqkv.reshape(DCN, 128, 24, 128).transpose(2, 1, 0, 3))
    com["wout"] = g("L0_attn_Wout")
    com["wr"] = np.ascontiguousarray(
        g("L0_ffn_Wr").T.reshape(DCN, 128, NE).transpose(1, 0, 2))
    com["b1c"] = np.ascontiguousarray(
        g("L0_ffn_b1").reshape(NE, FCN, 128).transpose(2, 0, 1))
    com["b2c"] = np.ascontiguousarray(
        g("L0_ffn_b2").reshape(NE, DCN, 128).transpose(2, 0, 1))
    com["b1ac"] = np.ascontiguousarray(g("L0_attn_b1").reshape(FCN, 128).T)
    com["b2ac"] = np.ascontiguousarray(g("L0_attn_b2").reshape(DCN, 128).T)
    com["bqkvc"] = np.ascontiguousarray(g("L0_attn_bqkv").reshape(24, 128).T)
    com["br_b"] = np.ascontiguousarray(np.broadcast_to(g("L0_ffn_br"), (128, NE)))
    com["g_f"] = np.ascontiguousarray(np.broadcast_to(g("L0_ffn_gamma"), (128, D)))
    com["be_f"] = np.ascontiguousarray(np.broadcast_to(g("L0_ffn_beta"), (128, D)))
    com["g_a"] = np.ascontiguousarray(np.broadcast_to(g("L0_attn_gamma"), (128, D)))
    com["be_a"] = np.ascontiguousarray(np.broadcast_to(g("L0_attn_beta"), (128, D)))
    com["bout_r"] = np.ascontiguousarray(g("L0_attn_bout").reshape(1, D))
    com["identf"] = np.eye(128, dtype=np.float32)
    com["ones1"] = np.ones((1, 128), dtype=np.float32)
    return com


def kernel(**inputs):
    com = _prep_weights(inputs)
    xs = np.ascontiguousarray(np.asarray(inputs["L0_attn_x"]))
    nc = Builder(debug=False).build()
    in_maps = []
    for b in range(8):
        m = dict(com)
        m["x"] = np.ascontiguousarray(xs[b])
        in_maps.append(m)
    res = run_bass_kernel_spmd(nc, in_maps, core_ids=list(range(8)))
    return np.stack([np.asarray(r["y"]) for r in res.results], axis=0)


# version 07
# speedup vs baseline: 2.0957x; 1.0300x over previous
import numpy as np
import ml_dtypes
from contextlib import ExitStack

import concourse.bass as bass
import concourse.mybir as mybir
from concourse.bass_utils import run_bass_kernel_spmd

F32 = mybir.dt.float32
BF16 = mybir.dt.bfloat16
AF = mybir.ActivationFunctionType
ALU = mybir.AluOpType
AX = mybir.AxisListType

S = 2048
TCN = 16
D = 1024
DCN = 8
FF = 4096
FCN = 32
NE = 8
TBS = 512
NTB = S // TBS
CPB = TBS // 128
NH = 16
HD = 64
STRIDES = [1, 1, 2, 2, 2, 2, 4, 4, 4, 4, 8, 8, 8, 8, 16, 16]
EPS = 1e-5


class Sem:
    def __init__(self, ctx, nc, name):
        self.h = ctx.enter_context(nc.semaphore(name))
        self.n = 0

    def inc(self, instr, k=1):
        instr.then_inc(self.h, k)
        self.n += k
        return self.n

    def rec(self):
        return (self, self.n)


def _wait(e, recs):
    seen = {}
    for r in recs:
        if r is None:
            continue
        s, v = r
        if v > 0 and seen.get(id(s), -1) < v:
            seen[id(s)] = v
            e.wait_ge(s.h, v)


class Builder:
    def __init__(self, debug=False, stop_after="full", n_tb=NTB, n_exp=NE,
                 do_router=True, do_ep=True):
        self.debug = debug
        self.stop_after = stop_after
        self.n_tb = n_tb
        self.n_exp = n_exp
        self.do_router = do_router
        self.do_ep = do_ep
        self.nc = bass.Bass("TRN2", target_bir_lowering=False)
        self._uid = 0

    def uid(self, p):
        self._uid += 1
        return f"{p}{self._uid}"

    def T(self, fn): self.blk.tensor(fn)

    def _fence(self, sem, fn):
        n = sem.n
        def g(e):
            if n > 0:
                e.wait_ge(sem.h, n)
            fn(e)
        return g

    def A(self, fn, dr=True):
        self.blk.scalar(self._fence(self.sA, fn) if dr else fn)

    def V(self, fn, dr=True):
        self.blk.vector(self._fence(self.sV, fn) if dr else fn)

    def P(self, fn, dr=True):
        self.blk.gpsimd(self._fence(self.sP, fn) if dr else fn)

    def Sy(self, fn): self.blk.sync(fn)

    def barrier(self):
        sems = [self.sT, self.sA, self.sV, self.sP, self.dS, self.sIc,
                self.sWo] + self.sW1 + self.sW1c + self.sW2 + self.sW2c + \
            self.sQ + self.sW1x + self.sW2x
        recs = [s.rec() for s in sems]
        for em in (self.T, self.A, self.V, self.P, self.Sy):
            em(lambda e: _wait(e, recs))

    def build(self):
        nc = self.nc
        d = {}
        d["x"] = nc.dram_tensor("x", [S, D], F32, kind="ExternalInput").ap()
        d["w1s"] = nc.dram_tensor("w1s", [NE, FCN, 128, DCN, 128], F32, kind="ExternalInput").ap()
        d["w2s"] = nc.dram_tensor("w2s", [NE, DCN, 128, FCN, 128], F32, kind="ExternalInput").ap()
        d["w1a"] = nc.dram_tensor("w1a", [FCN, 128, DCN, 128], F32, kind="ExternalInput").ap()
        d["w2a"] = nc.dram_tensor("w2a", [DCN, 128, FCN, 128], F32, kind="ExternalInput").ap()
        d["wqkv"] = nc.dram_tensor("wqkv", [24, 128, DCN, 128], F32, kind="ExternalInput").ap()
        d["wout"] = nc.dram_tensor("wout", [D, D], F32, kind="ExternalInput").ap()
        d["wr"] = nc.dram_tensor("wr", [128, DCN, NE], F32, kind="ExternalInput").ap()
        d["b1c"] = nc.dram_tensor("b1c", [128, NE, FCN], F32, kind="ExternalInput").ap()
        d["b2c"] = nc.dram_tensor("b2c", [128, NE, DCN], F32, kind="ExternalInput").ap()
        d["b1ac"] = nc.dram_tensor("b1ac", [128, FCN], F32, kind="ExternalInput").ap()
        d["b2ac"] = nc.dram_tensor("b2ac", [128, DCN], F32, kind="ExternalInput").ap()
        d["bqkvc"] = nc.dram_tensor("bqkvc", [128, 24], F32, kind="ExternalInput").ap()
        d["br_b"] = nc.dram_tensor("br_b", [128, NE], F32, kind="ExternalInput").ap()
        d["g_f"] = nc.dram_tensor("g_f", [128, D], F32, kind="ExternalInput").ap()
        d["be_f"] = nc.dram_tensor("be_f", [128, D], F32, kind="ExternalInput").ap()
        d["g_a"] = nc.dram_tensor("g_a", [128, D], F32, kind="ExternalInput").ap()
        d["be_a"] = nc.dram_tensor("be_a", [128, D], F32, kind="ExternalInput").ap()
        d["bout_r"] = nc.dram_tensor("bout_r", [1, D], F32, kind="ExternalInput").ap()
        d["identf"] = nc.dram_tensor("identf", [128, 128], F32, kind="ExternalInput").ap()
        d["ones1"] = nc.dram_tensor("ones1", [1, 128], F32, kind="ExternalInput").ap()
        d["y"] = nc.dram_tensor("y", [S, D], F32, kind="ExternalOutput").ap()
        d["w1b"] = nc.dram_tensor("w1b", [NE, FCN, 128, DCN, 128], BF16, kind="Internal").ap()
        d["w2b"] = nc.dram_tensor("w2b", [NE, DCN, 128, FCN, 128], BF16, kind="Internal").ap()
        d["w1ab"] = nc.dram_tensor("w1ab", [FCN, 128, DCN, 128], BF16, kind="Internal").ap()
        d["w2ab"] = nc.dram_tensor("w2ab", [DCN, 128, FCN, 128], BF16, kind="Internal").ap()
        if self.debug:
            d["dbg_x1"] = nc.dram_tensor("dbg_x1", [S, D], F32, kind="ExternalOutput").ap()
            d["dbg_x2"] = nc.dram_tensor("dbg_x2", [S, D], F32, kind="ExternalOutput").ap()
            d["dbg_rw"] = nc.dram_tensor("dbg_rw", [128, CPB, NE], F32, kind="ExternalOutput").ap()
            d["dbg_acc"] = nc.dram_tensor("dbg_acc", [128, DCN, TBS], F32, kind="ExternalOutput").ap()
            d["dbg_we"] = nc.dram_tensor("dbg_we", [128, TBS], F32, kind="ExternalOutput").ap()
            d["dbg_stat"] = nc.dram_tensor("dbg_stat", [128, 16, 8], F32, kind="ExternalOutput").ap()
            d["dbg_hc"] = nc.dram_tensor("dbg_hc", [128, DCN, S], BF16, kind="ExternalOutput").ap()
            d["dbg_a"] = nc.dram_tensor("dbg_a", [128, S], BF16, kind="ExternalOutput").ap()
            d["dbg_qkv"] = nc.dram_tensor("dbg_qkv", [128, 3, S], BF16, kind="ExternalOutput").ap()
            d["dbg_at"] = nc.dram_tensor("dbg_at", [128, 16, 128], BF16, kind="ExternalOutput").ap()
            d["dbg_vs"] = nc.dram_tensor("dbg_vs", [128, 16, HD], BF16, kind="ExternalOutput").ap()
            d["dbg_rlog"] = nc.dram_tensor("dbg_rlog", [128, CPB, NE], F32, kind="ExternalOutput").ap()
        self.d = d

        with ExitStack() as ctx:
            self.ctx = ctx
            self._alloc_outer()
            self.blk = ctx.enter_context(nc.Block())
            self._emit()
        return nc

    def _alloc_outer(self):
        nc, ctx = self.nc, self.ctx
        sb = lambda n, sh, dt: ctx.enter_context(nc.sbuf_tensor(n, sh, dt))
        self.x_tm = sb("s_x", [128, TCN, D], F32)
        self.gab = sb("s_gab", [128, D], F32)
        self.bab = sb("s_bab", [128, D], F32)
        self.wr_sb = sb("s_wr", [128, DCN, NE], F32)
        self.b1c = sb("s_b1c", [128, NE, FCN], F32)
        self.b2c = sb("s_b2c", [128, NE, DCN], F32)
        self.b1ac = sb("s_b1ac", [128, FCN], F32)
        self.b2ac = sb("s_b2ac", [128, DCN], F32)
        self.bqkvs = sb("s_bqkvs", [128, 24], F32)
        self.br_b = sb("s_brb", [128, NE], F32)
        self.bout_bf = sb("s_boutbf", [1, D], BF16)
        self.identf = sb("s_idf", [128, 128], F32)
        self.identb = sb("s_idb", [128, 128], BF16)
        self.ones_f = sb("s_onesf", [1, 128], F32)
        self.ones_b = sb("s_onesb", [1, 128], BF16)
        self.stat = sb("s_stat", [128, 16, 8], F32)
        self.rstd1 = sb("s_rstd1", [128, TCN], F32)
        self.nmr1 = sb("s_nmr1", [128, TCN], F32)
        self.rlog = sb("s_rlog", [128, CPB, NE], F32)
        self.rw = sb("s_rw", [128, CPB, NE], F32)
        self.req = sb("s_req", [128, 2, NE], F32)
        self.w_row = sb("s_wrow", [1, TBS], F32)
        self.sm = sb("s_sm", [128, 8, 4], F32)
        self.xa_scr = sb("s_xa", [128, D], F32)
        self.t1_scr = sb("s_t1", [128, D], F32)
        self.z_scr = sb("s_z", [128, D], F32)
        self.ps4 = ctx.enter_context(nc.psum_tensor("p_ps4", [128, 2048], F32))
        self.ps2 = ctx.enter_context(nc.psum_tensor("p_ps2", [128, 1024], F32))
        self.psb = ctx.enter_context(nc.psum_tensor("p_psb", [128, 1024], BF16))
        self.pss = ctx.enter_context(nc.psum_tensor("p_pss", [128, 512], F32))
        self.sT = Sem(ctx, nc, "sT")
        self.sA = Sem(ctx, nc, "sA")
        self.sV = Sem(ctx, nc, "sV")
        self.sP = Sem(ctx, nc, "sP")
        self.dS = Sem(ctx, nc, "dS")
        self.sIc = Sem(ctx, nc, "sIc")
        self.sW1 = [Sem(ctx, nc, f"sW1_{i}") for i in range(6)]
        self.sW1x = [Sem(ctx, nc, f"sW1x_{i}") for i in range(6)]
        self.sW2x = [Sem(ctx, nc, f"sW2x_{i}") for i in range(4)]
        self.sW1c = [Sem(ctx, nc, f"sW1c_{i}") for i in range(6)]
        self.sW2 = [Sem(ctx, nc, f"sW2_{i}") for i in range(4)]
        self.sW2c = [Sem(ctx, nc, f"sW2c_{i}") for i in range(4)]
        self.sQ = [Sem(ctx, nc, f"sQ_{i}") for i in range(2)]
        self.sWo = Sem(ctx, nc, "sWo")

    def _emit(self):
        d = self.d
        dS, sIc, sA = self.dS, self.sIc, self.sA

        def init_dmas(e):
            x_r = d["x"].rearrange("(tc p) dd -> p tc dd", p=128)
            for i in range(4):
                dS.inc(e.dma_start(out=self.x_tm[:, 4 * i:4 * i + 4, :],
                                   in_=x_r[:, 4 * i:4 * i + 4, :]), 16)
            for dst, src in ((self.wr_sb[:], d["wr"][:, :, :]),
                             (self.b1c[:], d["b1c"][:, :, :]),
                             (self.b2c[:], d["b2c"][:, :, :]),
                             (self.b1ac[:], d["b1ac"][:, :]),
                             (self.b2ac[:], d["b2ac"][:, :]),
                             (self.bqkvs[:], d["bqkvc"][:, :]),
                             (self.br_b[:], d["br_b"][:, :]),
                             (self.identf[:], d["identf"][:, :]),
                             (self.ones_f[:], d["ones1"][:, :]),
                             (self.gab[:], d["g_a"][:, :]),
                             (self.bab[:], d["be_a"][:, :])):
                dS.inc(e.dma_start(out=dst, in_=src), 16)
        self.Sy(init_dmas)
        init_dS = dS.rec()

        def init_casts(e):
            sIc.inc(e.dma_start(out=self.identb[:], in_=d["identf"][:, :]), 16)
            sIc.inc(e.dma_start(out=self.ones_b[:], in_=d["ones1"][:, :]), 16)
            sIc.inc(e.dma_start(out=self.bout_bf[:], in_=d["bout_r"][:, :]), 16)
        self.P(init_casts, dr=False)

        def init_zero(e):
            e.memset(self.stat[:], 0.0)
            e.memset(self.sm[:], 0.0)
            e.memset(self.rlog[:], 0.0)
            e.memset(self.req[:], 0.0)
            e.memset(self.rw[:], 0.0)
            self.sV.inc(e.memset(self.w_row[:], 0.0))
        self.V(init_zero, dr=False)
        init_zero_r = self.sV.rec()

        def init_scale(e):
            _wait(e, [init_dS])
            e.activation(self.xa_scr[:, 0:8], self.bqkvs[:, 0:8], AF.Identity, scale=0.125)
            e.drain()
            sA.inc(e.activation(self.bqkvs[:, 0:8], self.xa_scr[:, 0:8], AF.Identity))
        self.A(init_scale)
        self.x_ready = [init_dS, sA.rec()]
        self.w1_cached = {}
        self.w2_cached = {}
        self.w1a_cached = {}
        self.w2a_cached = {}

        self.barrier()
        if self.stop_after != "init":
            self._moe_call(0)
            self.barrier()
            if self.debug:
                self._dump_x("dbg_x1")
                self.barrier()
        if self.stop_after not in ("init", "moe0"):
            self._attn_block()
            self.barrier()
            if self.debug:
                self._dump_x("dbg_x2")
                self.barrier()
        if self.stop_after not in ("init", "moe0", "attn"):
            self._moe_call(1)
            self.barrier()

        def out_dma(e):
            y_r = d["y"].rearrange("(tc p) dd -> p tc dd", p=128)
            for i in range(4):
                self.dS.inc(e.dma_start(out=y_r[:, 4 * i:4 * i + 4, :],
                                        in_=self.x_tm[:, 4 * i:4 * i + 4, :]), 16)
            e.wait_ge(self.dS.h, self.dS.n)
        self.Sy(out_dma)

    def _dump_x(self, name):
        def f(e):
            y_r = self.d[name].rearrange("(tc p) dd -> p tc dd", p=128)
            for i in range(4):
                self.dS.inc(e.dma_start(out=y_r[:, 4 * i:4 * i + 4, :],
                                        in_=self.x_tm[:, 4 * i:4 * i + 4, :]), 16)
        self.Sy(f)

    def _ln_stats_chunk(self, src_ap, rstd_col, nmr_col, dep_recs, sq_scr):
        sA, sV = self.sA, self.sV
        ssq = self.stat[:, 0:1, 6]
        ssm = self.stat[:, 1:2, 6]
        mu = self.stat[:, 2:3, 6]
        var = self.stat[:, 0:1, 7]
        t2 = self.stat[:, 1:2, 7]
        lnv = self.stat[:, 2:3, 7]

        def v1(e):
            _wait(e, dep_recs)
            e.scalar_tensor_tensor(sq_scr, src_ap, 1.0, src_ap,
                                   op0=ALU.mult, op1=ALU.mult, accum_out=ssq)
            e.reduce_sum(ssm, src_ap, axis=AX.X)
            e.drain()
            e.tensor_scalar(mu, ssm, 1.0 / D, None, op0=ALU.mult)
            e.tensor_scalar(var, ssq, 1.0 / D, None, op0=ALU.mult)
            e.drain()
            e.tensor_tensor(t2, mu, mu, op=ALU.mult)
            e.drain()
            sV.inc(e.scalar_tensor_tensor(var, var, EPS, t2,
                                          op0=ALU.add, op1=ALU.subtract))
        self.V(v1)
        v1r = sV.rec()

        def a2(e):
            _wait(e, [v1r])
            e.activation(lnv, var, AF.Ln)
            e.drain()
            sA.inc(e.activation(rstd_col, lnv, AF.Exp, scale=-0.5))
        self.A(a2)
        a2r = sA.rec()

        def v2(e):
            _wait(e, [a2r])
            sV.inc(e.scalar_tensor_tensor(nmr_col, mu, -1.0, rstd_col,
                                          op0=ALU.mult, op1=ALU.mult))
        self.V(v2)
        return sV.rec()

    def _mlp_block(self, lnT, hbuf, w1sl, w2sl, nw1buf, nw2buf,
                   w1_src, w1_cache, w1_cached, b1col,
                   w2_src, w2_cache, w2_cached, b2fn,
                   first, lnT_ready, state):
        sT, sA, sV = self.sT, self.sA, self.sV
        sW1, sW1c = self.sW1, self.sW1c
        sW2, sW2c = self.sW2, self.sW2c

        for fc in range(FCN):
            buf = fc % nw1buf
            cache_rec = None
            if first:
                def wd(e, fc=fc, buf=buf):
                    _wait(e, state["w1buf_free"][buf])
                    self.sW1x[buf].inc(e.dma_start(out=w1sl[:, buf, :, :], in_=w1_src(fc)), 16)
                self.P(wd, dr=False)
                dmar = self.sW1x[buf].rec()

                def wc(e, fc=fc, buf=buf, dmar=dmar):
                    _wait(e, [dmar])
                    sW1c[buf].inc(e.dma_start(out=w1_cache(fc), in_=w1sl[:, buf, :, :]), 16)
                self.Sy(wc)
                cache_rec = sW1c[buf].rec()
            else:
                def wd(e, fc=fc, buf=buf):
                    _wait(e, state["w1buf_free"][buf] + w1_cached["all"])
                    sW1[buf].inc(e.dma_start(out=w1sl[:, buf, :, :], in_=w1_cache(fc)), 16)
                self.Sy(wd)
                dmar = sW1[buf].rec()

            hb = fc % 2
            psum = self.ps4[:, hb * 512:(hb + 1) * 512]

            def w1mm(e, fc=fc, buf=buf, psum=psum, dmar=dmar):
                _wait(e, [dmar, lnT_ready if fc < 2 else None,
                          state["h_evac"][fc - 2]])
                for dc in range(DCN):
                    mm = e.matmul(psum, w1sl[:, buf, dc, :], lnT[:, dc, :],
                             start=(dc == 0), stop=(dc == DCN - 1))
                sT.inc(mm)
            self.T(w1mm)
            state["pe_w1"][fc] = sT.rec()
            state["w1buf_free"][buf] = [state["pe_w1"][fc]] + \
                ([cache_rec] if cache_rec else [])

            def hev(e, fc=fc, psum=psum, gr=state["pe_w1"][fc]):
                _wait(e, [gr])
                sV.inc(e.tensor_scalar(hbuf[:, fc, :], psum, b1col(fc), 0.0,
                                       op0=ALU.add, op1=ALU.max))
            self.V(hev)
            state["h_evac"][fc] = sV.rec()

        if first:
            w1_cached["all"] = [s.rec() for s in sW1c]
        h_all = state["h_evac"][FCN - 1]
        for dc in range(DCN):
            buf = dc % nw2buf
            cache_rec2 = None
            if first:
                def wd2(e, dc=dc, buf=buf):
                    _wait(e, state["w2buf_free"][buf])
                    self.sW2x[buf].inc(e.dma_start(out=w2sl[:, buf, :, :], in_=w2_src(dc)), 16)
                self.P(wd2, dr=False)
                dmar2 = self.sW2x[buf].rec()

                def wc2(e, dc=dc, buf=buf, dmar2=dmar2):
                    _wait(e, [dmar2])
                    sW2c[buf].inc(e.dma_start(out=w2_cache(dc), in_=w2sl[:, buf, :, :]), 16)
                self.Sy(wc2)
                cache_rec2 = sW2c[buf].rec()
            else:
                def wd2(e, dc=dc, buf=buf):
                    _wait(e, state["w2buf_free"][buf] + w2_cached["all"])
                    sW2[buf].inc(e.dma_start(out=w2sl[:, buf, :, :], in_=w2_cache(dc)), 16)
                self.A(wd2, dr=False)
                dmar2 = sW2[buf].rec()

            ob = dc % 2
            psum = self.ps4[:, 1024 + ob * 512:1024 + (ob + 1) * 512]

            def w2mm(e, dc=dc, buf=buf, psum=psum, dmar2=dmar2,
                     pfree=state["o_evac"][dc - 2]):
                _wait(e, [dmar2, h_all if dc < 2 else None, pfree])
                for fc in range(FCN):
                    mm = e.matmul(psum, w2sl[:, buf, fc, :], hbuf[:, fc, :],
                             start=(fc == 0), stop=(fc == FCN - 1))
                sT.inc(mm)
            self.T(w2mm)
            state["pe_w2"][dc] = sT.rec()
            state["w2buf_free"][buf] = [state["pe_w2"][dc]] + \
                ([cache_rec2] if cache_rec2 else [])

            b2fn(dc, psum, state["pe_w2"][dc])
        if first:
            w2_cached["all"] = [s.rec() for s in sW2c]

    def _moe_call(self, call):
        nc, d = self.nc, self.d
        sT, sA, sV, sP = self.sT, self.sA, self.sV, self.sP
        u = self.uid
        with ExitStack() as ph:
            sb = lambda n, sh, dt: ph.enter_context(nc.sbuf_tensor(u(n), sh, dt))
            lnT = sb("m_lnT", [128, DCN, TBS], BF16)
            u16 = sb("m_u16", [128, DCN, TBS], F32)
            hbuf = sb("m_h", [128, FCN, TBS], BF16)
            w1sl = sb("m_w1sl", [128, 5, DCN, 128], BF16)
            w2sl = sb("m_w2sl", [128, 4, FCN, 128], BF16)
            w_e = sb("m_we", [128, 2, TBS], F32)
            o_tmp = sb("m_ot", [128, 2, TBS], F32)
            gfh = sb("m_gfh", [128, D], F32)
            bfh = sb("m_bfh", [128, D], F32)

            def gf_load(e):
                self.dS.inc(e.dma_start(out=gfh[:], in_=d["g_f"][:, :]), 16)
                self.dS.inc(e.dma_start(out=bfh[:], in_=d["be_f"][:, :]), 16)
            self.Sy(gf_load)
            gf_rec = self.dS.rec()

            state = dict(
                h_evac=[None] * FCN, pe_w1=[None] * FCN,
                o_evac=[None] * DCN, pe_w2=[None] * DCN,
                w1buf_free=[[] for _ in range(5)],
                w2buf_free=[[] for _ in range(4)],
            )
            o_add = [None] * DCN
            we_ready = [None, None]
            we_free = [None, None]

            for tb in range(self.n_tb):
                chunks = [tb * CPB + i for i in range(CPB)]
                lnF = u16
                lnF_ready = None
                prev_tr1 = getattr(self, "_moe_prev_tr1", None)
                for i, tc in enumerate(chunks):
                    rstd = self.stat[:, tc:tc + 1, 0]
                    nmr = self.stat[:, tc:tc + 1, 1]
                    dep = (self.x_ready if call == 0 else []) + \
                        [getattr(self, "_prev_epg", None)]
                    st_r = self._ln_stats_chunk(self.x_tm[:, tc, :], rstd, nmr,
                                                dep, self.t1_scr[:])

                    def ap1(e, tc=tc, rstd=rstd, nmr=nmr, st_r=st_r, pt=prev_tr1,
                            xu=getattr(self, "_prev_xupd", None)):
                        _wait(e, [st_r, pt, xu])
                        sV.inc(e.tensor_scalar(self.xa_scr[:], self.x_tm[:, tc, :],
                                               rstd, nmr, op0=ALU.mult, op1=ALU.add))
                    self.V(ap1)
                    ap1r = sV.rec()

                    def tr1(e, ap1r=ap1r, prev=lnF_ready):
                        _wait(e, [ap1r, prev])
                        for g in range(2):
                            for j in range(4):
                                dcx = g * 4 + j
                                mm = e.matmul(self.ps2[:, g * 512 + j * 128:g * 512 + (j + 1) * 128],
                                         self.xa_scr[:, dcx * 128:(dcx + 1) * 128],
                                         self.identf[:], start=(j == 0), stop=(j == 3),
                                         is_transpose=True)
                        sT.inc(mm)
                    self.T(tr1)
                    tr1r = sT.rec()
                    prev_tr1 = tr1r
                    self._moe_prev_tr1 = tr1r

                    def ev1(e, i=i, tr1r=tr1r):
                        _wait(e, [tr1r])
                        for g in range(2):
                            sV.inc(e.tensor_copy(
                                lnF.ap()[:, g * 4:(g + 1) * 4, i * 128:(i + 1) * 128],
                                self.ps2.ap()[:, g * 512:(g + 1) * 512]
                                .rearrange("p (a b) -> p a b", a=4)))
                    self.V(ev1)
                    lnF_ready = sV.rec()

                def cast_ln(e, lnF_ready=lnF_ready):
                    _wait(e, [lnF_ready])
                    sA.inc(e.copy(lnT[:], lnF[:]))
                self.A(cast_ln)
                lnT_ready = sA.rec()

                rmm_r = None
                router_r = None
                if not self.do_router:
                    pass
                def rmm(e, lnF_ready=lnF_ready):
                    _wait(e, [lnF_ready])
                    for i in range(CPB):
                        for dc in range(DCN):
                            mm = e.matmul(self.pss[:, i * 8:(i + 1) * 8],
                                     lnF[:, dc, i * 128:(i + 1) * 128],
                                     self.wr_sb[:, dc, :],
                                     start=(i == 0 and dc == 0),
                                     stop=(i == CPB - 1 and dc == DCN - 1))
                    sT.inc(mm)
                if self.do_router:
                    self.T(rmm)
                    rmm_r = sT.rec()

                def rsoft(e, rmm_r=rmm_r):
                    _wait(e, [rmm_r])
                    for i in range(CPB):
                        e.tensor_tensor(self.rlog[:, i, :], self.pss[:, i * 8:(i + 1) * 8],
                                        self.br_b[:], op=ALU.add)
                    e.drain()
                    m1 = self.stat[:, 0:CPB, 2]
                    e.reduce_max(m1, self.rlog[:], axis=AX.X)
                    e.drain()
                    for i in range(CPB):
                        e.tensor_scalar(self.req[:, 0, :], self.rlog[:, i, :],
                                        self.stat[:, i:i + 1, 2], None, op0=ALU.is_equal)
                        e.drain()
                        e.scalar_tensor_tensor(self.req[:, 1, :], self.req[:, 0, :], -1e30,
                                               self.rlog[:, i, :], op0=ALU.mult, op1=ALU.add)
                        e.drain()
                        e.reduce_max(self.stat[:, i:i + 1, 3], self.req[:, 1, :], axis=AX.X)
                        e.drain()
                    sV.inc(e.tensor_tensor(self.stat[:, 0:CPB, 4],
                                           self.stat[:, 0:CPB, 3],
                                           self.stat[:, 0:CPB, 2], op=ALU.subtract))
                if self.do_router:
                    self.V(rsoft)
                rsoft_r = sV.rec() if self.do_router else None

                def rtanh(e, rsoft_r=rsoft_r):
                    _wait(e, [rsoft_r])
                    sA.inc(e.activation(self.stat[:, 0:CPB, 5], self.stat[:, 0:CPB, 4],
                                        AF.Tanh, scale=0.5))
                if self.do_router:
                    self.A(rtanh)
                rtanh_r = sA.rec() if self.do_router else None

                def rfin(e, rtanh_r=rtanh_r):
                    _wait(e, [rtanh_r])
                    w2v = self.stat[:, 0:CPB, 4]
                    w1v = self.stat[:, 0:CPB, 5]
                    e.tensor_scalar(w2v, self.stat[:, 0:CPB, 5], 0.5, 0.5,
                                    op0=ALU.mult, op1=ALU.add)
                    e.drain()
                    e.tensor_scalar(w1v, w2v, -1.0, 1.0, op0=ALU.mult, op1=ALU.add)
                    e.drain()
                    last = None
                    for i in range(CPB):
                        e.tensor_scalar(self.req[:, 0, :], self.rlog[:, i, :],
                                        self.stat[:, i:i + 1, 2], None, op0=ALU.is_equal)
                        e.tensor_scalar(self.req[:, 1, :], self.rlog[:, i, :],
                                        self.stat[:, i:i + 1, 3], None, op0=ALU.is_equal)
                        e.drain()
                        e.tensor_scalar(self.req[:, 0, :], self.req[:, 0, :],
                                        self.stat[:, i:i + 1, 5], None, op0=ALU.mult)
                        e.drain()
                        last = e.scalar_tensor_tensor(self.rw[:, i, :], self.req[:, 1, :],
                                                      self.stat[:, i:i + 1, 4],
                                                      self.req[:, 0, :],
                                                      op0=ALU.mult, op1=ALU.add)
                        e.drain()
                    sV.inc(last)
                if self.do_router:
                    self.V(rfin)
                    router_r = sV.rec()

                u16_free = getattr(self, "_u16_ept_last", None)
                o_add = [None] * DCN

                first = (call == 0 and tb == 0)
                for ei in range(self.n_exp):
                    web = ei % 2

                    def wet(e, ei=ei, rr=router_r, wf=we_free[web],
                            wprev=we_ready[1 - web]):
                        _wait(e, [rr, wf, wprev])
                        for i in range(CPB):
                            mm = e.matmul(self.pss.ap()[0:1, i * 128:(i + 1) * 128],
                                     self.rw[:, i, ei:ei + 1], self.identf[:],
                                     start=(i == 0), stop=(i == CPB - 1),
                                     is_transpose=True)
                        sT.inc(mm)
                    self.T(wet)
                    wet_r = sT.rec()

                    def wev(e, wet_r=wet_r):
                        _wait(e, [wet_r])
                        sV.inc(e.tensor_copy(self.w_row[:], self.pss.ap()[0:1, 0:TBS]))
                    self.V(wev)
                    wev_r = sV.rec()

                    def webm(e, wev_r=wev_r):
                        _wait(e, [wev_r])
                        sT.inc(e.matmul(self.pss[:, 0:TBS], self.ones_f[:], self.w_row[:],
                                        start=True, stop=True))
                    self.T(webm)
                    webm_r = sT.rec()

                    def webe(e, web=web, webm_r=webm_r):
                        _wait(e, [webm_r])
                        sV.inc(e.tensor_copy(w_e[:, web, :], self.pss[:, 0:TBS]))
                    self.V(webe)
                    we_ready[web] = sV.rec()

                    def b2fn(dc, psum, group_rec, ei=ei, web=web):
                        def oev(e, dc=dc, psum=psum, gr=group_rec,
                                wer=we_ready[web], pf=o_add[(dc - 2) % DCN]):
                            _wait(e, [gr, wer, pf])
                            sV.inc(e.scalar_tensor_tensor(
                                o_tmp[:, dc % 2, :], psum, self.b2c[:, ei, dc:dc + 1],
                                w_e[:, web, :], op0=ALU.add, op1=ALU.mult))
                        self.V(oev)
                        state["o_evac"][dc] = sV.rec()
                        we_free[web] = sV.rec()

                        if ei == 0:
                            def oad(e, dc=dc, er=state["o_evac"][dc],
                                    uf=u16_free):
                                _wait(e, [er, uf])
                                sV.inc(e.tensor_copy(u16[:, dc, :],
                                                     o_tmp[:, dc % 2, :]))
                        else:
                            def oad(e, dc=dc, er=state["o_evac"][dc]):
                                _wait(e, [er])
                                sV.inc(e.tensor_tensor(u16[:, dc, :],
                                                       u16[:, dc, :],
                                                       o_tmp[:, dc % 2, :],
                                                       op=ALU.add))
                        self.V(oad)
                        o_add[dc] = sV.rec()

                    self._mlp_block(
                        lnT, hbuf, w1sl, w2sl, 5, 4,
                        lambda fc, ei=ei: d["w1s"][ei, fc, :, :, :],
                        lambda fc, ei=ei: d["w1b"][ei, fc, :, :, :],
                        self._kcache("w1", ei),
                        lambda fc, ei=ei: self.b1c[:, ei, fc:fc + 1],
                        lambda dc, ei=ei: d["w2s"][ei, dc, :, :, :],
                        lambda dc, ei=ei: d["w2b"][ei, dc, :, :, :],
                        self._kcache("w2", ei),
                        b2fn, first, lnT_ready, state)

                dbg_acc_deps = [o_add[dc] for dc in range(DCN)] + [router_r, we_ready[0]]

                if not self.do_ep:
                    continue
                acc_done = [o_add[dc] for dc in range(DCN)]
                ep_ev = None
                x_upd_prev = None
                epg_prev = None
                for i, tc in enumerate(chunks):
                    def ept(e, i=i, deps=acc_done, prev=ep_ev):
                        _wait(e, deps + [prev])
                        for g in range(2):
                            for j in range(4):
                                dcx = g * 4 + j
                                mm = e.matmul(self.ps2[:, g * 512 + j * 128:g * 512 + (j + 1) * 128],
                                         u16[:, dcx, i * 128:(i + 1) * 128],
                                         self.identf[:], start=(j == 0), stop=(j == 3),
                                         is_transpose=True)
                        sT.inc(mm)
                    self.T(ept)
                    eptr = sT.rec()
                    self._u16_ept_last = eptr

                    def epv(e, eptr=eptr, xup=x_upd_prev):
                        _wait(e, [eptr, xup])
                        sV.inc(e.tensor_copy(self.t1_scr[:], self.ps2[:]))
                    self.V(epv)
                    ep_ev = sV.rec()

                    def epa(e, tc=tc, pg=epg_prev, xu=x_upd_prev):
                        _wait(e, [pg, xu])
                        rstd = self.stat[:, tc:tc + 1, 0]
                        nmr = self.stat[:, tc:tc + 1, 1]
                        sV.inc(e.tensor_scalar(self.xa_scr[:], self.x_tm[:, tc, :],
                                               rstd, nmr, op0=ALU.mult, op1=ALU.add))
                    self.V(epa)
                    epar = sV.rec()

                    def epz(e, epar=epar, evr=ep_ev):
                        _wait(e, [epar, evr])
                        sP.inc(e.tensor_tensor(self.z_scr[:], self.xa_scr[:],
                                               self.t1_scr[:], op=ALU.add))
                    self.P(epz)
                    z_r = sP.rec()

                    rstdz = self.stat[:, 8 + i:9 + i, 4]
                    nmrz = self.stat[:, 8 + i:9 + i, 5]
                    st_r = self._ln_stats_chunk(self.z_scr[:], rstdz, nmrz, [z_r],
                                                self.t1_scr[:])

                    def epn(e, st_r=st_r, rstdz=rstdz, nmrz=nmrz):
                        _wait(e, [st_r])
                        sV.inc(e.tensor_scalar(self.t1_scr[:], self.z_scr[:],
                                               rstdz, nmrz, op0=ALU.mult, op1=ALU.add))
                    self.V(epn)
                    epnr = sV.rec()

                    def epg(e, epnr=epnr, gfr=(gf_rec if (tb == 0 and i == 0) else None)):
                        _wait(e, [epnr, gfr])
                        e.scalar_tensor_tensor(self.xa_scr[:], self.t1_scr[:],
                                               0.5, gfh[:],
                                               op0=ALU.mult, op1=ALU.mult)
                        e.drain()
                        sV.inc(e.scalar_tensor_tensor(self.xa_scr[:], bfh[:],
                                                      0.5, self.xa_scr[:],
                                                      op0=ALU.mult, op1=ALU.add))
                    self.V(epg)
                    epgr = sV.rec()
                    epg_prev = epgr
                    self._prev_epg = epgr

                    def epx(e, tc=tc, epgr=epgr):
                        _wait(e, [epgr])
                        sP.inc(e.tensor_tensor(self.x_tm[:, tc, :],
                                               self.x_tm[:, tc, :],
                                               self.xa_scr[:], op=ALU.add))
                    self.P(epx)
                    x_upd_prev = sP.rec()
                    self._prev_xupd = x_upd_prev

                if self.debug and call == 0 and tb == 0 and self.n_tb == 1:
                    def dbg_dump(e, deps=dbg_acc_deps + [x_upd_prev, sA.rec(),
                                                         sV.rec(), sT.rec()]):
                        _wait(e, deps)
                        self.dS.inc(e.dma_start(out=d["dbg_rw"][:, :, :], in_=self.rw[:]), 16)
                        self.dS.inc(e.dma_start(out=d["dbg_acc"][:, :, :], in_=u16[:]), 16)
                        self.dS.inc(e.dma_start(out=d["dbg_we"][:, :], in_=w_e[:, 0, :]), 16)
                        self.dS.inc(e.dma_start(out=d["dbg_stat"][:, :, :], in_=self.stat[:]), 16)
                        self.dS.inc(e.dma_start(out=d["dbg_rlog"][:, :, :], in_=self.rlog[:]), 16)
                    self.Sy(dbg_dump)

    def _kcache(self, kind, ei):
        base = self.w1_cached if kind == "w1" else self.w2_cached
        return base.setdefault(ei, {})

    def _attn_block(self):
        nc, d = self.nc, self.d
        sT, sA, sV, sP = self.sT, self.sA, self.sV, self.sP
        dS, sQ, sWo = self.dS, self.sQ, self.sWo
        u = self.uid
        with ExitStack() as ph:
            sbA = lambda n, sh, dt: ph.enter_context(nc.sbuf_tensor(u(n), sh, dt))
            hcT = sbA("a_hcT", [128, DCN, S], BF16)
            h1v = hcT.ap().rearrange("p dc (tc m) -> p dc tc m", m=128)
            h1_written = [None] * TCN
            hc_last = None

            with ExitStack() as ph2:
                sb2 = lambda n, sh, dt: ph2.enter_context(nc.sbuf_tensor(u(n), sh, dt))
                ln1T = sb2("a_ln1T", [128, DCN, S], BF16)
                qkvT = sb2("a_qkvT", [128, 3, S], BF16)
                qkvw = sb2("a_qkvw", [128, 2, 3, DCN, 128], BF16)
                abuf = sb2("a_abuf", [128, 2, S], BF16)
                atb = sb2("a_atb", [128, 16, 128], BF16)
                vst = sb2("a_vst", [128, 16, HD], BF16)

                ln_ev = None
                prev_tr1 = getattr(self, "_moe_prev_tr1", None)
                for tc in range(TCN):
                    st_r = self._ln_stats_chunk(self.x_tm[:, tc, :],
                                                self.rstd1[:, tc:tc + 1],
                                                self.nmr1[:, tc:tc + 1], [],
                                                self.t1_scr[:])

                    def ap1(e, tc=tc, st_r=st_r, pt=prev_tr1):
                        _wait(e, [st_r, pt])
                        sV.inc(e.tensor_scalar(self.xa_scr[:], self.x_tm[:, tc, :],
                                               self.rstd1[:, tc:tc + 1],
                                               self.nmr1[:, tc:tc + 1],
                                               op0=ALU.mult, op1=ALU.add))
                    self.V(ap1)
                    ap1r = sV.rec()

                    def tr1(e, ap1r=ap1r, prev=ln_ev):
                        _wait(e, [ap1r, prev])
                        for g in range(2):
                            for j in range(4):
                                dcx = g * 4 + j
                                mm = e.matmul(self.ps2[:, g * 512 + j * 128:g * 512 + (j + 1) * 128],
                                         self.xa_scr[:, dcx * 128:(dcx + 1) * 128],
                                         self.identf[:], start=(j == 0), stop=(j == 3),
                                         is_transpose=True)
                        sT.inc(mm)
                    self.T(tr1)
                    tr1r = sT.rec()
                    prev_tr1 = tr1r
                    self._moe_prev_tr1 = tr1r

                    def ev1(e, tc=tc, tr1r=tr1r):
                        _wait(e, [tr1r])
                        for g in range(2):
                            sV.inc(e.tensor_copy(
                                ln1T.ap()[:, g * 4:(g + 1) * 4, tc * 128:(tc + 1) * 128],
                                self.ps2.ap()[:, g * 512:(g + 1) * 512]
                                .rearrange("p (a b) -> p a b", a=4)))
                    self.V(ev1)
                    ln_ev = sV.rec()
                ln1_ready = ln_ev

                qkv_dma = [None, None]
                qkv_pe_done = [None, None]
                at_ev_last = None
                vev_last = None
                bank_gate = [None, None]
                abuf_gate = [None, None]
                for hp in range(8):
                    qb = hp % 2
                    jcs = [hp, 8 + hp, 16 + hp]

                    def qwd(e, jcs=jcs, qb=qb, free=qkv_pe_done[qb]):
                        _wait(e, [free])
                        for si, jc in enumerate(jcs):
                            sQ[qb].inc(e.dma_start(out=qkvw[:, qb, si, :, :],
                                                   in_=d["wqkv"][jc, :, :, :]), 16)
                    self.P(qwd, dr=False)
                    qkv_dma[qb] = sQ[qb].rec()

                    qev_last = None
                    for si in range(3):
                        for tq in range(4):
                            hb = (si * 4 + tq) % 2
                            psum = self.ps4[:, hb * 512:(hb + 1) * 512]

                            def qmm(e, si=si, tq=tq, qb=qb, psum=psum, dr=qkv_dma[qb],
                                    ev=qev_last):
                                _wait(e, [dr, ln1_ready, ev])
                                for dc in range(DCN):
                                    mm = e.matmul(psum, qkvw[:, qb, si, dc, :],
                                             ln1T[:, dc, tq * 512:(tq + 1) * 512],
                                             start=(dc == 0), stop=(dc == DCN - 1))
                                sT.inc(mm)
                            self.T(qmm)
                            qmr = sT.rec()

                            def qev(e, si=si, tq=tq, psum=psum, qmr=qmr, hp=hp):
                                _wait(e, [qmr])
                                col = [hp, 8 + hp, 16 + hp][si]
                                sV.inc(e.tensor_scalar(
                                    qkvT[:, si, tq * 512:(tq + 1) * 512], psum,
                                    0.125 if si == 0 else 1.0,
                                    self.bqkvs[:, col:col + 1],
                                    op0=ALU.mult, op1=ALU.add))
                            self.V(qev)
                            qev_last = sV.rec()
                    qkv_ready = qev_last

                    for hh in range(2):
                        h = 2 * hp + hh
                        s = STRIDES[h]
                        Ls = S // s
                        KC = Ls // 128
                        off = 64 * hh

                        def vtr(e, s=s, off=off, KC=KC, qr=qkv_ready):
                            _wait(e, [qr])
                            for kc in range(KC):
                                src = qkvT.ap()[off:off + 64, 2,
                                                kc * 128 * s:(kc * 128 + 128) * s:s]
                                mm = e.matmul(self.psb[:, kc * 64:(kc + 1) * 64], src,
                                         self.identb.ap()[off:off + 64, off:off + 64],
                                         start=(kc == 0), stop=(kc == KC - 1),
                                         is_transpose=True)
                            sT.inc(mm)
                        self.T(vtr)
                        vtr_r = sT.rec()

                        def vev(e, KC=KC, vtr_r=vtr_r):
                            _wait(e, [vtr_r])
                            sV.inc(e.tensor_copy(vst.ap()[:, 0:KC, :],
                                                 self.psb.ap()[:, 0:KC * 64]
                                                 .rearrange("p (a b) -> p a b", b=64)))
                        self.V(vev)
                        vev_r = sV.rec()
                        vev_last = vev_r

                        pss_gate = None
                        pend = None

                        def emit_tail(qc, qg, par, a2_r):
                            nonlocal pss_gate, at_ev_last, hc_last
                            at_evs = []
                            nbat = (KC + 7) // 8
                            for bi in range(nbat):
                                k0, k1 = bi * 8, min(KC, bi * 8 + 8)

                                def atrb(e, k0=k0, k1=k1, r=a2_r, vr=vev_r,
                                         aev=at_evs[-1] if at_evs else None, par=par):
                                    _wait(e, [r, vr, aev])
                                    for kc in range(k0, k1):
                                        mm = e.matmul(self.psb[:, (kc - k0) * 128:(kc - k0 + 1) * 128],
                                                 abuf[:, par, kc * 128:(kc + 1) * 128],
                                                 self.identb[:],
                                                 start=(kc == k0), stop=(kc == k1 - 1),
                                                 is_transpose=True)
                                    sT.inc(mm)
                                self.T(atrb)
                                atr_r = sT.rec()

                                def atev(e, k0=k0, k1=k1, atr_r=atr_r):
                                    _wait(e, [atr_r])
                                    sV.inc(e.tensor_copy(
                                        atb.ap()[:, k0:k1, :],
                                        self.psb.ap()[:, 0:(k1 - k0) * 128]
                                        .rearrange("p (a b) -> p a b", b=128)))
                                self.V(atev)
                                at_evs.append(sV.rec())
                                at_ev_last = sV.rec()
                            at_ready = at_evs[-1]
                            abuf_gate[par] = sT.rec()

                            def avm(e, qg=qg, KC=KC, atr=at_ready, vr=vev_r,
                                    gate=pss_gate):
                                _wait(e, [atr, vr] + ([gate] if qg == 0 else []))
                                for kc in range(KC):
                                    mm = e.matmul(self.pss.ap()[0:64, qg * 128:(qg + 1) * 128],
                                             vst[:, kc, :], atb[:, kc, :],
                                             start=(qg == 0 and kc == 0),
                                             stop=(qg == 3 and kc == KC - 1))
                                sT.inc(mm)
                            self.T(avm)
                            av_r = sT.rec()

                            if qg == 3:
                                def avev(e, h=h, qc=qc, av_r=av_r):
                                    _wait(e, [av_r])
                                    dst = hcT.ap()[64 * (h % 2):64 * (h % 2) + 64,
                                                   h // 2, (qc - 3) * 128:(qc + 1) * 128]
                                    sA.inc(e.activation(dst, self.pss.ap()[0:64, 0:512],
                                                        AF.Identity))
                                self.A(avev)
                                hc_last = sA.rec()
                                pss_gate = hc_last

                        for qc in range(TCN + 1):
                            if qc < TCN:
                                qg = qc % 4
                                par = qc % 2
                                bpar = (qc % 2) if Ls <= 1024 else 0
                                boff = bpar * 1024
                                nmc = self.sm[:, 0:1, qg]
                                zc = self.sm[:, 1:2, qg]
                                rzc = self.sm[:, 2:3, qg]

                                def smm(e, off=off, qc=qc, s=s, Ls=Ls, qr=qkv_ready,
                                        gate=bank_gate[bpar], boff=boff):
                                    _wait(e, [qr, gate])
                                    nkb = (Ls + 511) // 512
                                    for kb in range(nkb):
                                        n = min(512, Ls - kb * 512)
                                        src = qkvT.ap()[off:off + 64, 1,
                                                        kb * 512 * s:(kb * 512 + n) * s:s]
                                        mm = e.matmul(self.ps4[:, boff + kb * 512:boff + kb * 512 + n],
                                                 qkvT.ap()[off:off + 64, 0,
                                                           qc * 128:(qc + 1) * 128],
                                                 src, start=True, stop=True)
                                    sT.inc(mm)
                                self.T(smm)
                                smm_r = sT.rec()

                                def smax(e, Ls=Ls, nmc=nmc, smm_r=smm_r, boff=boff):
                                    _wait(e, [smm_r])
                                    sV.inc(e.reduce_max(nmc, self.ps4[:, boff:boff + Ls],
                                                        axis=AX.X, negate=True))
                                self.V(smax)
                                smax_r = sV.rec()

                                def sexp1(e, Ls=Ls, nmc=nmc, zc=zc, smax_r=smax_r,
                                          gate=abuf_gate[par], boff=boff, par=par):
                                    _wait(e, [smax_r, gate])
                                    sA.inc(e.activation(abuf[:, par, 0:Ls],
                                                        self.ps4[:, boff:boff + Ls],
                                                        AF.Exp, bias=nmc, accum_out=zc))
                                self.A(sexp1)
                                sexp1_r = sA.rec()
                                bank_gate[bpar] = sexp1_r

                                def srz(e, zc=zc, rzc=rzc, r=sexp1_r):
                                    _wait(e, [r])
                                    sV.inc(e.reciprocal(rzc, zc))
                                self.V(srz)
                                srz_r = sV.rec()

                                def sscale(e, Ls=Ls, rzc=rzc, par=par, r=srz_r):
                                    _wait(e, [r])
                                    sV.inc(e.tensor_scalar(abuf[:, par, 0:Ls],
                                                           abuf[:, par, 0:Ls],
                                                           rzc, None, op0=ALU.mult))
                                self.V(sscale)
                                a2_r = sV.rec()
                            if pend is not None:
                                emit_tail(*pend)
                            pend = (qc, qg, par, a2_r) if qc < TCN else None

                    qkv_pe_done[qb] = sT.rec()

            if self.debug:
                def dump_hc(e, hl=hc_last, qd=qkv_pe_done[1], aev=at_ev_last,
                            vv=vev_last):
                    _wait(e, [hl, qd, aev, vv])
                    self.dS.inc(e.dma_start(out=d["dbg_hc"][:, :, :], in_=hcT[:]), 16)
                    self.dS.inc(e.dma_start(out=d["dbg_a"][:, :], in_=abuf.ap()[:, 1, :]), 16)
                    self.dS.inc(e.dma_start(out=d["dbg_qkv"][:, :, :], in_=qkvT[:]), 16)
                    self.dS.inc(e.dma_start(out=d["dbg_at"][:, :, :], in_=atb[:]), 16)
                    self.dS.inc(e.dma_start(out=d["dbg_vs"][:, :, :], in_=vst[:]), 16)
                self.Sy(dump_hc)
            self.barrier()
            with ExitStack() as ph3:
                sb3 = lambda n, sh, dt: ph3.enter_context(nc.sbuf_tensor(u(n), sh, dt))
                wout_sb = sb3("a_wout", [128, DCN, D], BF16)

                def wod(e):
                    sWo.inc(e.dma_start(out=wout_sb[:], in_=d["wout"]
                                        .rearrange("(dc p) n -> p dc n", p=128)), 16)
                self.P(wod, dr=False)
                wout_r = sWo.rec()

                z_prev = None
                wog_prev = None
                h1w_prev = None
                for tc in range(TCN):
                    def womm(e, tc=tc, wr=wout_r, hca=hc_last, zp=z_prev):
                        _wait(e, [wr, hca, zp])
                        for half in range(2):
                            for dc in range(DCN):
                                e.matmul(self.ps2[:, half * 512:(half + 1) * 512],
                                         hcT.ap()[:, dc, tc * 128:(tc + 1) * 128],
                                         wout_sb[:, dc, half * 512:(half + 1) * 512],
                                         start=(dc == 0), stop=False)
                            mm = e.matmul(self.ps2[:, half * 512:(half + 1) * 512],
                                     self.ones_b[:],
                                     self.bout_bf.ap()[:, half * 512:(half + 1) * 512],
                                     start=False, stop=True)
                        sT.inc(mm)
                    self.T(womm)
                    womm_r = sT.rec()

                    def woxa(e, tc=tc, wp=wog_prev):
                        _wait(e, [wp])
                        sV.inc(e.tensor_scalar(self.xa_scr[:], self.x_tm[:, tc, :],
                                               self.rstd1[:, tc:tc + 1],
                                               self.nmr1[:, tc:tc + 1],
                                               op0=ALU.mult, op1=ALU.add))
                    self.V(woxa)
                    woxa_r = sV.rec()

                    def woz(e, womm_r=womm_r, woxa_r=woxa_r):
                        _wait(e, [womm_r, woxa_r])
                        sV.inc(e.tensor_tensor(self.z_scr[:], self.xa_scr[:],
                                               self.ps2[:], op=ALU.add))
                    self.V(woz)
                    z_prev = sV.rec()

                    rstdh = self.sm[:, 4:5, 0]
                    nmrh = self.sm[:, 5:6, 0]
                    st_r = self._ln_stats_chunk(self.z_scr[:], rstdh, nmrh,
                                                [z_prev, h1w_prev], self.t1_scr[:])

                    def woh(e, st_r=st_r, rstdh=rstdh, nmrh=nmrh):
                        _wait(e, [st_r])
                        sV.inc(e.tensor_scalar(self.xa_scr[:], self.z_scr[:],
                                               rstdh, nmrh, op0=ALU.mult, op1=ALU.add))
                    self.V(woh)
                    woh_r = sV.rec()

                    def wog(e, woh_r=woh_r):
                        _wait(e, [woh_r])
                        sV.inc(e.tensor_tensor(self.t1_scr[:], self.xa_scr[:],
                                               self.gab[:], op=ALU.mult))
                    self.V(wog)
                    wog_r = sV.rec()
                    wog_prev = wog_r

                    def woh1(e, tc=tc, wog_r=wog_r, hp_=h1w_prev):
                        _wait(e, [wog_r, hp_])
                        sP.inc(e.tensor_tensor(
                            h1v[:, :, tc, :],
                            self.t1_scr.ap().rearrange("p (a b) -> p a b", b=128),
                            self.bab.ap().rearrange("p (a b) -> p a b", b=128),
                            op=ALU.add))
                    self.P(woh1)
                    h1_written[tc] = sP.rec()
                    h1w_prev = h1_written[tc]
            self.barrier()

            with ExitStack() as ph4:
                sb4 = lambda n, sh, dt: ph4.enter_context(nc.sbuf_tensor(u(n), sh, dt))
                lnT = sb4("f_lnT", [128, DCN, TBS], BF16)
                u16 = sb4("f_u16", [128, DCN, TBS], F32)
                hbuf = sb4("f_h", [128, FCN, TBS], BF16)
                w1sl = sb4("f_w1sl", [128, 4, DCN, 128], BF16)
                w2sl = sb4("f_w2sl", [128, 2, FCN, 128], BF16)

                state = dict(
                    h_evac=[None] * FCN, pe_w1=[None] * FCN,
                    o_evac=[None] * DCN, pe_w2=[None] * DCN,
                    w1buf_free=[[] for _ in range(4)],
                    w2buf_free=[[], []],
                )
                o_guard = [None] * DCN

                for tb in range(NTB):
                    chunks = [tb * CPB + i for i in range(CPB)]
                    first = (tb == 0)
                    lnq = None
                    for i, tc in enumerate(chunks):
                        def htr(e, tc=tc, h1r=h1_written[tc], prev=lnq):
                            _wait(e, [h1r, prev])
                            for dcx in range(DCN):
                                mm = e.matmul(self.psb[:, dcx * 128:(dcx + 1) * 128],
                                         h1v[:, dcx, tc, :], self.identb[:],
                                         start=(dcx == 0), stop=(dcx == DCN - 1),
                                         is_transpose=True)
                            sT.inc(mm)
                        self.T(htr)
                        htr_r = sT.rec()

                        def hev2(e, i=i, htr_r=htr_r):
                            _wait(e, [htr_r])
                            sV.inc(e.tensor_copy(lnT.ap()[:, :, i * 128:(i + 1) * 128],
                                                 self.psb.ap()[:, 0:1024]
                                                 .rearrange("p (a b) -> p a b", a=8)))
                        self.V(hev2)
                        lnq = sV.rec()
                    lnT_ready = lnq

                    def b2fn(dc, psum, group_rec):
                        def oev(e, dc=dc, psum=psum, gr=group_rec, og=o_guard[dc]):
                            _wait(e, [gr, og])
                            sV.inc(e.tensor_scalar(u16[:, dc, :], psum,
                                                   self.b2ac[:, dc:dc + 1], None,
                                                   op0=ALU.add))
                        self.V(oev)
                        state["o_evac"][dc] = sV.rec()

                    self._mlp_block(
                        lnT, hbuf, w1sl, w2sl, 4, 2,
                        lambda fc: d["w1a"][fc, :, :, :],
                        lambda fc: d["w1ab"][fc, :, :, :],
                        self.w1a_cached,
                        lambda fc: self.b1ac[:, fc:fc + 1],
                        lambda dc: d["w2a"][dc, :, :, :],
                        lambda dc: d["w2ab"][dc, :, :, :],
                        self.w2a_cached,
                        b2fn, first, lnT_ready, state)

                    ffn_done = [state["o_evac"][dc] for dc in range(DCN)]
                    ep_ev = None
                    x_upd_prev = None
                    ffn_epn_prev = getattr(self, "_ffn_epn_prev", [None])
                    self._ffn_epn_prev = ffn_epn_prev
                    for i, tc in enumerate(chunks):
                        def ept(e, i=i, deps=ffn_done, prev=ep_ev):
                            _wait(e, deps + [prev])
                            for g in range(2):
                                for j in range(4):
                                    dcx = g * 4 + j
                                    mm = e.matmul(self.ps2[:, g * 512 + j * 128:g * 512 + (j + 1) * 128],
                                             u16[:, dcx, i * 128:(i + 1) * 128],
                                             self.identf[:], start=(j == 0), stop=(j == 3),
                                             is_transpose=True)
                            sT.inc(mm)
                        self.T(ept)
                        eptr = sT.rec()
                        o_guard = [eptr] * DCN

                        def epv(e, eptr=eptr, xup=x_upd_prev):
                            _wait(e, [eptr, xup])
                            sV.inc(e.tensor_copy(self.t1_scr[:], self.ps2[:]))
                        self.V(epv)
                        ep_ev = sV.rec()

                        def epz(e, tc=tc, evr=ep_ev, h1r=h1_written[tc],
                                pn=ffn_epn_prev[0]):
                            _wait(e, [evr, h1r, pn])
                            sP.inc(e.tensor_tensor(
                                self.z_scr.ap().rearrange("p (a b) -> p a b", b=128),
                                self.t1_scr.ap().rearrange("p (a b) -> p a b", b=128),
                                h1v[:, :, tc, :], op=ALU.add))
                        self.P(epz)
                        z_r = sP.rec()

                        rstdz = self.sm[:, 6:7, 0]
                        nmrz = self.sm[:, 7:8, 0]
                        st_r = self._ln_stats_chunk(self.z_scr[:], rstdz, nmrz, [z_r],
                                                    self.t1_scr[:])

                        def epn(e, st_r=st_r, rstdz=rstdz, nmrz=nmrz):
                            _wait(e, [st_r])
                            sV.inc(e.tensor_scalar(self.t1_scr[:], self.z_scr[:],
                                                   rstdz, nmrz, op0=ALU.mult, op1=ALU.add))
                        self.V(epn)
                        epnr = sV.rec()
                        ffn_epn_prev[0] = epnr

                        def epg(e, epnr=epnr):
                            _wait(e, [epnr])
                            sV.inc(e.tensor_tensor(self.xa_scr[:], self.t1_scr[:],
                                                   self.gab[:], op=ALU.mult))
                        self.V(epg)
                        epgr = sV.rec()

                        def epx(e, tc=tc, epgr=epgr):
                            _wait(e, [epgr])
                            e.tensor_tensor(self.x_tm[:, tc, :], self.x_tm[:, tc, :],
                                            self.xa_scr[:], op=ALU.add)
                            e.drain()
                            sP.inc(e.tensor_tensor(self.x_tm[:, tc, :],
                                                   self.x_tm[:, tc, :],
                                                   self.bab[:], op=ALU.add))
                        self.P(epx)
                        x_upd_prev = sP.rec()


def _prep_weights(ins):
    g = lambda k: np.ascontiguousarray(np.asarray(ins[k]))
    W1 = g("L0_ffn_W1")
    W2 = g("L0_ffn_W2")
    W1a = g("L0_attn_W1")
    W2a = g("L0_attn_W2")
    Wqkv = g("L0_attn_Wqkv")
    com = {}
    com["w1s"] = np.ascontiguousarray(
        W1.reshape(NE, FCN, 128, DCN, 128).transpose(0, 1, 4, 3, 2))
    com["w2s"] = np.ascontiguousarray(
        W2.reshape(NE, DCN, 128, FCN, 128).transpose(0, 1, 4, 3, 2))
    com["w1a"] = np.ascontiguousarray(
        W1a.reshape(DCN, 128, FCN, 128).transpose(2, 1, 0, 3))
    com["w2a"] = np.ascontiguousarray(
        W2a.reshape(FCN, 128, DCN, 128).transpose(2, 1, 0, 3))
    com["wqkv"] = np.ascontiguousarray(
        Wqkv.reshape(DCN, 128, 24, 128).transpose(2, 1, 0, 3))
    com["wout"] = g("L0_attn_Wout")
    com["wr"] = np.ascontiguousarray(
        g("L0_ffn_Wr").T.reshape(DCN, 128, NE).transpose(1, 0, 2))
    com["b1c"] = np.ascontiguousarray(
        g("L0_ffn_b1").reshape(NE, FCN, 128).transpose(2, 0, 1))
    com["b2c"] = np.ascontiguousarray(
        g("L0_ffn_b2").reshape(NE, DCN, 128).transpose(2, 0, 1))
    com["b1ac"] = np.ascontiguousarray(g("L0_attn_b1").reshape(FCN, 128).T)
    com["b2ac"] = np.ascontiguousarray(g("L0_attn_b2").reshape(DCN, 128).T)
    com["bqkvc"] = np.ascontiguousarray(g("L0_attn_bqkv").reshape(24, 128).T)
    com["br_b"] = np.ascontiguousarray(np.broadcast_to(g("L0_ffn_br"), (128, NE)))
    com["g_f"] = np.ascontiguousarray(np.broadcast_to(g("L0_ffn_gamma"), (128, D)))
    com["be_f"] = np.ascontiguousarray(np.broadcast_to(g("L0_ffn_beta"), (128, D)))
    com["g_a"] = np.ascontiguousarray(np.broadcast_to(g("L0_attn_gamma"), (128, D)))
    com["be_a"] = np.ascontiguousarray(np.broadcast_to(g("L0_attn_beta"), (128, D)))
    com["bout_r"] = np.ascontiguousarray(g("L0_attn_bout").reshape(1, D))
    com["identf"] = np.eye(128, dtype=np.float32)
    com["ones1"] = np.ones((1, 128), dtype=np.float32)
    return com


def kernel(**inputs):
    com = _prep_weights(inputs)
    xs = np.ascontiguousarray(np.asarray(inputs["L0_attn_x"]))
    nc = Builder(debug=False).build()
    in_maps = []
    for b in range(8):
        m = dict(com)
        m["x"] = np.ascontiguousarray(xs[b])
        in_maps.append(m)
    res = run_bass_kernel_spmd(nc, in_maps, core_ids=list(range(8)))
    return np.stack([np.asarray(r["y"]) for r in res.results], axis=0)
